# version 12
# baseline (speedup 1.0000x reference)
import numpy as np
import concourse.bass as bass
import concourse.mybir as mybir
from concourse.bass_utils import run_bass_kernel_spmd

F32 = mybir.dt.float32
BF16 = mybir.dt.bfloat16
AF = mybir.ActivationFunctionType
ALU = mybir.AluOpType
AX = mybir.AxisListType

D, H, HD, DH = 1024, 16, 64, 4096
LH = 4096
SEG = 64
S4 = float(64.0 ** -0.25)
EPS = 1e-5
MAGIC = float(1.5 * 2 ** 23)
NITER = 6
PA_NT = 8
S4B_SIZES = [510] * 8 + [16]


class Prog:
    ENGS = ("sync", "scalar", "vector", "tensor", "gpsimd")

    def __init__(self, nc):
        self.nc = nc
        self.ops = []
        self.counts = {}
        self.labels = {}
        self.label_idx = {}

    def _resolve(self, wait):
        out = []
        for w in wait or ():
            assert isinstance(w, str) and w in self.labels, f"bad wait {w}"
            out.append(self.labels[w])
        return out

    ENG_SEM = {"vector": "cV", "scalar": "cS", "tensor": "cP", "gpsimd": "cG"}

    def op(self, eng, fn, wait=(), inc=None, label=None, sw=False):
        waits = self._resolve(wait)
        if sw:
            es = self.ENG_SEM[eng]
            c = self.counts.get(es, 0)
            if c > 0:
                waits.append((es, c))
        if inc is not None:
            self.counts[inc] = self.counts.get(inc, 0) + 1
            if label:
                self.labels[label] = (inc, self.counts[inc])
                self.label_idx[label] = len(self.ops)
        else:
            assert label is None
        self.ops.append((eng, fn, waits, inc, "one"))

    def dma(self, eng, fn, n, wait=(), inc=None, label=None):
        waits = self._resolve(wait)
        assert inc is not None
        self.counts[inc] = self.counts.get(inc, 0) + 16 * n
        if label:
            self.labels[label] = (inc, self.counts[inc])
            self.label_idx[label] = len(self.ops)
        self.ops.append((eng, fn, waits, inc, n))

    def cc(self, fn, wait=(), inc=None, label=None):
        waits = self._resolve(wait)
        self.labels[label] = (inc, 1)
        self.label_idx[label] = len(self.ops)
        self.ops.append(("gpsimd", fn, waits, inc, "cc"))

    def barrier_wait(self, eng, wait):
        self.ops.append((eng, lambda e: None, self._resolve(wait), None, "one"))

    def lower(self, sems):
        with self.nc.Block() as block:
            for eng_name in self.ENGS:
                ops = [o for o in self.ops if o[0] == eng_name]
                if not ops:
                    continue

                def body(eng, ops=ops):
                    waited = {}
                    for _, fn, waits, inc, kind in ops:
                        for sem_name, val in waits:
                            if waited.get(sem_name, -1) < val:
                                eng.wait_ge(sems[sem_name], val)
                                waited[sem_name] = val
                        r = fn(eng)
                        if inc is None:
                            continue
                        if kind == "cc":
                            r.then_inc(sems[inc], 1)
                        elif kind == "one":
                            if isinstance(r, (list, tuple)):
                                r = r[-1]
                            r.then_inc(sems[inc], 1)
                        else:
                            if not isinstance(r, (list, tuple)):
                                r = [r]
                            assert len(r) == kind, f"dma count {len(r)} != {kind}"
                            for ins in r:
                                ins.then_inc(sems[inc], 16)

                getattr(block, eng_name)(body)


def ceil_div(a, b):
    return (a + b - 1) // b


def build_program(dbg=(), stop=None):
    nc = bass.Bass("TRN2", target_bir_lowering=False, debug=False)
    dbg = set(dbg)

    def DI(name, shape, dt=F32):
        return nc.dram_tensor(name, list(shape), dt, kind="ExternalInput").ap()

    xin = DI("xin", [LH, D])
    xh_d = DI("xh", [128, D])
    wqT_d = DI("wqT", [D, D]); wkT_d = DI("wkT", [D, D]); wvT_d = DI("wvT", [D, D])
    w1T_d = DI("w1T", [D, DH]); w2T_d = DI("w2T", [DH, D])
    bq_d = DI("bq_r", [128, 8]); bk_d = DI("bk_r", [128, 8])
    bv_d = DI("bv_row", [1, D])
    convw_d = DI("convw_r", [128, 96]); convb_d = DI("convb_r", [128, 32])
    cones_d = DI("cones", [128, 2])
    eye_d = DI("eye_f", [128, 128])
    eye4_d = DI("eye4_f", [128, 512])
    onesrow_d = DI("onesrow", [1, 128])
    m1row_d = DI("m1row", [1, 128])
    onescol_d = DI("onescol", [128, 1])
    pmask_d = DI("pmask", [128, 24])
    hmask_d = DI("hmask", [128, 64])

    yout = nc.dram_tensor("yout", [LH, D], F32, kind="ExternalOutput").ap()
    dbg_outs = {}

    def DBG(name, shape, dt=F32):
        dbg_outs[name] = nc.dram_tensor("dbg_" + name, list(shape), dt, kind="ExternalOutput").ap()
        return dbg_outs[name]

    q_sp = nc.dram_tensor("q_sp", [8, 128, LH], BF16, kind="Internal").ap()
    k_sp = nc.dram_tensor("k_sp", [8, 128, LH], BF16, kind="Internal").ap()
    v_sp = nc.dram_tensor("v_sp", [32, 128, D], BF16, kind="Internal").ap()
    xq_sp = nc.dram_tensor("xq_sp", [8, 128, LH + 2], BF16, kind="Internal").ap()
    x2_sp = nc.dram_tensor("x2_sp", [LH, D], F32, kind="Internal").ap()
    w1q_sp = nc.dram_tensor("w1q_sp", [D, DH], BF16, kind="Internal").ap()
    w2q_sp = nc.dram_tensor("w2q_sp", [DH, D], BF16, kind="Internal").ap()
    e1_in = nc.dram_tensor("e1_in", [128, 1024], F32, kind="Internal").ap()
    e1_out = nc.dram_tensor("e1_out", [1024, 1024], F32, kind="Internal").ap()
    E2W = 1042
    e2_in = nc.dram_tensor("e2_in", [128, E2W], F32, kind="Internal").ap()
    e2_out = nc.dram_tensor("e2_out", [1024, E2W], F32, kind="Internal").ap()

    def alloc(name, shape, dt, region):
        t = nc.alloc_sbuf_tensor_at(name, list(shape), dt, offset=region[0])
        bpp = int(np.prod(shape[1:])) * (2 if dt == BF16 else 4)
        region[0] += (bpp + 31) // 32 * 32
        return t

    used = nc.SBUF_PARTITION_SIZE_BYTES - nc.sbuf_bytes_remaining
    PR = [(used + 511) // 512 * 512]
    c_eye_b = alloc("c_eye_b", [128, 128], BF16, PR)
    c_cones = alloc("c_cones", [128, 2], BF16, PR)
    c_onesrow = alloc("c_onesrow", [1, 128], BF16, PR)
    c_m1row = alloc("c_m1row", [1, 128], BF16, PR)
    c_onescol = alloc("c_onescol", [128, 1], BF16, PR)
    c_pmask = alloc("c_pmask", [128, 24], F32, PR)
    c_hmask = alloc("c_hmask", [128, 64], F32, PR)
    c_bq = alloc("c_bq", [128, 8], F32, PR)
    c_bk = alloc("c_bk", [128, 8], F32, PR)
    c_bv_b = alloc("c_bv_b", [1, D], BF16, PR)
    c_convw = alloc("c_convw", [128, 96], F32, PR)
    c_convb = alloc("c_convb", [128, 32], F32, PR)
    c_xh = alloc("c_xh", [128, D], F32, PR)
    fscr = alloc("fscr", [128, 128], F32, PR)
    pscr_b = alloc("pscr_b", [128, 512], BF16, PR)
    rsw1 = alloc("rsw1", [128, 1], F32, PR)
    rsw2 = alloc("rsw2", [128, 1], F32, PR)
    sw1 = alloc("sw1", [128, 1], F32, PR)
    sw2 = alloc("sw2", [128, 1], F32, PR)
    qmax_run = alloc("qmax_run", [128, 1], F32, PR)
    klmb = alloc("klmb", [128, 16 * 128], BF16, PR)
    csmax = alloc("csmax", [128, 1], F32, PR)
    negb1 = alloc("negb1", [128, 1], F32, PR)
    negb2 = alloc("negb2", [128, 1], F32, PR)
    gcol = alloc("gcol", [128, 1], F32, PR)
    rgcol = alloc("rgcol", [128, 1], F32, PR)
    csb = alloc("csb", [128, 1024], BF16, PR)
    qh_sb = alloc("qh_sb", [128, 1024], BF16, PR)
    sm = alloc("sm", [128, 32], F32, PR)
    smb = alloc("smb", [128, 8], BF16, PR)
    lnmv = alloc("lnmv", [128, 8], F32, PR)
    rstd4 = alloc("rstd4", [128, 4], F32, PR)
    veps4 = alloc("veps4", [128, 4], F32, PR)
    negmr4 = alloc("negmr4", [128, 4], F32, PR)
    magicc = alloc("magicc", [128, 1], F32, PR)
    lnscr = alloc("lnscr", [128, 4], F32, PR)
    rowf = alloc("rowf", [1, 64], F32, PR)
    rowb = alloc("rowb", [1, 64], BF16, PR)
    wabs = alloc("wabs", [128, 40], F32, PR)
    wabs_b = alloc("wabs_b", [128, 2], BF16, PR)
    fcol = alloc("fcol", [128, 2], F32, PR)
    mrun = alloc("mrun", [1, 2], F32, PR)
    rs_sb = alloc("rs_sb", [128, 16], F32, PR)
    s2mx = alloc("s2mx", [128, 4], F32, PR)
    sf2col = alloc("sf2col", [128, 4], F32, PR)
    urow_b = alloc("urow_b", [1, 128], BF16, PR)
    PERSIST_END = PR[0]
    assert PERSIST_END <= PR[0] + 20 * 1024, PERSIST_END
    RBASE = PERSIST_END

    RA = [RBASE]
    wq_sb = alloc("wq_sb", [128, 8 * D], BF16, RA)
    wk_sb = alloc("wk_sb", [128, 8 * D], BF16, RA)
    wv_sb = alloc("wv_sb", [128, 8 * D], BF16, RA)
    xbuf = [alloc("xbuf0", [128, 4 * D], F32, RA), None]
    y1tm = alloc("y1tm", [128, 4 * D], BF16, RA)
    y1fm = alloc("y1fm", [128, 8 * 512], BF16, RA)
    qstage = alloc("qstage", [128, 8 * 512], BF16, RA)
    kstage = alloc("kstage", [128, 8 * 512], BF16, RA)
    vstage = alloc("vstage", [128, 4 * D], BF16, RA)
    wstage = [alloc("wstage0", [128, 4096], F32, RA), alloc("wstage1", [128, 4096], F32, RA)]
    _wqo_off = RA[0]
    wqout = [alloc("wqout0", [128, 4096], BF16, RA), alloc("wqout1", [128, 4096], BF16, RA)]
    xbuf[1] = nc.alloc_sbuf_tensor_at("xbuf1", [128, 4 * D], F32, offset=_wqo_off)
    qmx = alloc("qmx", [128, 512], F32, RA)
    qsq = alloc("qsq", [128, 4096], F32, RA)
    xlm_sb = alloc("xlm_sb", [128, 512], F32, RA)
    xlm_b = alloc("xlm_b", [128, 512], BF16, RA)
    qlm_own = alloc("qlm_own", [128, 512], F32, RA)
    klm_own = alloc("klm_own", [128, 512], F32, RA)
    assert RA[0] <= 229376, RA[0]

    RB = [RBASE]
    kt = [alloc("kt0", [128, 8 * 512], BF16, RB), alloc("kt1", [128, 8 * 512], BF16, RB)]
    vt = [alloc("vt0", [128, 4 * D], BF16, RB), alloc("vt1", [128, 4 * D], BF16, RB)]
    esb = alloc("esb", [128, 2 * 2048], BF16, RB)
    kmx = alloc("kmx", [128, 512], F32, RB)
    ksq = alloc("ksq", [128, 4096], F32, RB)
    grbuf = [alloc("grbuf0", [128, E2W], F32, RB), alloc("grbuf1", [128, E2W], F32, RB)]
    zeven = alloc("zeven", [128, E2W], F32, RB)
    zodd = alloc("zodd", [128, E2W], F32, RB)
    v0t = alloc("v0t", [128, 16 * 128], BF16, RB)
    v1t = alloc("v1t", [128, 16 * 128], BF16, RB)
    k2T = alloc("k2T", [128, 16 * 128], BF16, RB)
    kvt = alloc("kvt", [128, 16 * 128], BF16, RB)
    kvTt = alloc("kvTt", [128, 16 * 128 + 128], BF16, RB)
    t13 = alloc("t13", [128, 16 * 128], BF16, RB)
    sq_scr = alloc("sq_scr", [128, 1024], F32, RB)
    qlm = alloc("qlm", [128, 1024], F32, RB)
    klm = alloc("klm", [128, 1024], F32, RB)
    qlmb = alloc("qlmb", [128, 16 * 128], BF16, RB)
    k2sb = alloc("k2sb", [128, 2048], BF16, RB)
    zrun = alloc("zrun", [128, 1040], F32, RB)
    kvn = alloc("kvn", [128, 1024], BF16, RB)
    c_e7 = alloc("c_e7", [128, 512], BF16, RB)
    c_e15 = alloc("c_e15", [128, 512], BF16, RB)
    c_e13 = alloc("c_e13", [128, 512], BF16, RB)
    wqst = [alloc(f"wqst{q_}", [128, 2048], F32, RB) for q_ in range(4)]
    wqo = [alloc(f"wqo{q_}", [128, 2048], BF16, RB) for q_ in range(4)]
    assert RB[0] <= 229376, RB[0]
    assert RB[0] <= 229376, RB[0]

    RC = [RBASE]
    qt4 = alloc("qt4", [128, 8 * 512], BF16, RC)
    x4 = [alloc("x40", [128, D], F32, RC), alloc("x41", [128, D], F32, RC)]
    e4 = alloc("e4", [128, 8 * 512], BF16, RC)
    aslot = alloc("aslot", [128, D], F32, RC)
    t0slot = alloc("t0slot", [128, D], F32, RC)
    x2slot = [alloc("x2s0", [128, D], F32, RC), alloc("x2s1", [128, D], F32, RC)]
    xqtm2 = [alloc("xqtm0", [128, D], BF16, RC), alloc("xqtm1", [128, D], BF16, RC)]
    xqsub = alloc("xqsub", [128, 1024], BF16, RC)
    assert RC[0] <= 229376, RC[0]

    RD = [RBASE]
    h1buf = alloc("h1buf", [128, 32 * 512], BF16, RD)
    xqw = alloc("xqw", [128, 8 * 512], BF16, RD)
    x2blk = [alloc("x2b0", [128, D], F32, RD), alloc("x2b1", [128, D], F32, RD)]
    csc1 = alloc("csc1", [128, 512], BF16, RD)
    csc2 = alloc("csc2", [128, 512], BF16, RD)
    hmxp = alloc("hmxp", [128, 512], BF16, RD)
    hmnp = alloc("hmnp", [128, 512], BF16, RD)
    uq = alloc("uq", [128, 512], F32, RD)
    s2bc = alloc("s2bc", [128, 512], BF16, RD)
    fbuf = alloc("fbuf", [128, D], F32, RD)
    s2rowb = alloc("s2rowb", [1, 512], BF16, RD)
    w1sb = alloc("w1sb", [128, 8 * DH], BF16, RD)
    w2sb = alloc("w2sb", [128, 32 * D], BF16, RD)
    assert RD[0] <= 229376, RD[0]

    PS = [nc.alloc_psum_tensor(f"B{i}", [128, 512], F32) for i in range(7)]
    BT = nc.alloc_psum_tensor("BT", [128, 1024], BF16)

    p = Prog(nc)
    _warm_n = [0]

    def warm(wait, bank=0, n=64):
        _warm_n[0] += 1
        p.op("tensor", (lambda e, b=bank, n_=n: e.matmul(PS[b][:, 0:n_], c_eye_b[:, :],
                                                         c_eye_b[:, 0:n_], start=True, stop=True)),
             wait=wait, inc="cP")

    def mm(ps_ap, lhsT_ap, rhs_ap, start=True, stop=True):
        return lambda e: e.matmul(ps_ap, lhsT_ap, rhs_ap, start=start, stop=stop)

    def tr(ps_ap, in_ap):
        return lambda e: e.transpose(ps_ap, in_ap, c_eye_b[:])

    def cp(out_ap, in_ap):
        return lambda e: e.tensor_copy(out_ap, in_ap)

    def act2(out_ap, in_ap, func, bias=0.0, scale=1.0):
        return lambda e: e.activation(out_ap, in_ap, func, bias=bias, scale=scale)

    def acta(out_ap, in_ap, func, accum, bias=0.0, scale=1.0):
        return lambda e: e.activation(out_ap, in_ap, func, bias=bias, scale=scale, accum_out=accum)

    def ts1(out_ap, in_ap, s1, op0):
        return lambda e: e.tensor_scalar(out_ap, in_ap, s1, None, op0)

    def ts2(out_ap, in_ap, s1, s2, op0, op1):
        return lambda e: e.tensor_scalar(out_ap, in_ap, s1, s2, op0, op1)

    def tt(out_ap, a_ap, b_ap, opx):
        return lambda e: e.tensor_tensor(out_ap, a_ap, b_ap, opx)

    def stt(out_ap, in0, scalar, in1, op0, op1):
        return lambda e: e.scalar_tensor_tensor(out_ap, in0, scalar, in1, op0, op1)

    def red(out_ap, in_ap, opx, absv=False):
        return lambda e: e.tensor_reduce(out_ap, in_ap, AX.X, opx, apply_absolute_value=absv)

    def recip(out_ap, in_ap):
        return lambda e: e.reciprocal(out_ap, in_ap)

    def mset(ap, v):
        return lambda e: e.memset(ap, v)

    def dmas(*pairs):
        def f(e):
            return [e.dma_start(out=o_, in_=i_) for (o_, i_) in pairs]
        return f

    def dmas_nc(*pairs):
        def f(e):
            return [e.dma_start(out=o_, in_=i_, allow_slow_non_contiguous=True) for (o_, i_) in pairs]
        return f

    p.dma("gpsimd", dmas((fscr[:, 0:128], eye_d)), 1, inc="dG", label="ld_eye")
    p.op("vector", cp(c_eye_b[:], fscr[:, 0:128]), wait=["ld_eye"], inc="cV", label="eyeb")
    p.dma("gpsimd", dmas((c_pmask[:], pmask_d), (c_hmask[:], hmask_d), (c_bq[:], bq_d),
                         (c_bk[:], bk_d), (c_convw[:], convw_d), (c_convb[:], convb_d),
                         (c_xh[:], xh_d)), 7, inc="dG", label="ld_c1")
    p.dma("gpsimd", dmas((fscr[:, 0:2], cones_d), (fscr[:, 2:3], onescol_d)), 2,
          wait=["eyeb"], inc="dG", label="ld_c2")
    p.op("vector", cp(c_cones[:], fscr[:, 0:2]), wait=["ld_c2"], inc="cV")
    p.op("vector", cp(c_onescol[:], fscr[:, 2:3]), inc="cV", label="cones_b")
    p.dma("gpsimd", dmas((fscr[0:1, 0:128], onesrow_d)), 1, wait=["cones_b"], inc="dG", label="ld_or")
    p.op("vector", cp(c_onesrow[:], fscr[0:1, 0:128]), wait=["ld_or"], inc="cV", label="orow_b")
    p.dma("gpsimd", dmas((fscr[0:1, 0:128], m1row_d)), 1, wait=["orow_b"], inc="dG", label="ld_m1")
    p.op("vector", cp(c_m1row[:], fscr[0:1, 0:128]), wait=["ld_m1"], inc="cV", label="m1row_b")
    for c8 in range(8):
        p.dma("gpsimd", dmas((fscr[0:1, 0:128], bv_d[:, c8 * 128:(c8 + 1) * 128])), 1,
              wait=["m1row_b" if c8 == 0 else f"bvb{c8 - 1}"], inc="dG", label=f"ld_bv{c8}")
        p.op("vector", cp(c_bv_b[0:1, c8 * 128:(c8 + 1) * 128], fscr[0:1, 0:128]),
             wait=[f"ld_bv{c8}"], inc="cV", label=f"bvb{c8}")
    p.op("vector", ts1(c_bq[:], c_bq[:], S4, ALU.mult), wait=["ld_c1"], inc="cV")
    p.op("vector", ts1(c_bk[:], c_bk[:], S4, ALU.mult), inc="cV", label="bias_s4")
    p.op("vector", mset(qmax_run[:], 0.0), inc="cV", label="qmax0")
    p.op("vector", mset(fcol[:, 0:2], 1.0), inc="cV", label="fcol1")
    p.op("vector", mset(magicc[:], MAGIC), inc="cV", label="magic_c")
    for i0 in range(2):
        p.dma("sync", dmas((xbuf[i0][:].rearrange("p (s d) -> p s d", d=D),
                            xin[i0 * 512:i0 * 512 + 512, :].rearrange("(s pp) d -> pp s d", pp=128))), 1,
              wait=["bias_s4"], inc="dS", label=f"pa{i0}_xld")

    def wsl(j):
        return wstage[(j % 8) // 4][:, (j % 4) * 1024:(j % 4 + 1) * 1024]
    for wi, (wt_d, wsb) in enumerate([(wqT_d, wq_sb), (wkT_d, wk_sb), (wvT_d, wv_sb)]):
        q_eng, q_inc = [("scalar", "dA"), ("gpsimd", "dG"), ("sync", "dS")][wi]
        for k in range(8):
            j = wi * 8 + k
            wwait = [f"wcast_{j - 8}"] if j >= 8 else ["ld_c2"]
            p.dma(q_eng, dmas((wsl(j), wt_d[k * 128:(k + 1) * 128, :])), 1,
                  wait=wwait, inc=q_inc, label=f"wld_{j}")
            c_eng, c_inc = ("vector", "cV") if j % 2 == 0 else ("gpsimd", "cG")
            p.op(c_eng, cp(wsb[:, k * 1024:(k + 1) * 1024], wsl(j)),
                 wait=[f"wld_{j}"], inc=c_inc, label=f"wcast_{j}")

    wmean_chunks = [("w1", w1T_d, k, 4096, k) for k in range(8)] + \
                   [("w2", w2T_d, k, 1024, 8 + k) for k in range(32)]

    def emit_wmean(j):
        name, src, k, fdim, abscol = wmean_chunks[j]
        slot = j % 2
        wwait = [f"wm_red_{j - 2}"] if j >= 2 else ["wcast_22", "wcast_23"]
        p.dma("sync", dmas((wstage[slot][:, 0:fdim], src[k * 128:(k + 1) * 128, :])), 1,
              wait=wwait, inc="dS", label=f"wm_ld_{j}")
        p.op("vector", red(wabs[:, abscol:abscol + 1], wstage[slot][:, 0:fdim], ALU.add, absv=True),
             wait=[f"wm_ld_{j}"], inc="cV", label=f"wm_red_{j}")

    def emit_ln(x_ap_fn, nsub, pfx, wait0, psz=128, rv_wait=()):
        for s in range(nsub):
            xs = x_ap_fn(s)
            w = list(wait0) if s == 0 else []
            p.op("vector", (lambda e, xs=xs, psz=psz: e.bn_stats(sm[0:psz, 16:22], xs[0:psz, 0:512])),
                 wait=w, inc="cV")
            p.op("vector", (lambda e, xs=xs, psz=psz: e.bn_stats(sm[0:psz, 22:28], xs[0:psz, 512:1024])), inc="cV")
            p.op("vector", (lambda e, psz=psz: e.bn_aggr(sm[0:psz, 28:30], sm[0:psz, 16:22])),
                 inc="cV", sw=True)
            p.op("vector", (lambda e, psz=psz: e.bn_aggr(sm[0:psz, 30:32], sm[0:psz, 22:28])),
                 inc="cV", sw=True)
            p.op("vector", tt(lnmv[0:psz, 2 * s:2 * s + 1], sm[0:psz, 28:29], sm[0:psz, 30:31], ALU.add),
                 inc="cV", sw=True)
            p.op("vector", ts1(lnmv[0:psz, 2 * s:2 * s + 1], lnmv[0:psz, 2 * s:2 * s + 1], 0.5, ALU.mult),
                 inc="cV", sw=True)
            p.op("vector", tt(sm[0:psz, 14:15], sm[0:psz, 28:29], sm[0:psz, 30:31], ALU.subtract),
                 inc="cV", sw=True)
            p.op("vector", tt(sm[0:psz, 14:15], sm[0:psz, 14:15], sm[0:psz, 14:15], ALU.mult),
                 inc="cV", sw=True)
            p.op("vector", tt(sm[0:psz, 13:14], sm[0:psz, 29:30], sm[0:psz, 31:32], ALU.add),
                 inc="cV", sw=True)
            p.op("vector", stt(sm[0:psz, 13:14], sm[0:psz, 14:15], 0.5, sm[0:psz, 13:14], ALU.mult, ALU.add),
                 inc="cV", sw=True)
            p.op("vector", ts1(lnmv[0:psz, 2 * s + 1:2 * s + 2], sm[0:psz, 13:14], 0.5, ALU.mult),
                 inc="cV", sw=True)
            p.op("vector", ts1(veps4[0:psz, s:s + 1], lnmv[0:psz, 2 * s + 1:2 * s + 2], EPS, ALU.add),
                 inc="cV", sw=True, label=f"{pfx}_v{s}")
        p.op("vector", cp(rstd4[0:psz, 0:nsub], veps4[0:psz, 0:nsub]), wait=list(rv_wait), inc="cV",
             sw=True, label=f"{pfx}_rv")
        p.op("scalar", act2(lnscr[0:psz, 0:nsub], rstd4[0:psz, 0:nsub], AF.Ln), wait=[f"{pfx}_rv"], inc="cS")
        p.op("scalar", act2(rstd4[0:psz, 0:nsub], lnscr[0:psz, 0:nsub], AF.Exp, scale=-0.5),
             inc="cS", sw=True, label=f"{pfx}_rstd")

    def emit_pa_ln(i):
        lp = f"pa{i}"
        xb_ = xbuf[i % 2]
        emit_ln(lambda s: xb_[:, s * D:(s + 1) * D], 4, lp, [f"{lp}_xld"],
                rv_wait=([f"pa{i-1}_y3"] if i > 0 else ()))
        for s in range(4):
            p.op("vector", stt(negmr4[:, s:s + 1], lnmv[:, 2 * s:2 * s + 1], -1.0, rstd4[:, s:s + 1],
                               ALU.mult, ALU.mult),
                 wait=[f"{lp}_rstd"] if s == 0 else (), inc="cV", sw=True,
                 label=f"{lp}_nmr" if s == 3 else None)
        for s in range(4):
            yw = [f"{lp}_nmr"] + ([f"pa{i-1}_lm"] if i > 0 else [])
            p.op("scalar", act2(y1tm[:, s * D:(s + 1) * D], xb_[:, s * D:(s + 1) * D], AF.Identity,
                                bias=negmr4[:, s:s + 1], scale=rstd4[:, s:s + 1]),
                 wait=yw, inc="cS", label=f"{lp}_y{s}")

    emit_pa_ln(0)
    for i in range(PA_NT):
        t0 = i * 512
        lp = f"pa{i}"

        for pr in range(4):
            twait = [f"{lp}_tev{pr - 1}"] if pr >= 1 else \
                ([f"pa{i - 1}_tev3", f"pa{i - 1}_qnr3", f"pa{i - 1}_qkvdone"] if i > 0 else ["eyeb"])
            for c in (2 * pr, 2 * pr + 1):
                half = c % 2
                for s in range(4):
                    p.op("tensor", tr(BT[:, half * 512 + s * 128: half * 512 + (s + 1) * 128],
                                      y1tm[:, s * D + c * 128: s * D + (c + 1) * 128]),
                         wait=[f"{lp}_y{s}"] + twait, inc="cP",
                         label=f"{lp}_tr{pr}" if (s == 3 and c == 2 * pr + 1) else None)
            p.op("scalar", act2(y1fm[:, pr * 1024:(pr + 1) * 1024], BT[:, 0:1024], AF.Copy),
                 wait=[f"{lp}_tr{pr}"], inc="cS", label=f"{lp}_tev{pr}")

        for c in range(8):
            for s in range(4):
                p.op("tensor", mm(PS[6][:, c * 64 + i * 8 + s * 2: c * 64 + i * 8 + s * 2 + 2],
                                  y1tm[:, s * D + c * 128: s * D + (c + 1) * 128], c_cones[:]),
                     wait=[f"{lp}_y{s}", "cones_b"], inc="cP",
                     label=f"{lp}_lm" if (c == 7 and s == 3) else None)

        def emit_qk(nm, wsb, stg, bias, dst):
            for m in range(8):
                bank = PS[m % 2]
                for k in range(8):
                    mmw = [f"{lp}_tev{k // 2}"]
                    if m >= 2:
                        mmw.append(f"{lp}_{nm}ev{m - 2}")
                    elif nm == "k":
                        mmw.append(f"{lp}_qev{6 + m}")
                    elif i > 0:
                        mmw.append(f"pa{i-1}_kev{6 + m}")
                    p.op("tensor", mm(bank[:, 0:512], wsb[:, k * 1024 + m * 128: k * 1024 + (m + 1) * 128],
                                      y1fm[:, k * 512:(k + 1) * 512], start=(k == 0), stop=(k == 7)),
                         wait=mmw, inc="cP", label=f"{lp}_{nm}mm{m}" if k == 7 else None)
                evw = [f"{lp}_{nm}mm{m}"]
                if i > 0:
                    evw.append(f"pa{i-1}_{nm}sp")
                p.op("scalar", act2(stg[:, m * 512:(m + 1) * 512], bank[:, 0:512], AF.Identity,
                                    bias=bias[:, m:m + 1], scale=S4),
                     wait=evw, inc="cS", label=f"{lp}_{nm}ev{m}")
            p.dma("scalar", (lambda e, stg=stg, dst=dst, t0=t0:
                             [e.dma_start(out=dst[m_, :, t0:t0 + 512], in_=stg[:, m_ * 512:(m_ + 1) * 512])
                              for m_ in range(8)]), 8,
                  wait=[f"{lp}_{nm}ev7"], inc="dA", label=f"{lp}_{nm}sp")

        emit_qk("q", wq_sb, qstage, c_bq, q_sp)
        p.op("vector", tt(qsq[:], qstage[:], qstage[:], ALU.mult), wait=[f"{lp}_qev7"], inc="cV",
             label=f"{lp}_qsq")
        for c in range(1, 8):
            p.op("vector", tt(qsq[:, 0:512], qsq[:, 0:512], qsq[:, c * 512:(c + 1) * 512], ALU.add),
                 wait=[f"{lp}_qsq"] if c == 1 else (), inc="cV", sw=True, label=f"{lp}_qt{c}")
        for bb_ in range(4):
            p.op("vector", cp(pscr_b[:, bb_ * 128:(bb_ + 1) * 128], qsq[:, bb_ * 128:(bb_ + 1) * 128]),
                 inc="cV", sw=True, label=f"{lp}_qnc{bb_}")
        emit_qk("k", wk_sb, kstage, c_bk, k_sp)

        if i + 2 < PA_NT:
            xb2 = xbuf[i % 2]
            p.dma("sync", dmas((xb2[:].rearrange("p (s d) -> p s d", d=D),
                                xin[(i + 2) * 512:(i + 2) * 512 + 512, :].rearrange(
                                    "(s pp) d -> pp s d", pp=128))), 1,
                  wait=[f"pa{i}_y3"], inc="dS", label=f"pa{i + 2}_xld")
        if i + 1 < PA_NT:
            emit_pa_ln(i + 1)

        for s in range(4):
            vw = [f"{lp}_vev{s - 2}"] if s >= 2 else ([f"pa{i-1}_vev{2 + s}"] if i > 0 else [])
            for k in range(8):
                p.op("tensor", mm(PS[2 + (s % 2)][:, 0:512],
                                  y1fm[:, k * 512 + s * 128: k * 512 + s * 128 + 128],
                                  wv_sb[:, k * 1024: k * 1024 + 512], start=(k == 0), stop=(k == 7)),
                     wait=[f"{lp}_tev{k // 2}"] + vw, inc="cP")
            for k in range(8):
                p.op("tensor", mm(PS[4 + (s % 2)][:, 0:512],
                                  y1fm[:, k * 512 + s * 128: k * 512 + s * 128 + 128],
                                  wv_sb[:, k * 1024 + 512: k * 1024 + 1024], start=(k == 0), stop=(k == 7)),
                     inc="cP", label=f"{lp}_vmm{s}" if k == 7 else None)
            evw = [f"{lp}_vmm{s}"]
            if s >= 1:
                evw.append(f"{lp}_vsp{s-1}")
            elif i > 0:
                evw.append(f"pa{i-1}_vsp3")
            p.op("scalar", act2(vstage[:, s * D: s * D + 512], PS[2 + (s % 2)][:, 0:512], AF.Copy),
                 wait=evw, inc="cS")
            p.op("scalar", act2(vstage[:, s * D + 512: (s + 1) * D], PS[4 + (s % 2)][:, 0:512], AF.Copy),
                 inc="cS", label=f"{lp}_vev{s}")
            p.dma("scalar", dmas((v_sp[i * 4 + s, :, :], vstage[:, s * D:(s + 1) * D])), 1,
                  wait=[f"{lp}_vev{s}"], inc="dA", label=f"{lp}_vsp{s}")

        for bb_ in range(4):
            p.op("tensor", tr(BT[:, 512 + bb_ * 128: 512 + (bb_ + 1) * 128],
                              pscr_b[:, bb_ * 128:(bb_ + 1) * 128]),
                 wait=[f"{lp}_qnc{bb_}", f"{lp}_tev3"] + ([f"{lp}_qnr{bb_ - 1}"] if bb_ >= 1 else []),
                 inc="cP", label=f"{lp}_qntr{bb_}")
            p.op("vector", red(qmx[:, bb_:bb_ + 1], BT[:, 512 + bb_ * 128: 512 + (bb_ + 1) * 128], ALU.add),
                 wait=[f"{lp}_qntr{bb_}"], inc="cV", label=f"{lp}_qnr{bb_}")
        p.op("vector", red(sm[:, 15:16], qmx[:, 0:4], ALU.max), wait=["qmax0"], inc="cV", sw=True)
        p.op("vector", tt(qmax_run[:], qmax_run[:], sm[:, 15:16], ALU.max), inc="cV", sw=True,
             label=f"{lp}_qmax")
        p.op("vector", cp(sm[:, 14:15], sm[:, 14:15]), inc="cV", label=f"{lp}_qkvdone")
        for j in range(5 * i, min(5 * i + 5, 40)):
            emit_wmean(j)

    if "y1" in dbg:
        p.dma("sync", dmas((DBG("y1", [128, 4 * D]), xbuf[(PA_NT - 1) % 2][:]), (DBG("y1b", [128, 4 * D], BF16), y1tm[:]),
                           (DBG("lnst", [128, 32]), sm[:]), (DBG("lnmv", [128, 8]), lnmv[:]),
                           (DBG("rstd", [128, 4]), rstd4[:]), (DBG("veps", [128, 4]), veps4[:])), 6,
              wait=[f"pa{PA_NT-1}_y3"], inc="dS", label="dbg_y1")

    emit_ln(lambda s: c_xh[:], 1, "pah", ["ld_c1", f"pa{PA_NT-1}_y3"])
    p.op("vector", ts2(y1tm[:, 0:D], c_xh[:], lnmv[:, 0:1], rstd4[:, 0:1], ALU.subtract, ALU.mult),
         wait=["pah_rstd", f"pa{PA_NT-1}_lm"], inc="cV", label="pah_y0")
    for c in range(8):
        trw = ["pah_y0", f"pa{PA_NT-1}_qkvdone", f"pa{PA_NT-1}_tev3", f"pa{PA_NT-1}_qnr3"]
        if c >= 1:
            trw.append(f"pah_tev{c - 1}")
        p.op("tensor", tr(BT[:, c * 128:(c + 1) * 128], y1tm[:, c * 128:(c + 1) * 128]),
             wait=trw, inc="cP", label=f"pah_tr{c}")
        p.op("vector", cp(y1fm[:, c * 128:(c + 1) * 128], BT[:, c * 128:(c + 1) * 128]),
             wait=[f"pah_tr{c}"], inc="cV", label=f"pah_tev{c}")
    for m in range(8):
        for k in range(8):
            p.op("tensor", mm(PS[m % 2][:, 0:128], wq_sb[:, k * 1024 + m * 128: k * 1024 + (m + 1) * 128],
                              y1fm[:, k * 128:(k + 1) * 128], start=(k == 0), stop=(k == 7)),
                 wait=[f"pah_tev{k}"] + ([f"pah_qev{m - 2}"] if m >= 2 else [f"pa{PA_NT-1}_kev{6+m}"]),
                 inc="cP", label=f"pah_qmm{m}" if k == 7 else None)
        p.op("scalar", act2(qh_sb[:, m * 128:(m + 1) * 128], PS[m % 2][:, 0:128], AF.Identity,
                            bias=c_bq[:, m:m + 1], scale=S4),
             wait=[f"pah_qmm{m}"], inc="cS", label=f"pah_qev{m}")

    p.op("scalar", act2(xlm_sb[:], PS[6][:, 0:512], AF.Copy), wait=[f"pa{PA_NT-1}_lm"], inc="cS",
         label="xlm_ev")
    p.op("vector", cp(xlm_b[:], xlm_sb[:]), wait=["xlm_ev"], inc="cV", label="xlm_bb")
    for nm, wsb, dstp, bias in (("q", wq_sb, qlm_own, c_bq), ("k", wk_sb, klm_own, c_bk)):
        for m in range(8):
            bank = PS[2 + (m % 2)]
            for k in range(8):
                mmw = ["xlm_bb"]
                if m >= 2:
                    mmw.append(f"lmp_{nm}ev{m - 2}")
                elif nm == "k":
                    mmw.append(f"lmp_qev{6 + m}")
                else:
                    mmw.append(f"pa{PA_NT-1}_vev{2 + (m % 2)}")
                p.op("tensor", mm(bank[:, 0:64], wsb[:, k * 1024 + m * 128: k * 1024 + (m + 1) * 128],
                                  xlm_b[:, k * 64:(k + 1) * 64], start=(k == 0), stop=(k == 7)),
                     wait=mmw, inc="cP", label=f"lmp_{nm}mm{m}" if k == 7 else None)
            p.op("scalar", act2(dstp[:, m * 64:(m + 1) * 64], bank[:, 0:64], AF.Identity,
                                bias=bias[:, m:m + 1], scale=S4 / SEG),
                 wait=[f"lmp_{nm}mm{m}"], inc="cS", label=f"lmp_{nm}ev{m}")
    p.dma("scalar", dmas((e1_in[:, 0:512], qlm_own[:]), (e1_in[:, 512:1024], klm_own[:])), 2,
          wait=["lmp_qev7", "lmp_kev7"], inc="dA", label="e1_stage")
    p.cc(lambda e: e.collective_compute("AllGather", ALU.bypass, replica_groups=[list(range(8))],
                                        ins=[e1_in], outs=[e1_out]),
         wait=["e1_stage"], inc="cc1", label="e1_done")

    p.op("vector", red(wabs[:, 0:1], wabs[:, 0:8], ALU.add), wait=["wm_red_7"], inc="cV", sw=True)
    p.op("vector", red(wabs[:, 8:9], wabs[:, 8:40], ALU.add), wait=["wm_red_39"], inc="cV", sw=True, label="wabs_red")
    p.op("vector", cp(wabs_b[:, 0:1], wabs[:, 0:1]), wait=["wabs_red"], inc="cV", sw=True)
    p.op("vector", cp(wabs_b[:, 1:2], wabs[:, 8:9]), inc="cV", sw=True, label="wabs_bb")
    p.op("tensor", mm(PS[6][0:1, 0:2], c_onescol[:], wabs_b[:, 0:2]),
         wait=["wabs_bb", "cones_b", "xlm_ev"], inc="cP", label="wsum_mm")
    p.op("vector", ts2(rowf[0:1, 0:1], PS[6][0:1, 0:1], 1.0 / (DH * D), 1e-5, ALU.mult, ALU.max),
         wait=["wsum_mm"], inc="cV")
    p.op("vector", ts2(rowf[0:1, 1:2], PS[6][0:1, 1:2], 1.0 / (DH * D), 1e-5, ALU.mult, ALU.max),
         inc="cV", sw=True, label="rsw_row")
    p.op("vector", cp(rowb[0:1, 0:2], rowf[0:1, 0:2]), wait=["rsw_row"], inc="cV", sw=True, label="rsw_rb")
    p.op("tensor", mm(PS[6][:, 2:4], c_onesrow[:], rowb[0:1, 0:2]), wait=["rsw_rb", "orow_b"],
         inc="cP", label="rsw_bc")
    p.op("vector", cp(rsw1[:], PS[6][:, 2:3]), wait=["rsw_bc"], inc="cV")
    p.op("vector", cp(rsw2[:], PS[6][:, 3:4]), inc="cV", label="rsw_done")
    p.op("vector", recip(sw1[:], rsw1[:]), inc="cV", sw=True)
    p.op("vector", recip(sw2[:], rsw2[:]), inc="cV", sw=True, label="sw_done")

    wq_chunks = []
    for i_ in range(8):
        for h_ in range(2):
            wq_chunks.append(("w1", w1T_d, w1q_sp, i_, 2 * i_ + h_, 2048, 0))
        for k_ in range(4 * i_, 4 * i_ + 4):
            wq_chunks.append(("w2", w2T_d, w2q_sp, k_, None, 1024, 1))

    def emit_wq(j):
        name, src, dst, k, hh, fdim, wsel = wq_chunks[j]
        slot = j % 4
        if name == "w1":
            src_ap = src[k * 128:(k + 1) * 128, (hh % 2) * 2048:(hh % 2) * 2048 + 2048]
            dst_ap = dst[k * 128:(k + 1) * 128, (hh % 2) * 2048:(hh % 2) * 2048 + 2048]
        else:
            src_ap = src[k * 128:(k + 1) * 128, :]
            dst_ap = dst[k * 128:(k + 1) * 128, :]
        wwait = [f"wq_st_{j - 4}"] if j >= 4 else ["wm_red_38", "wm_red_39"]
        p.dma("scalar", dmas((wqst[slot][:, 0:fdim], src_ap)), 1,
              wait=wwait, inc="dA", label=f"wq_ld_{j}")
        swc = sw1[:, 0:1] if wsel == 0 else sw2[:, 0:1]
        p.op("scalar", act2(wqst[slot][:, 0:fdim], wqst[slot][:, 0:fdim], AF.Identity,
                            bias=magicc[:, 0:1], scale=swc),
             wait=[f"wq_ld_{j}", "sw_done", "magic_c"], inc="cS", label=f"wq_a_{j}")
        p.op("vector", ts2(wqst[slot][:, 0:fdim], wqst[slot][:, 0:fdim], MAGIC, 1.0,
                           ALU.subtract, ALU.min),
             wait=[f"wq_a_{j}"], inc="cV")
        p.op("vector", ts1(wqo[slot][:, 0:fdim], wqst[slot][:, 0:fdim], -1.0, ALU.max),
             inc="cV", label=f"wq_q_{j}")
        p.dma("gpsimd", dmas((dst_ap, wqo[slot][:, 0:fdim])), 1,
              wait=[f"wq_q_{j}"], inc="dG", label=f"wq_st_{j}")

    p.op("vector", mset(qlm[:], 0.0), wait=["e1_done"], inc="cV")
    p.op("vector", mset(klm[:], 0.0), inc="cV", label="lmacc0")
    for r in range(8):
        slot = r % 2
        gw = ["e1_done", "lmacc0"] + ([f"s2a_acc{r - 2}"] if r >= 2 else [])
        p.dma("sync", dmas((grbuf[slot][:, 0:1024], e1_out[r * 128:(r + 1) * 128, :])), 1,
              wait=gw, inc="dS", label=f"s2a_gld{r}")
        half = r % 2
        for c in range(8):
            p.op("vector", stt(qlm[:, c * 128 + half * 64: c * 128 + half * 64 + 64],
                               grbuf[slot][:, c * 64:(c + 1) * 64], c_pmask[:, r:r + 1],
                               qlm[:, c * 128 + half * 64: c * 128 + half * 64 + 64], ALU.mult, ALU.add),
                 wait=[f"s2a_gld{r}"], inc="cV", sw=(c == 0))
            p.op("vector", stt(klm[:, c * 128 + half * 64: c * 128 + half * 64 + 64],
                               grbuf[slot][:, 512 + c * 64: 512 + (c + 1) * 64], c_pmask[:, r:r + 1],
                               klm[:, c * 128 + half * 64: c * 128 + half * 64 + 64], ALU.mult, ALU.add),
                 inc="cV", label=f"s2a_acc{r}")
    p.op("vector", mset(qlmb[:], 0.0), wait=["s2a_acc7"], inc="cV", sw=True)
    p.op("vector", mset(klmb[:], 0.0), inc="cV")
    for h in range(H):
        pb_ = 64 * (h % 2)
        cc0_ = (h // 2) * 128
        p.op("vector", cp(qlmb[pb_:pb_ + 64, h * 128:(h + 1) * 128], qlm[pb_:pb_ + 64, cc0_:cc0_ + 128]),
             inc="cV")
        p.op("vector", cp(klmb[pb_:pb_ + 64, h * 128:(h + 1) * 128], klm[pb_:pb_ + 64, cc0_:cc0_ + 128]),
             inc="cV", label=f"lmp_pad{h}")
    p.op("vector", cp(sm[:, 12:13], sm[:, 12:13]), inc="cV", label="qlmb_done")
    p.op("vector", cp(sm[:, 12:13], sm[:, 12:13]), inc="cV", label="klmb_done")
    if "qlm" in dbg:
        p.dma("sync", dmas((DBG("qlm", [128, 1024]), qlm[:]), (DBG("klm", [128, 1024]), klm[:])), 2,
              wait=["klmb_done"], inc="dS", label="dbg_qlm")

    def lmnorm(src, outc, lbl):
        p.op("scalar", act2(sq_scr[:, 0:1024], src, AF.Square), wait=["klmb_done"], inc="cS",
             label=f"{lbl}_sq")
        p.op("vector", tt(sq_scr[:, 0:128], sq_scr[:, 0:128], sq_scr[:, 128:256], ALU.add),
             wait=[f"{lbl}_sq"], inc="cV", sw=True)
        for c in range(2, 8):
            p.op("vector", tt(sq_scr[:, 0:128], sq_scr[:, 0:128], sq_scr[:, c * 128:(c + 1) * 128], ALU.add),
                 inc="cV", sw=True)
        p.op("vector", cp(pscr_b[:, 0:128], sq_scr[:, 0:128]), inc="cV", sw=True, label=f"{lbl}_cb")
        p.op("tensor", tr(BT[:, 0:128], pscr_b[:, 0:128]), wait=[f"{lbl}_cb"], inc="cP", label=f"{lbl}_tr")
        p.op("vector", red(sm[:, 13:14], BT[:, 0:128], ALU.add), wait=[f"{lbl}_tr"], inc="cV", sw=True)
        p.op("vector", cp(smb[:, 1:2], sm[:, 13:14]), inc="cV", sw=True, label=f"{lbl}_c2")
        p.op("tensor", tr(BT[0:1, 128:256], smb[:, 1:2]), wait=[f"{lbl}_c2"], inc="cP", label=f"{lbl}_tr2")
        p.op("vector", red(rowf[0:1, outc:outc + 1], BT[0:1, 128:256], ALU.max),
             wait=[f"{lbl}_tr2"], inc="cV", label=f"{lbl}_mx")
        p.op("scalar", act2(rowf[0:1, outc + 1:outc + 2], rowf[0:1, outc:outc + 1], AF.Ln),
             wait=[f"{lbl}_mx"], inc="cS")
        p.op("scalar", act2(rowf[0:1, outc:outc + 1], rowf[0:1, outc + 1:outc + 2], AF.Exp, scale=0.5),
             inc="cS", label=lbl)

    lmnorm(klm[:], 8, "knorm")
    lmnorm(qlm[:], 10, "qnorm")

    p.op("vector", mset(csmax[:], 0.0), inc="cV", label="cs0")
    for h in range(H):
        bank = PS[h % 2]
        k2w = ["qlmb_done", "klmb_done"] + ([f"k2_{h - 2}"] if h >= 2 else [])
        p.op("tensor", mm(bank[:, 0:128], qlmb[:, h * 128:(h + 1) * 128], klmb[:, h * 128:(h + 1) * 128]),
             wait=k2w, inc="cP", label=f"k2mm_{h}")
        p.op("vector", red(sm[:, 12:13], bank[:, 0:128], ALU.max), wait=[f"k2mm_{h}"], inc="cV")
        p.op("vector", ts1(sm[:, 11:12], sm[:, 12:13], -1.0, ALU.mult), inc="cV", sw=True, label=f"k2neg_{h}")
        p.op("scalar", acta(k2sb[:, h * 128:(h + 1) * 128], bank[:, 0:128], AF.Exp,
                            sm[:, 10:11], bias=sm[:, 11:12]),
             wait=[f"k2neg_{h}"], inc="cS", label=f"k2exp_{h}")
        p.op("vector", recip(sm[:, 9:10], sm[:, 10:11]), wait=[f"k2exp_{h}"], inc="cV")
        p.op("vector", ts1(k2sb[:, h * 128:(h + 1) * 128], k2sb[:, h * 128:(h + 1) * 128],
                           sm[:, 9:10], ALU.mult), inc="cV", sw=True, label=f"k2n_{h}")
        p.op("tensor", mm(PS[2][:, (h % 4):(h % 4) + 1], k2sb[:, h * 128:(h + 1) * 128], c_onescol[:]),
             wait=[f"k2n_{h}"] + ([f"k2_{h - 1}"] if h >= 1 else []), inc="cP", label=f"k2cs_{h}")
        p.op("vector", tt(csmax[:], csmax[:], PS[2][:, (h % 4):(h % 4) + 1], ALU.max),
             wait=[f"k2cs_{h}", "cs0"], inc="cV", sw=True, label=f"k2_{h}")
    if "k2" in dbg:
        p.dma("sync", dmas((DBG("k2", [128, 2048], BF16), k2sb[:])), 1, wait=["k2_15"], inc="dS",
              label="dbg_k2")

    for i in range(PA_NT):
        t0 = i * 512
        lp = f"pb{i}"
        slot = i % 2
        kw = [f"pb{i - 2}_zm"] if i >= 2 else ["qlmb_done"]
        p.dma("sync", dmas((kt[slot][:].rearrange("p (c t) -> p c t", t=512),
                            k_sp[:, :, t0:t0 + 512].rearrange("c p t -> p c t"))), 1,
              wait=kw + [f"pa{PA_NT-1}_ksp"], inc="dS", label=f"{lp}_kld")
        p.dma("sync", dmas((vt[slot][:].rearrange("p (s f) -> p s f", f=D),
                            v_sp[i * 4:(i + 1) * 4, :, :].rearrange("s p f -> p s f"))), 1,
              wait=kw + [f"pa{PA_NT-1}_vsp3"], inc="dS", label=f"{lp}_vld")

        p.op("scalar", act2(ksq[:], kt[slot][:], AF.Square), wait=[f"{lp}_kld"], inc="cS",
             label=f"{lp}_ksq")
        for c in range(1, 8):
            p.op("vector", tt(ksq[:, 0:512], ksq[:, 0:512], ksq[:, c * 512:(c + 1) * 512], ALU.add),
                 wait=[f"{lp}_ksq"] if c == 1 else (), inc="cV", sw=True)
        for bb_ in range(4):
            p.op("vector", cp(pscr_b[:, 0:128], ksq[:, bb_ * 128:(bb_ + 1) * 128]), inc="cV", sw=True,
                 label=f"{lp}_knc{bb_}")
            p.op("tensor", tr(BT[:, bb_ * 128:(bb_ + 1) * 128], pscr_b[:, 0:128]),
                 wait=[f"{lp}_knc{bb_}"] + ([f"{lp}_knr{bb_ - 1}"] if bb_ >= 1 else
                                            ([f"pb{i-1}_km"] if i > 0 else [])),
                 inc="cP", label=f"{lp}_kntr{bb_}")
            p.op("vector", red(kmx[:, bb_:bb_ + 1], BT[:, bb_ * 128:(bb_ + 1) * 128], ALU.add),
                 wait=[f"{lp}_kntr{bb_}"], inc="cV", label=f"{lp}_knr{bb_}")
        for bb_ in range(4):
            warm([f"{lp}_knr{bb_}"], bank=0)
        p.op("vector", red(sm[:, 8:9], kmx[:, 0:4], ALU.max), inc="cV", sw=True)
        p.op("vector", cp(smb[:, 2:3], sm[:, 8:9]), inc="cV", sw=True, label=f"{lp}_kmc")
        p.op("tensor", tr(BT[0:1, 512:640], smb[:, 2:3]),
             wait=[f"{lp}_kmc", f"{lp}_kntr3"], inc="cP", label=f"{lp}_kmtr")
        p.op("vector", red(rowf[0:1, 12:13], BT[0:1, 512:640], ALU.max), wait=[f"{lp}_kmtr"], inc="cV",
             label=f"{lp}_km")
        p.op("scalar", act2(rowf[0:1, 11:12], rowf[0:1, 12:13], AF.Ln), wait=[f"{lp}_km"], inc="cS")
        p.op("scalar", act2(rowf[0:1, 12:13], rowf[0:1, 11:12], AF.Exp, scale=0.5), inc="cS", sw=True,
             label=f"{lp}_ksqrt")
        p.op("vector", tt(rowf[0:1, 13:14], rowf[0:1, 12:13], rowf[0:1, 10:11], ALU.mult),
             wait=[f"{lp}_ksqrt", "qnorm"], inc="cV", sw=True)
        p.op("vector", ts1(rowf[0:1, 13:14], rowf[0:1, 13:14], 1.02, ALU.mult),
             inc="cV", sw=True, label=f"{lp}_b2")
        p.op("vector", cp(rowb[0:1, 4:5], rowf[0:1, 13:14]), inc="cV", sw=True, label=f"{lp}_b2b")
        p.op("tensor", mm(PS[6][:, 16:17], c_m1row[:], rowb[0:1, 4:5]),
             wait=[f"{lp}_b2b", "m1row_b", "wsum_mm"] + ([f"pb{i-1}_negb2", f"pb{i-1}s3_zacc"] if i > 0 else []),
             inc="cP", label=f"{lp}_b2bc")
        nbw = [f"{lp}_b2bc"] + ([f"pb{i-1}s3_eexp3"] if i > 0 else [])
        p.op("vector", cp(negb2[:], PS[6][:, 16:17]), wait=nbw, inc="cV", label=f"{lp}_negb2")

        for s in range(4):
            sub = f"{lp}s{s}"
            prev = f"{lp}s{s-1}" if s > 0 else (f"pb{i-1}s3" if i > 0 else None)
            for h in range(H):
                bank = PS[h // 4]
                col = (h % 4) * 128
                mw = [f"{lp}_kld", "qlmb_done"]
                if prev is not None:
                    mw.append(f"{prev}_eexp{h // 4}")
                p.op("tensor", mm(bank[:, col:col + 128],
                                  kt[slot][:, (h // 2) * 512 + s * 128:(h // 2) * 512 + s * 128 + 128],
                                  qlmb[:, h * 128:(h + 1) * 128]),
                     wait=mw, inc="cP", label=f"{sub}_st{h}" if h % 4 == 3 else None)
            for b in range(4):
                ew = [f"{sub}_st{b * 4 + 3}", f"{lp}_negb2"]
                if prev is not None:
                    ew.append(f"{prev}_zdone")
                p.op("scalar", act2(esb[:, (s % 2) * 2048 + b * 512:(s % 2) * 2048 + (b + 1) * 512],
                                    PS[b][:, 0:512], AF.Exp, bias=negb2[:]),
                     wait=ew, inc="cS", label=f"{sub}_eexp{b}")
            for h in range(H):
                ebase = (s % 2) * 2048 + h * 128
                zw = [f"{sub}_eexp{h // 4}", f"{lp}_vld"]
                if prev is not None:
                    zw.append(f"{prev}_zacc")
                p.op("tensor", mm(PS[4 + h // 8][:, (h % 8) * 64:(h % 8) * 64 + 64],
                                  esb[:, ebase:ebase + 128],
                                  vt[slot][:, s * D + h * 64: s * D + h * 64 + 64]),
                     wait=zw, inc="cP")
                p.op("tensor", mm(PS[6][:, 32 + h: 33 + h], esb[:, ebase:ebase + 128], c_onescol[:]),
                     inc="cP", label=f"{sub}_zdone" if h == H - 1 else None)
            if i == 0 and s == 0:
                p.op("vector", cp(mrun[0:1, 0:1], rowf[0:1, 13:14]), wait=[f"{lp}_b2", f"{sub}_zdone"],
                     inc="cV", label=f"{lp}_mupd")
                p.op("vector", cp(zrun[:, 0:512], PS[4][:, 0:512]), inc="cV")
                p.op("vector", cp(zrun[:, 512:1024], PS[5][:, 0:512]), inc="cV")
                p.op("vector", cp(zrun[:, 1024:1040], PS[6][:, 32:48]), inc="cV", label=f"{sub}_zacc")
            else:
                zw2 = [f"{sub}_zdone"]
                if s == 0:
                    p.op("vector", tt(rowf[0:1, 14:15], mrun[0:1, 0:1], rowf[0:1, 13:14], ALU.max),
                         wait=[f"{lp}_b2", f"pb{i-1}_mupd", f"{sub}_zdone"], inc="cV", sw=True)
                    p.op("vector", tt(rowf[0:1, 15:16], mrun[0:1, 0:1], rowf[0:1, 14:15], ALU.subtract),
                         inc="cV", sw=True)
                    p.op("vector", tt(rowf[0:1, 16:17], rowf[0:1, 13:14], rowf[0:1, 14:15], ALU.subtract),
                         inc="cV", sw=True, label=f"{lp}_fsub")
                    p.op("scalar", act2(rowf[0:1, 15:17], rowf[0:1, 15:17], AF.Exp), wait=[f"{lp}_fsub"],
                         inc="cS", label=f"{lp}_fexp")
                    p.op("vector", cp(mrun[0:1, 0:1], rowf[0:1, 14:15]), wait=[f"{lp}_fexp"], inc="cV", sw=True,
                         label=f"{lp}_mupd")
                    p.op("vector", cp(rowb[0:1, 6:8], rowf[0:1, 15:17]), inc="cV", label=f"{lp}_fb")
                    p.op("tensor", mm(PS[6][:, 17:19], c_m1row[:], rowb[0:1, 6:8]),
                         wait=[f"{lp}_fb"], inc="cP", label=f"{lp}_fbc")
                    p.op("vector", ts1(fcol[:, 0:2], PS[6][:, 17:19], -1.0, ALU.mult),
                         wait=[f"{lp}_fbc"], inc="cV", label=f"{lp}_fcol")
                    p.op("vector", ts1(zrun[:, 0:1040], zrun[:, 0:1040], fcol[:, 0:1], ALU.mult),
                         wait=[f"{lp}_fcol"], inc="cV", sw=True, label=f"{lp}_zrs")
                    zw2.append(f"{lp}_zrs")
                fz = fcol[:, 1:2]
                p.op("vector", stt(zrun[:, 0:512], PS[4][:, 0:512], fz, zrun[:, 0:512], ALU.mult, ALU.add),
                     wait=zw2 + (["fcol1"] if (i == 0) else []), inc="cV", sw=True)
                p.op("vector", stt(zrun[:, 512:1024], PS[5][:, 0:512], fz, zrun[:, 512:1024],
                                   ALU.mult, ALU.add), inc="cV")
                p.op("vector", stt(zrun[:, 1024:1040], PS[6][:, 32:48], fz, zrun[:, 1024:1040],
                                   ALU.mult, ALU.add), inc="cV", label=f"{sub}_zacc")
        p.op("vector", cp(sm[:, 7:8], sm[:, 7:8]), wait=[f"{lp}s3_zacc"], inc="cV", label=f"{lp}_zm")
        for j_ in range(6 * i, min(6 * i + 6, 48)):
            emit_wq(j_)

    lastz = f"pb{PA_NT-1}_zm"
    p.op("vector", cp(rowb[0:1, 5:6], mrun[0:1, 0:1]), wait=[lastz, f"pb{PA_NT-1}_mupd"], inc="cV", sw=True,
         label="e2_mb")
    p.op("tensor", mm(PS[6][:, 20:21], c_m1row[:], rowb[0:1, 5:6]), wait=["e2_mb"], inc="cP", label="e2_mbc")
    p.op("vector", ts1(sm[:, 6:7], PS[6][:, 20:21], -1.0, ALU.mult), wait=["e2_mbc"], inc="cV",
         label="e2_mcol")
    p.dma("scalar", dmas_nc((e2_in[:, 0:1040], zrun[:, 0:1040]),
                            (e2_in[:, 1040:1041], sm[:, 6:7]),
                            (e2_in[:, 1041:1042], csmax[:])), 3,
          wait=[lastz, "e2_mcol", "k2_15"], inc="dA", label="e2_stage")
    p.cc(lambda e: e.collective_compute("AllGather", ALU.bypass, replica_groups=[list(range(8))],
                                        ins=[e2_in], outs=[e2_out]),
         wait=["e2_stage"], inc="cc2", label="e2_done")
    if "zrun" in dbg:
        p.dma("sync", dmas((DBG("zrun", [128, 1040]), zrun[:]), (DBG("rowf", [1, 64]), rowf[:]),
                           (DBG("mrun", [1, 2]), mrun[:])), 3,
              wait=["e2_stage"], inc="dS", label="dbg_zrun")

    p.op("vector", mset(zeven[:], 0.0), wait=["e2_stage"], inc="cV")
    p.op("vector", mset(zodd[:], 0.0), inc="cV")
    p.op("vector", mset(gcol[:], 0.0), inc="cV", label="s3_init")
    for r in range(8):
        slot = r % 2
        gw = ["e2_done", "s3_init"] + ([f"s3_acc{r - 2}"] if r >= 2 else [])
        p.dma("sync", dmas((grbuf[slot][:], e2_out[r * 128:(r + 1) * 128, :])), 1,
              wait=gw, inc="dS", label=f"s3_gld{r}")
        p.op("vector", stt(zeven[:], grbuf[slot][:], c_pmask[:, 8 + r:9 + r], zeven[:], ALU.mult, ALU.add),
             wait=[f"s3_gld{r}"], inc="cV")
        p.op("vector", stt(zodd[:], grbuf[slot][:], c_pmask[:, 16 + r:17 + r], zodd[:], ALU.mult, ALU.add),
             inc="cV")
        p.op("vector", tt(gcol[:], gcol[:], grbuf[slot][:, 1041:1042], ALU.max), inc="cV", sw=True,
             label=f"s3_acc{r}")
        warm([f"s3_acc{r}"], bank=3)
    p.op("vector", tt(sm[:, 0:1], zeven[:, 1040:1041], zodd[:, 1040:1041], ALU.max), wait=["s3_acc7"],
         inc="cV", sw=True)
    p.op("vector", tt(sm[:, 1:2], zeven[:, 1040:1041], sm[:, 0:1], ALU.subtract), inc="cV", sw=True)
    p.op("vector", tt(sm[:, 2:3], zodd[:, 1040:1041], sm[:, 0:1], ALU.subtract), inc="cV", sw=True, label="s3_fsub")
    p.op("scalar", act2(sm[:, 1:3], sm[:, 1:3], AF.Exp), wait=["s3_fsub"], inc="cS", label="s3_fexp")
    p.op("vector", ts1(zrun[:, 0:1040], zeven[:, 0:1040], sm[:, 1:2], ALU.mult), wait=["s3_fexp"],
         inc="cV")
    p.op("vector", stt(zrun[:, 0:1040], zodd[:, 0:1040], sm[:, 2:3], zrun[:, 0:1040], ALU.mult, ALU.add),
         inc="cV", label="s3_zmg")
    p.op("vector", recip(rgcol[:], gcol[:]), wait=["s3_acc7"], inc="cV", sw=True, label="s3_rg")
    warm(["s3_zmg"], bank=3)
    p.op("vector", recip(rs_sb[:, 0:16], zrun[:, 1024:1040]), wait=["s3_zmg"], inc="cV", sw=True, label="s3_rs")
    for h in range(H):
        p.op("vector", ts1(kvn[:, h * 64:(h + 1) * 64], zrun[:, h * 64:(h + 1) * 64],
                           rs_sb[:, h:h + 1], ALU.mult),
             wait=["s3_rs"], inc="cV", sw=(h == 0), label=f"s3_kvn{h}")
        if h % 4 == 3:
            warm([f"s3_kvn{h}"], bank=3)
    if "kvn" in dbg:
        p.dma("sync", dmas((DBG("kvn", [128, 1024], BF16), kvn[:])), 1, wait=["s3_kvn15"], inc="dS",
              label="dbg_kvn")

    for g in range(4):
        for j in range(4):
            h = g * 4 + j
            trw = [f"k2n_{h}", "s3_rg"] + ([f"s3_k2T{g - 1}g"] if g >= 1 else ["e2_mbc"])
            p.op("tensor", tr(BT[:, j * 128:(j + 1) * 128], k2sb[:, h * 128:(h + 1) * 128]),
                 wait=trw, inc="cP", label=f"s3_k2tr{g}" if j == 3 else None)
        p.op("vector", ts1(v0t[:, g * 512:(g + 1) * 512], BT[:, 0:512], rgcol[:], ALU.mult),
             wait=[f"s3_k2tr{g}"], inc="cV", label=f"s3_v0g{g}")
        p.op("vector", cp(k2T[:, g * 512:(g + 1) * 512], BT[:, 0:512]), inc="cV", label=f"s3_k2T{g}g")
    for h in range(H):
        p.labels[f"s3_v0_{h}"] = p.labels[f"s3_v0g{h // 4}"]
        p.labels[f"s3_k2T{h}"] = p.labels[f"s3_k2T{h // 4}g"]

    p.dma("sync", dmas((sq_scr[:, 0:512], eye4_d)), 1, wait=["qnorm", "knorm"], inc="dS",
          label="s3_e4ld")
    p.op("vector", ts1(c_e7[:], sq_scr[:, 0:512], 7.0, ALU.mult), wait=["s3_e4ld"], inc="cV", label="e7")
    p.op("vector", ts1(c_e15[:], sq_scr[:, 0:512], 15.0, ALU.mult), inc="cV", label="e15")
    p.op("vector", ts1(c_e13[:], sq_scr[:, 0:512], 13.0, ALU.mult), inc="cV", label="e13")
    vcur, vnext = v0t, v1t
    for it in range(NITER):
        for g in range(4):
            hs = list(range(g * 4, g * 4 + 4))
            gl = f"i{it}g{g}"
            prevg = f"i{it}g{g-1}" if g > 0 else (f"i{it-1}g3" if it > 0 else None)
            for j, h in enumerate(hs):
                w = [f"s3_k2T{h}"]
                if it == 0:
                    w.append(f"s3_v0_{h}")
                else:
                    w.append(f"i{it-1}g{g}_vev")
                if prevg is not None:
                    w.append(f"{prevg}_kvev")
                p.op("tensor", mm(PS[0][:, j * 128:(j + 1) * 128], k2T[:, h * 128:(h + 1) * 128],
                                  vcur[:, h * 128:(h + 1) * 128]),
                     wait=w, inc="cP", label=f"{gl}_kv" if j == 3 else None)
            p.op("scalar", act2(kvt[:, g * 512:(g + 1) * 512], PS[0][:, 0:512], AF.Copy),
                 wait=[f"{gl}_kv"], inc="cS", label=f"{gl}_kvev")
            for j, h in enumerate(hs):
                trw = [f"{gl}_kvev"]
                if prevg is not None:
                    trw.append(f"{prevg}_kvTev")
                    trw.append(f"{prevg}_vTev")
                p.op("tensor", tr(BT[:, j * 128:(j + 1) * 128], kvt[:, h * 128:(h + 1) * 128]),
                     wait=trw, inc="cP", label=f"{gl}_kvtr" if j == 3 else None)
            p.op("scalar", act2(kvTt[:, g * 512:(g + 1) * 512], BT[:, 0:512], AF.Copy),
                 wait=[f"{gl}_kvtr"], inc="cS", label=f"{gl}_kvTev")
            p.op("vector", stt(t13[:, g * 512:(g + 1) * 512], kvt[:, g * 512:(g + 1) * 512], -1.0,
                               c_e7[:], ALU.mult, ALU.add),
                 wait=[f"{gl}_kvev", "e7"] + ([f"{prevg}_t2"] if prevg else []), inc="cV",
                 label=f"{gl}_t1")
            for j, h in enumerate(hs):
                w = [f"{gl}_t1", f"{gl}_kvTev"]
                if prevg is not None:
                    w.append(f"{prevg}_t3")
                p.op("tensor", mm(PS[1][:, j * 128:(j + 1) * 128], kvTt[:, h * 128:(h + 1) * 128],
                                  t13[:, h * 128:(h + 1) * 128]),
                     wait=w, inc="cP", label=f"{gl}_t2" if j == 3 else None)
            p.op("vector", stt(t13[:, g * 512:(g + 1) * 512], PS[1][:, 0:512], -1.0, c_e15[:],
                               ALU.mult, ALU.add),
                 wait=[f"{gl}_t2", "e15"], inc="cV", label=f"{gl}_t3")
            for j, h in enumerate(hs):
                w = [f"{gl}_t3"]
                if prevg is not None:
                    w.append(f"{prevg}_t5")
                p.op("tensor", mm(PS[2][:, j * 128:(j + 1) * 128], kvTt[:, h * 128:(h + 1) * 128],
                                  t13[:, h * 128:(h + 1) * 128]),
                     wait=w, inc="cP", label=f"{gl}_t4" if j == 3 else None)
            p.op("vector", stt(t13[:, g * 512:(g + 1) * 512], PS[2][:, 0:512], -1.0, c_e13[:],
                               ALU.mult, ALU.add),
                 wait=[f"{gl}_t4", "e13"], inc="cV", label=f"{gl}_t5")
            for j, h in enumerate(hs):
                trw = [f"{gl}_t5"]
                if prevg is not None:
                    trw.append(f"{prevg}_vTev")
                p.op("tensor", tr(BT[:, 512 + j * 128: 512 + (j + 1) * 128], vcur[:, h * 128:(h + 1) * 128]),
                     wait=trw, inc="cP", label=f"{gl}_vtr" if j == 3 else None)
            p.op("scalar", act2(kvt[:, g * 512:(g + 1) * 512], BT[:, 512:1024], AF.Copy),
                 wait=[f"{gl}_vtr"], inc="cS", label=f"{gl}_vTev")
            for j, h in enumerate(hs):
                w = [f"{gl}_vTev", f"{gl}_t5"]
                if prevg is not None:
                    w.append(f"{prevg}_vev")
                p.op("tensor", mm(PS[3][:, j * 128:(j + 1) * 128], kvt[:, g * 512 + j * 128: g * 512 + (j + 1) * 128],
                                  t13[:, h * 128:(h + 1) * 128]),
                     wait=w, inc="cP", label=f"{gl}_vmm" if j == 3 else None)
            p.op("vector", ts1(vnext[:, g * 512:(g + 1) * 512], PS[3][:, 0:512], 0.25, ALU.mult),
                 wait=[f"{gl}_vmm"], inc="cV", label=f"{gl}_vev")
        vcur, vnext = vnext, vcur
    W2 = vcur

    for h in range(H):
        gl = f"ch{h}"
        w0 = [f"i{NITER-1}g{h // 4}_vev"] + ([f"ch{h-1}_ev"] if h >= 1 else [])
        p.op("tensor", tr(BT[:, 0:128], W2[:, h * 128:(h + 1) * 128]), wait=w0, inc="cP", label=f"{gl}_tr")
        p.op("scalar", act2(kvTt[:, 2048:2176], BT[:, 0:128], AF.Copy), wait=[f"{gl}_tr"], inc="cS",
             label=f"{gl}_w2T")
        p.op("tensor", mm(PS[0][:, 0:64], kvTt[:, 2048:2176], kvn[:, h * 64:(h + 1) * 64]),
             wait=[f"{gl}_w2T", f"s3_kvn{h}"], inc="cP", label=f"{gl}_c1")
        p.op("tensor", mm(PS[1][:, 0:1], kvTt[:, 2048:2176], c_onescol[:]),
             wait=[f"{gl}_w2T"], inc="cP", label=f"{gl}_u")
        p.op("vector", cp(smb[:, 4:5], PS[1][:, 0:1]), wait=[f"{gl}_u"], inc="cV", label=f"{gl}_ub")
        p.op("tensor", tr(BT[0:1, 512:640], smb[:, 4:5]), wait=[f"{gl}_ub"], inc="cP", label=f"{gl}_utr")
        p.op("scalar", act2(urow_b[0:1, 0:128], BT[0:1, 512:640], AF.Copy), wait=[f"{gl}_utr"], inc="cS",
             label=f"{gl}_urow")
        p.op("tensor", mm(PS[0][:, 0:64], urow_b[0:1, 0:128], c_bv_b[0:1, h * 64:(h + 1) * 64],
                          start=False, stop=True),
             wait=[f"{gl}_urow", "bvb7", f"{gl}_c1"], inc="cP", label=f"{gl}_c2")
        p.op("vector", cp(csb[:, h * 64:(h + 1) * 64], PS[0][:, 0:64]), wait=[f"{gl}_c2"], inc="cV",
             label=f"{gl}_ev")
    if "csb" in dbg:
        p.dma("sync", dmas((DBG("csb", [128, 1024], BF16), csb[:])), 1, wait=["ch15_ev"], inc="dS",
              label="dbg_csb")

    p.op("vector", cp(smb[:, 3:4], qmax_run[:]), wait=[f"pa{PA_NT-1}_qmax"], inc="cV", label="b1_c")
    p.op("tensor", tr(BT[0:1, 384:512], smb[:, 3:4]), wait=["b1_c", "ch15_ev"], inc="cP", label="b1_tr")
    p.op("vector", red(rowf[0:1, 20:21], BT[0:1, 384:512], ALU.max), wait=["b1_tr"], inc="cV", label="b1_mx")
    p.op("scalar", act2(rowf[0:1, 19:20], rowf[0:1, 20:21], AF.Ln), wait=["b1_mx"], inc="cS")
    p.op("scalar", act2(rowf[0:1, 20:21], rowf[0:1, 19:20], AF.Exp, scale=0.5), inc="cS", sw=True,
         label="b1_sqrt")
    p.op("vector", tt(rowf[0:1, 21:22], rowf[0:1, 20:21], rowf[0:1, 8:9], ALU.mult),
         wait=["b1_sqrt", "knorm"], inc="cV", sw=True)
    p.op("vector", ts1(rowf[0:1, 21:22], rowf[0:1, 21:22], 1.02, ALU.mult), inc="cV", sw=True, label="b1_s")
    p.op("vector", cp(rowb[0:1, 9:10], rowf[0:1, 21:22]), inc="cV", sw=True, label="b1_sb")
    p.op("tensor", mm(PS[6][:, 21:22], c_m1row[:], rowb[0:1, 9:10]), wait=["b1_sb"], inc="cP", label="b1_bc")
    p.op("vector", cp(negb1[:], PS[6][:, 21:22]), wait=["b1_bc"], inc="cV", label="b1_done")

    p.dma("sync", dmas((w1sb[:].rearrange("p (c h) -> p c h", h=DH),
                        w1q_sp.rearrange("(c pp) h -> pp c h", pp=128))), 1,
          wait=["wq_st_46", "wq_st_47", "ch15_ev", "b1_done"], inc="dS", label="w1_ld")
    p.dma("sync", dmas((w2sb[:].rearrange("p (c dd) -> p c dd", dd=D),
                        w2q_sp.rearrange("(c pp) dd -> pp c dd", pp=128))), 1,
          wait=["wq_st_46", "wq_st_47", "ch15_ev", "b1_done"], inc="dS", label="w2_ld")

    def emit_attn_sub(sub, qsrc_fn, ntok, extra_l1_wait, prev_sub):
        for h in range(H):
            bank = PS[h % 2]
            w = list(extra_l1_wait) + ["b1_done", "klmb_done"]
            if h >= 2:
                w.append(f"{sub}_ee{h - 2}")
            elif prev_sub is not None:
                w.append(f"{prev_sub}_ee{14 + h}")
            p.op("tensor", mm(bank[:, 0:ntok], klmb[:, h * 128:(h + 1) * 128], qsrc_fn(h)),
                 wait=w, inc="cP", label=f"{sub}_l1_{h}")
            ew = [f"{sub}_l1_{h}"]
            if h >= 4:
                ew.append(f"{sub}_om{h - 4}")
            elif prev_sub is not None:
                ew.append(f"{prev_sub}_om{12 + h}")
            p.op("scalar", act2(e4[:, (h % 8) * 512 + (h // 8) * 128: (h % 8) * 512 + (h // 8) * 128 + ntok],
                                bank[:, 0:ntok], AF.Exp, bias=negb1[:]),
                 wait=ew, inc="cS", label=f"{sub}_ee{h}")
            zw = [f"{sub}_ee{h}", "ch15_ev"]
            if prev_sub is not None:
                zw.append(f"{prev_sub}_aev")
            p.op("tensor", mm(PS[4 + h // 8][0:ntok, (h % 8) * 64:(h % 8) * 64 + 64],
                              e4[:, (h % 8) * 512 + (h // 8) * 128: (h % 8) * 512 + (h // 8) * 128 + ntok],
                              csb[:, h * 64:(h + 1) * 64]),
                 wait=zw, inc="cP")
            p.op("tensor", mm(PS[6][0:ntok, 48 + h:49 + h],
                              e4[:, (h % 8) * 512 + (h // 8) * 128: (h % 8) * 512 + (h // 8) * 128 + ntok],
                              c_onescol[:]),
                 inc="cP", label=f"{sub}_om{h}")
        p.op("vector", recip(rs_sb[0:ntok, 0:16], PS[6][0:ntok, 48:64]), wait=[f"{sub}_om15"], inc="cV",
             label=f"{sub}_rs")
        for h in range(H):
            p.op("scalar", (lambda e, h=h, ntok=ntok, sub=sub: e.activation(
                aslot[0:ntok, h * 64:(h + 1) * 64],
                PS[4 + h // 8][0:ntok, (h % 8) * 64:(h % 8) * 64 + 64],
                AF.Identity, scale=rs_sb[0:ntok, h:h + 1])),
                 wait=[f"{sub}_rs"], inc="cS", label=f"{sub}_aev" if h == H - 1 else None)

    prev_sub = None
    pending_tr = []
    for i in range(8):
        t0 = i * 512
        lp = f"fa{i}"
        qw = [f"fa{i-1}s3_xq"] if i > 0 else ["b1_done"]
        p.dma("sync", dmas((qt4[:].rearrange("p (c t) -> p c t", t=512),
                            q_sp[:, :, t0:t0 + 512].rearrange("c p t -> p c t"))), 1,
              wait=qw + [f"pa{PA_NT-1}_qsp"], inc="dS", label=f"{lp}_qld")
        for s in range(4):
            sub = f"{lp}s{s}"
            ss = s % 2
            xldw = [f"{lp}s{s-2}_x2"] if s >= 2 else ([f"fa{i-1}s{2+s}_x2"] if i > 0 else ["b1_done"])
            p.dma("sync", dmas((x4[ss][:], xin[t0 + s * 128:t0 + s * 128 + 128, :])), 1,
                  wait=xldw, inc="dS", label=f"{sub}_xld")
            emit_attn_sub(sub, (lambda h, s=s: qt4[:, (h // 2) * 512 + s * 128:(h // 2) * 512 + s * 128 + 128]),
                          128, [f"{lp}_qld"], prev_sub)
            prev_sub = sub
            while pending_tr:
                pending_tr.pop(0)()
            emit_ln(lambda s_: aslot[:], 1, sub + "A", [f"{sub}_aev"])
            p.op("vector", ts2(t0slot[:], aslot[:], lnmv[:, 0:1], rstd4[:, 0:1], ALU.subtract, ALU.mult),
                 wait=[sub + "A_rstd"], inc="cV", label=f"{sub}_t0")
            x2w = [f"{sub}_t0", f"{sub}_xld"]
            if s >= 2:
                x2w.append(f"{lp}s{s-2}_x2sp")
            elif i > 0:
                x2w.append(f"fa{i-1}s{2+s}_x2sp")
            p.op("gpsimd", tt(x2slot[ss][:], t0slot[:], x4[ss][:], ALU.add), wait=x2w, inc="cG",
                 label=f"{sub}_x2")
            p.dma("scalar", dmas((x2_sp[t0 + s * 128: t0 + s * 128 + 128, :], x2slot[ss][:])), 1,
                  wait=[f"{sub}_x2"], inc="dA", label=f"{sub}_x2sp")
            emit_ln(lambda s_: x2slot[ss][:], 1, sub + "B", [f"{sub}_x2"])
            p.op("vector", ts2(t0slot[:], x2slot[ss][:], lnmv[:, 0:1], rstd4[:, 0:1], ALU.subtract, ALU.mult),
                 wait=[sub + "B_rstd", f"{sub}_t0"], inc="cV", label=f"{sub}_y2")
            p.op("vector", red(sm[:, 5:6], t0slot[:], ALU.max, absv=True), wait=[f"{sub}_y2"], inc="cV")
            p.op("vector", ts1(sm[:, 4:5], sm[:, 5:6], 1e-5, ALU.max), inc="cV", sw=True)
            p.op("vector", recip(sm[:, 3:4], sm[:, 4:5]), inc="cV", sw=True)
            p.op("vector", ts1(sm[:, 2:3], sm[:, 3:4], 127.0, ALU.mult), inc="cV", sw=True, label=f"{sub}_s1")
            p.op("vector", ts1(sm[:, 1:2], sm[:, 4:5], 1.0 / 127.0, ALU.mult), inc="cV", sw=True)
            p.op("vector", tt(sm[:, 0:1], sm[:, 1:2], rsw1[:], ALU.mult), wait=["rsw_done"], inc="cV", sw=True,
                 label=f"{sub}_sf1")
            p.op("vector", ts2(t0slot[:], t0slot[:], sm[:, 2:3], MAGIC, ALU.mult, ALU.add),
                 wait=[f"{sub}_s1"], inc="cV", sw=True)
            p.op("vector", ts2(xqtm2[s % 2][:], t0slot[:], MAGIC, sm[:, 0:1], ALU.subtract, ALU.mult),
                 wait=[f"{sub}_sf1"] + ([f"{lp}s{s-2}_xqt"] if s >= 2 else
                                        ([f"fa{i-1}s{2+s}_xqt"] if i > 0 else [])),
                 inc="cV", label=f"{sub}_xq")
            def _emit_xqtr(sub=sub, s=s, i=i, lp=lp, t0=t0):
                xtw = [f"{sub}_xq"]
                if s > 0:
                    xtw.append(f"{lp}s{s-1}_xqe")
                elif i > 0:
                    xtw.append(f"fa{i-1}s3_xqe")
                for c in range(8):
                    p.op("tensor", tr(BT[:, c * 128:(c + 1) * 128], xqtm2[s % 2][:, c * 128:(c + 1) * 128]),
                         wait=xtw, inc="cP", label=f"{sub}_xqt" if c == 7 else None)
                p.op("scalar", act2(xqsub[:], BT[:, 0:1024], AF.Copy),
                     wait=[f"{sub}_xqt"] + ([f"{lp}s{s-1}_xqsp"] if s >= 1 else
                                            ([f"fa{i-1}s3_xqsp"] if i > 0 else [])),
                     inc="cS", label=f"{sub}_xqe")
                p.dma("scalar", (lambda e, t0=t0, s=s: [
                    e.dma_start(out=xq_sp[c_, :, 1 + t0 + s * 128: 1 + t0 + s * 128 + 128],
                                in_=xqsub[:, c_ * 128:(c_ + 1) * 128]) for c_ in range(8)]), 8,
                      wait=[f"{sub}_xqe"], inc="dA", label=f"{sub}_xqsp")
            pending_tr.append(_emit_xqtr)
        p.op("vector", cp(sm[:, 13:14], sm[:, 13:14]), wait=[f"{lp}s3_xq"], inc="cV",
             label=f"{lp}_xq_pre")

    while pending_tr:
        pending_tr.pop(0)()
    p.op("vector", cp(sm[:, 13:14], sm[:, 13:14]), wait=["fa7s3_xqsp"], inc="cV", label="fa7_xq_sp")
    if "x2" in dbg:
        d = DBG("x2", [LH, D])
        p.dma("sync", dmas((d, x2_sp)), 1, wait=["fa7s3_x2sp"], inc="dS", label="dbg_x2")

    hp = "hal"
    emit_attn_sub(hp, (lambda h: qh_sb[:, (h // 2) * 128:(h // 2) * 128 + 128]),
                  128, ["pah_qev7", "fa7_xq_sp"], prev_sub)
    emit_ln(lambda s_: aslot[:], 1, hp + "A", [f"{hp}_aev"])
    p.op("vector", ts2(t0slot[:], aslot[:], lnmv[:, 0:1], rstd4[:, 0:1], ALU.subtract, ALU.mult),
         wait=[hp + "A_rstd"], inc="cV", label=f"{hp}_t0")
    p.op("gpsimd", tt(x2slot[0][:], t0slot[:], c_xh[:], ALU.add),
         wait=[f"{hp}_t0", "fa7s2_x2sp"], inc="cG", label=f"{hp}_x2")
    emit_ln(lambda s_: x2slot[0][:], 1, hp + "B", [f"{hp}_x2"])
    p.op("vector", ts2(t0slot[:], x2slot[0][:], lnmv[:, 0:1], rstd4[:, 0:1], ALU.subtract, ALU.mult),
         wait=[hp + "B_rstd", f"{hp}_t0"], inc="cV", label=f"{hp}_y2")
    p.op("vector", red(sm[:, 5:6], t0slot[:], ALU.max, absv=True), wait=[f"{hp}_y2"], inc="cV")
    p.op("vector", ts1(sm[:, 4:5], sm[:, 5:6], 1e-5, ALU.max), inc="cV", sw=True)
    p.op("vector", recip(sm[:, 3:4], sm[:, 4:5]), inc="cV", sw=True)
    p.op("vector", ts1(sm[:, 2:3], sm[:, 3:4], 127.0, ALU.mult), inc="cV", sw=True, label=f"{hp}_s1")
    p.op("vector", ts1(sm[:, 1:2], sm[:, 4:5], 1.0 / 127.0, ALU.mult), inc="cV", sw=True)
    p.op("vector", tt(sm[:, 0:1], sm[:, 1:2], rsw1[:], ALU.mult), inc="cV", sw=True, label=f"{hp}_sf1")
    p.op("vector", ts2(t0slot[:], t0slot[:], sm[:, 2:3], MAGIC, ALU.mult, ALU.add),
         wait=[f"{hp}_s1"], inc="cV", sw=True)
    p.op("vector", ts2(xqtm2[0][:], t0slot[:], MAGIC, sm[:, 0:1], ALU.subtract, ALU.mult),
         wait=[f"{hp}_sf1", "fa7s2_xqt"], inc="cV")
    p.op("vector", ts1(xqtm2[0][:], xqtm2[0][:], c_hmask[:, 0:1], ALU.mult),
         wait=["ld_c1"], inc="cV", label=f"{hp}_xq")
    for c in range(8):
        p.op("tensor", tr(BT[:, c * 128:(c + 1) * 128], xqtm2[0][:, c * 128:(c + 1) * 128]),
             wait=[f"{hp}_xq", "fa7s3_xqe"], inc="cP", label=f"{hp}_xqt" if c == 7 else None)
    p.op("scalar", act2(xqsub[:], BT[:, 0:1024], AF.Copy),
         wait=[f"{hp}_xqt", "fa7s3_xqsp"], inc="cS", label=f"{hp}_xqe")
    p.dma("scalar", (lambda e: [e.dma_start(out=xq_sp[c_, :, 0:1], in_=xqsub[:, c_ * 128:c_ * 128 + 1],
                                            allow_slow_non_contiguous=True)
                                for c_ in range(8)] +
                               [e.dma_start(out=xq_sp[c_, :, LH + 1:LH + 2],
                                            in_=xqsub[:, c_ * 128 + 1:c_ * 128 + 2],
                                            allow_slow_non_contiguous=True)
                                for c_ in range(8)]), 16,
          wait=[f"{hp}_xqe"], inc="dA", label="hal_xq_sp")

    store_labels = []
    p0s = np.cumsum([0] + S4B_SIZES).tolist()
    for j, nj in enumerate(S4B_SIZES):
        p0 = int(p0s[j])
        lp = f"fb{j}"
        nw = nj + 2
        bsz = [128, 128, 128, nj - 384] if nj > 384 else [nj]
        nblk = len(bsz)
        xw = [f"fb{j-1}_mm1_31"] if j > 0 else ["hal_xq_sp"]
        p.dma("sync", dmas((xqw[:].rearrange("p (c t) -> p c t", t=512)[:, :, 0:nw],
                            xq_sp[:, :, p0:p0 + nw].rearrange("c p t -> p c t"))), 1,
              wait=xw, inc="dS", label=f"{lp}_xqld")
        for m in range(32):
            bank = PS[m % 2]
            for k in range(8):
                w = [f"{lp}_xqld", "w1_ld"]
                if m >= 2:
                    w.append(f"{lp}_h1ev{m - 2}")
                elif j > 0:
                    w.append(f"fb{j-1}_mm2done{m}")
                p.op("tensor", mm(bank[:, 0:nw], w1sb[:, k * DH + m * 128: k * DH + (m + 1) * 128],
                                  xqw[:, k * 512: k * 512 + nw], start=(k == 0), stop=(k == 7)),
                     wait=w, inc="cP", label=f"{lp}_mm1_{m}" if k == 7 else None)
            p.op("scalar", act2(h1buf[:, m * 512: m * 512 + nw], bank[:, 0:nw], AF.Copy),
                 wait=[f"{lp}_mm1_{m}"] + ([f"fb{j-1}_q2_{m}"] if j > 0 else []),
                 inc="cS", label=f"{lp}_h1ev{m}")
            cw = [f"{lp}_h1ev{m}"]
            if m >= 1:
                cw.append(f"{lp}_cb{m - 1}")
            p.op("vector", ts1(csc1[:, 0:nj], h1buf[:, m * 512: m * 512 + nj],
                               c_convw[:, 3 * m: 3 * m + 1], ALU.mult),
                 wait=cw + ["ld_c1"], inc="cV", label=f"{lp}_ca{m}")
            p.op("vector", stt(csc2[:, 0:nj], h1buf[:, m * 512 + 1: m * 512 + 1 + nj],
                               c_convw[:, 3 * m + 1: 3 * m + 2], csc1[:, 0:nj], ALU.mult, ALU.add),
                 wait=[f"{lp}_ca{m}"], inc="cV", sw=(nj < 128))
            p.op("vector", stt(h1buf[:, m * 512 + 1: m * 512 + 1 + nj],
                               h1buf[:, m * 512 + 2: m * 512 + 2 + nj],
                               c_convw[:, 3 * m + 2: 3 * m + 3], csc2[:, 0:nj], ALU.mult, ALU.add),
                 inc="cV", sw=(nj < 128), label=f"{lp}_cb{m}")
            p.op("scalar", act2(h1buf[:, m * 512 + 1: m * 512 + 1 + nj],
                                h1buf[:, m * 512 + 1: m * 512 + 1 + nj],
                                AF.Silu, bias=c_convb[:, m:m + 1]),
                 wait=[f"{lp}_cb{m}"], inc="cS", label=f"{lp}_cv{m}")
            if m == 1:
                p.op("vector", tt(hmxp[:, 0:nj], h1buf[:, 1: 1 + nj], h1buf[:, 512 + 1: 512 + 1 + nj],
                                  ALU.max), wait=[f"{lp}_cv0", f"{lp}_cv1"], inc="cV", sw=(nj < 128))
                p.op("vector", tt(hmnp[:, 0:nj], h1buf[:, 1: 1 + nj], h1buf[:, 512 + 1: 512 + 1 + nj],
                                  ALU.min), inc="cV", sw=(nj < 128))
            elif m >= 2:
                lblm = f"{lp}_hm" if m == 31 else (f"{lp}_tm{m}" if m >= 28 else None)
                p.op("vector", tt(hmxp[:, 0:nj], hmxp[:, 0:nj], h1buf[:, m * 512 + 1: m * 512 + 1 + nj],
                                  ALU.max), wait=[f"{lp}_cv{m}"], inc="cV", sw=(nj < 128))
                p.op("vector", tt(hmnp[:, 0:nj], hmnp[:, 0:nj], h1buf[:, m * 512 + 1: m * 512 + 1 + nj],
                                  ALU.min), inc="cV", sw=(nj < 128), label=lblm)
                if m >= 28:
                    warm([lblm], bank=0)
        for b in range(nblk):
            bs = bsz[b]
            p.op("tensor", tr(BT[0:bs, (b % 2) * 256:(b % 2) * 256 + 128], hmxp[:, b * 128: b * 128 + bs]),
                 wait=[f"{lp}_hm"] + ([f"{lp}_sf2{b - 1}", f"{lp}_s2c{b - 1}"] if b >= 1 else []),
                 inc="cP", label=f"{lp}_s2t{b}")
            p.op("tensor", tr(BT[0:bs, (b % 2) * 256 + 128:(b % 2) * 256 + 256],
                              hmnp[:, b * 128: b * 128 + bs]),
                 wait=[f"{lp}_hm"] + ([f"{lp}_sf2{b - 1}", f"{lp}_s2c{b - 1}"] if b >= 1 else []),
                 inc="cP", label=f"{lp}_s2tn{b}")
            p.op("vector", red(s2mx[0:bs, b:b + 1], BT[0:bs, (b % 2) * 256:(b % 2) * 256 + 128], ALU.max),
                 wait=[f"{lp}_s2t{b}"], inc="cV")
            p.op("vector", red(sm[0:bs, 6:7], BT[0:bs, (b % 2) * 256 + 128:(b % 2) * 256 + 256], ALU.min),
                 wait=[f"{lp}_s2tn{b}"], inc="cV")
            p.op("vector", ts1(sm[0:bs, 6:7], sm[0:bs, 6:7], -1.0, ALU.mult), inc="cV", sw=True)
            p.op("vector", tt(s2mx[0:bs, b:b + 1], s2mx[0:bs, b:b + 1], sm[0:bs, 6:7], ALU.max), inc="cV", sw=True)
            p.op("vector", ts1(s2mx[0:bs, b:b + 1], s2mx[0:bs, b:b + 1], 1e-5, ALU.max), inc="cV", sw=True,
                 label=f"{lp}_s2m{b}")
            warm([f"{lp}_s2m{b}"], bank=0)
            p.op("vector", recip(sm[0:bs, 5:6], s2mx[0:bs, b:b + 1]), inc="cV", sw=True)
            p.op("vector", ts1(sm[0:bs, 4:5], sm[0:bs, 5:6], 127.0, ALU.mult), inc="cV", sw=True, label=f"{lp}_s2s{b}")
            p.op("vector", ts1(sm[0:bs, 3:4], s2mx[0:bs, b:b + 1], 1.0 / 127.0, ALU.mult), inc="cV", sw=True)
            p.op("vector", tt(sf2col[0:bs, b:b + 1], sm[0:bs, 3:4], rsw2[0:bs, :], ALU.mult),
                 wait=["rsw_done"], inc="cV", sw=True, label=f"{lp}_sf2{b}")
            p.op("vector", cp(smb[0:bs, 5:6], sm[0:bs, 4:5]), inc="cV", sw=True, label=f"{lp}_s2cb{b}")
            p.op("tensor", tr(BT[0:1, 512 + (b % 2) * 128: 512 + (b % 2) * 128 + 128], smb[:, 5:6]),
                 wait=[f"{lp}_s2cb{b}"], inc="cP", label=f"{lp}_s2rt{b}")
            p.op("scalar", act2(s2rowb[0:1, b * 128: b * 128 + bs],
                                BT[0:1, 512 + (b % 2) * 128: 512 + (b % 2) * 128 + bs], AF.Copy),
                 wait=[f"{lp}_s2rt{b}"], inc="cS", label=f"{lp}_s2c{b}")
        p.op("tensor", mm(PS[6][:, 0:nj], c_onesrow[:], s2rowb[0:1, 0:nj]),
             wait=[f"{lp}_s2c{b}" for b in range(nblk)] + ["orow_b"] + ([f"fb{j-1}_s2bce"] if j > 0 else []), inc="cP", label=f"{lp}_s2bc")
        p.op("vector", cp(s2bc[:, 0:nj], PS[6][:, 0:nj]), wait=[f"{lp}_s2bc"], inc="cV",
             label=f"{lp}_s2bce")
        for b in range(nblk):
            warm([f"{lp}_sf2{b}"], bank=0)
        for m in range(32):
            p.op("vector", tt(uq[:, 0:nj], h1buf[:, m * 512 + 1: m * 512 + 1 + nj], s2bc[:, 0:nj],
                              ALU.mult), wait=[f"{lp}_s2bce", f"{lp}_hm"], inc="cV", sw=(m == 0))
            p.op("vector", ts2(h1buf[:, m * 512 + 1: m * 512 + 1 + nj], uq[:, 0:nj], MAGIC, MAGIC,
                               ALU.add, ALU.subtract), inc="cV", sw=(nj < 128), label=f"{lp}_q2_{m}")
            warm([f"{lp}_q2_{m}"], bank=0)
        if j == 8 and "fb8" in dbg:
            p.dma("sync", dmas((DBG("fb8h1", [128, 1024], BF16), h1buf[:, 0:1024]),
                               (DBG("fb8xq", [128, 1024], BF16), xqw[:, 0:1024]),
                               (DBG("fb8sc", [128, 4]), sf2col[:, 0:4]),
                               (DBG("fb8s2", [128, 16], BF16), s2bc[:, 0:16]),
                               (DBG("fb8mx", [128, 16], BF16), hmxp[:, 0:16])), 5,
                  wait=[f"{lp}_q2_31"], inc="dS", label="dbg_fb8")
        for b in range(nblk):
            bs = bsz[b]
            bb = 2 + (b % 2) * 2
            for k in range(32):
                w = [f"{lp}_q2_{k}", "w2_ld"]
                if b >= 2:
                    w.append(f"{lp}_f{b - 2}")
                elif j > 0:
                    w.append(f"fb{j-1}_f{nblk_prev - 2 + b}" if nblk_prev >= 3 else f"fb{j-1}_f0")
                p.op("tensor", mm(PS[bb][0:bs, 0:512],
                                  h1buf[:, k * 512 + 1 + b * 128: k * 512 + 1 + b * 128 + bs],
                                  w2sb[:, k * 1024: k * 1024 + 512], start=(k == 0), stop=(k == 31)),
                     wait=w, inc="cP")
                p.op("tensor", mm(PS[bb + 1][0:bs, 0:512],
                                  h1buf[:, k * 512 + 1 + b * 128: k * 512 + 1 + b * 128 + bs],
                                  w2sb[:, k * 1024 + 512: k * 1024 + 1024], start=(k == 0), stop=(k == 31)),
                     inc="cP", label=f"{lp}_mm2_{b}" if k == 31 else None)
            fw = [f"{lp}_mm2_{b}", f"{lp}_sf2{b}"]
            if b >= 1:
                fw.append(f"{lp}_st{b - 1}")
            elif j > 0:
                fw.append(f"fb{j-1}_st{nblk_prev - 1}")
            p.op("scalar", act2(fbuf[0:bs, 0:512], PS[bb][0:bs, 0:512], AF.Identity,
                                scale=sf2col[0:bs, b:b + 1]),
                 wait=fw, inc="cS")
            p.op("scalar", act2(fbuf[0:bs, 512:1024], PS[bb + 1][0:bs, 0:512], AF.Identity,
                                scale=sf2col[0:bs, b:b + 1]), inc="cS", label=f"{lp}_f{b}")
            x2w_ = [f"{lp}_f{b}"]
            if j == 0 and b == 0:
                x2w_.append("fa7s3_x2sp")
            if b >= 2:
                x2w_.append(f"{lp}_o{b - 2}")
            elif j > 0:
                x2w_.append(f"fb{j-1}_o{max(0, nblk_prev - 2 + b)}")
            p.dma("sync", dmas((x2blk[b % 2][0:bs, :], x2_sp[p0 + b * 128: p0 + b * 128 + bs, :])), 1,
                  wait=x2w_, inc="dS", label=f"{lp}_x2ld{b}")
            emit_ln(lambda s_: fbuf[:], 1, f"{lp}LN{b}", [f"{lp}_f{b}"], psz=bs)
            p.op("vector", ts2(fbuf[0:bs, :], fbuf[0:bs, :], lnmv[0:bs, 0:1], rstd4[0:bs, 0:1],
                               ALU.subtract, ALU.mult),
                 wait=[f"{lp}LN{b}_rstd"], inc="cV", label=f"{lp}_fn{b}")
            p.op("gpsimd", tt(fbuf[0:bs, :], fbuf[0:bs, :], x2blk[b % 2][0:bs, :], ALU.add),
                 wait=[f"{lp}_fn{b}", f"{lp}_x2ld{b}"], inc="cG", label=f"{lp}_o{b}")
            p.dma("scalar", dmas((yout[p0 + b * 128: p0 + b * 128 + bs, :], fbuf[0:bs, :])), 1,
                  wait=[f"{lp}_o{b}"], inc="dA", label=f"{lp}_st{b}")
            store_labels.append(f"{lp}_st{b}")
        for m in range(32):
            p.labels[f"{lp}_mm2done{m}"] = p.labels[f"{lp}_mm2_{nblk - 1}"]
        nblk_prev = nblk

    if stop is not None:
        p.ops = p.ops[: p.label_idx[stop] + 1]
        p.barrier_wait("sync", [stop])
    else:
        p.barrier_wait("sync", store_labels)

    from contextlib import ExitStack
    sems = {}
    ctx = ExitStack()
    for s in ["dS", "dA", "dG", "cV", "cS", "cP", "cG", "cc1", "cc2"]:
        sems[s] = ctx.enter_context(nc.semaphore(s))
    p.lower(sems)
    nc._sem_ctx = ctx
    return nc, dbg_outs


def _stage_inputs(L0_attn_x, Wq, bq, Wk, bk, Wv, bv, w1, conv_w, conv_b, w2):
    x = np.ascontiguousarray(np.asarray(L0_attn_x, dtype=np.float32))
    B, L, _ = x.shape
    wqT = np.ascontiguousarray(np.asarray(Wq, np.float32).T)
    wkT = np.ascontiguousarray(np.asarray(Wk, np.float32).T)
    wvT = np.ascontiguousarray(np.asarray(Wv, np.float32).T)
    w1T = np.ascontiguousarray(np.asarray(w1, np.float32).T)
    w2T = np.ascontiguousarray(np.asarray(w2, np.float32).T)
    bq_r = np.ascontiguousarray(np.asarray(bq, np.float32).reshape(8, 128).T)
    bk_r = np.ascontiguousarray(np.asarray(bk, np.float32).reshape(8, 128).T)
    bv_row = np.ascontiguousarray(np.asarray(bv, np.float32).reshape(1, D))
    convw_r = np.ascontiguousarray(np.asarray(conv_w, np.float32).reshape(DH, 3)
                                   .reshape(32, 128, 3).transpose(1, 0, 2).reshape(128, 96))
    convb_r = np.ascontiguousarray(np.asarray(conv_b, np.float32).reshape(32, 128).T)
    cones = np.zeros((128, 2), np.float32)
    cones[0:64, 0] = 1.0
    cones[64:128, 1] = 1.0
    eye_f = np.eye(128, dtype=np.float32)
    eye4_f = np.ascontiguousarray(np.tile(eye_f, (1, 4)))
    onesrow = np.ones((1, 128), np.float32)
    m1row = -np.ones((1, 128), np.float32)
    onescol = np.ones((128, 1), np.float32)

    in_maps = []
    for c in range(8):
        b = c // 2
        half = c % 2
        t0 = half * LH
        xs = np.ascontiguousarray(x[b, t0:t0 + LH, :])
        xhv = np.zeros((128, D), np.float32)
        if t0 > 0:
            xhv[0] = x[b, t0 - 1, :]
        if t0 + LH < L:
            xhv[1] = x[b, t0 + LH, :]
        hmask = np.zeros((128, 64), np.float32)
        hmask[0, 0] = 1.0 if t0 > 0 else 0.0
        hmask[1, 0] = 1.0 if t0 + LH < L else 0.0
        pmask = np.zeros((128, 24), np.float32)
        pmask[:, 2 * b] = 1.0
        pmask[:, 2 * b + 1] = 1.0
        pmask[:, 8 + 2 * b] = 1.0
        pmask[:, 16 + 2 * b + 1] = 1.0
        in_maps.append(dict(
            xin=xs, xh=xhv, wqT=wqT, wkT=wkT, wvT=wvT, w1T=w1T, w2T=w2T,
            bq_r=bq_r, bk_r=bk_r, bv_row=bv_row, convw_r=convw_r, convb_r=convb_r,
            cones=cones, eye_f=eye_f, eye4_f=eye4_f, onesrow=onesrow, m1row=m1row,
            onescol=onescol, pmask=pmask, hmask=hmask,
        ))
    return in_maps


_CACHED = {}


def kernel(L0_attn_x, L0_attn_Wq, L0_attn_bq, L0_attn_Wk, L0_attn_bk, L0_attn_Wv,
           L0_attn_bv, L0_ffn_w1, L0_ffn_conv_w, L0_ffn_conv_b, L0_ffn_w2):
    in_maps = _stage_inputs(L0_attn_x, L0_attn_Wq, L0_attn_bq, L0_attn_Wk, L0_attn_bk,
                            L0_attn_Wv, L0_attn_bv, L0_ffn_w1, L0_ffn_conv_w,
                            L0_ffn_conv_b, L0_ffn_w2)
    if "nc" not in _CACHED:
        _CACHED["nc"] = build_program()[0]
    nc = _CACHED["nc"]
    res = run_bass_kernel_spmd(nc, in_maps, core_ids=list(range(8)))
    out = np.empty((4, 2 * LH, D), np.float32)
    for c in range(8):
        out[c // 2, (c % 2) * LH:(c % 2 + 1) * LH, :] = res.results[c]["yout"]
    return out


# version 13
# speedup vs baseline: 1.0191x; 1.0191x over previous
import numpy as np
import concourse.bass as bass
import concourse.mybir as mybir
from concourse.bass_utils import run_bass_kernel_spmd

F32 = mybir.dt.float32
BF16 = mybir.dt.bfloat16
AF = mybir.ActivationFunctionType
ALU = mybir.AluOpType
AX = mybir.AxisListType

D, H, HD, DH = 1024, 16, 64, 4096
LH = 4096
SEG = 64
S4 = float(64.0 ** -0.25)
EPS = 1e-5
MAGIC = float(1.5 * 2 ** 23)
NITER = 6
PA_NT = 8
S4B_SIZES = [510] * 8 + [16]


class Prog:
    ENGS = ("sync", "scalar", "vector", "tensor", "gpsimd")

    def __init__(self, nc):
        self.nc = nc
        self.ops = []
        self.counts = {}
        self.labels = {}
        self.label_idx = {}

    def _resolve(self, wait):
        out = []
        for w in wait or ():
            assert isinstance(w, str) and w in self.labels, f"bad wait {w}"
            out.append(self.labels[w])
        return out

    ENG_SEM = {"vector": "cV", "scalar": "cS", "tensor": "cP", "gpsimd": "cG"}

    def op(self, eng, fn, wait=(), inc=None, label=None, sw=False):
        waits = self._resolve(wait)
        if sw:
            es = self.ENG_SEM[eng]
            c = self.counts.get(es, 0)
            if c > 0:
                waits.append((es, c))
        if inc is not None:
            self.counts[inc] = self.counts.get(inc, 0) + 1
            if label:
                self.labels[label] = (inc, self.counts[inc])
                self.label_idx[label] = len(self.ops)
        else:
            assert label is None
        self.ops.append((eng, fn, waits, inc, "one"))

    def dma(self, eng, fn, n, wait=(), inc=None, label=None):
        waits = self._resolve(wait)
        assert inc is not None
        self.counts[inc] = self.counts.get(inc, 0) + 16 * n
        if label:
            self.labels[label] = (inc, self.counts[inc])
            self.label_idx[label] = len(self.ops)
        self.ops.append((eng, fn, waits, inc, n))

    def cc(self, fn, wait=(), inc=None, label=None):
        waits = self._resolve(wait)
        self.labels[label] = (inc, 1)
        self.label_idx[label] = len(self.ops)
        self.ops.append(("gpsimd", fn, waits, inc, "cc"))

    def barrier_wait(self, eng, wait):
        self.ops.append((eng, lambda e: None, self._resolve(wait), None, "one"))

    def lower(self, sems):
        with self.nc.Block() as block:
            for eng_name in self.ENGS:
                ops = [o for o in self.ops if o[0] == eng_name]
                if not ops:
                    continue

                def body(eng, ops=ops):
                    waited = {}
                    for _, fn, waits, inc, kind in ops:
                        for sem_name, val in waits:
                            if waited.get(sem_name, -1) < val:
                                eng.wait_ge(sems[sem_name], val)
                                waited[sem_name] = val
                        r = fn(eng)
                        if inc is None:
                            continue
                        if kind == "cc":
                            r.then_inc(sems[inc], 1)
                        elif kind == "one":
                            if isinstance(r, (list, tuple)):
                                r = r[-1]
                            r.then_inc(sems[inc], 1)
                        else:
                            if not isinstance(r, (list, tuple)):
                                r = [r]
                            assert len(r) == kind, f"dma count {len(r)} != {kind}"
                            for ins in r:
                                ins.then_inc(sems[inc], 16)

                getattr(block, eng_name)(body)


def ceil_div(a, b):
    return (a + b - 1) // b


def build_program(dbg=(), stop=None):
    nc = bass.Bass("TRN2", target_bir_lowering=False, debug=False)
    dbg = set(dbg)

    def DI(name, shape, dt=F32):
        return nc.dram_tensor(name, list(shape), dt, kind="ExternalInput").ap()

    xin = DI("xin", [LH, D])
    xh_d = DI("xh", [128, D])
    wqT_d = DI("wqT", [D, D]); wkT_d = DI("wkT", [D, D]); wvT_d = DI("wvT", [D, D])
    w1T_d = DI("w1T", [D, DH]); w2T_d = DI("w2T", [DH, D])
    bq_d = DI("bq_r", [128, 8]); bk_d = DI("bk_r", [128, 8])
    bv_d = DI("bv_row", [1, D])
    convw_d = DI("convw_r", [128, 96]); convb_d = DI("convb_r", [128, 32])
    cones_d = DI("cones", [128, 2])
    eye_d = DI("eye_f", [128, 128])
    eye4_d = DI("eye4_f", [128, 512])
    onesrow_d = DI("onesrow", [1, 128])
    m1row_d = DI("m1row", [1, 128])
    onescol_d = DI("onescol", [128, 1])
    pmask_d = DI("pmask", [128, 24])
    hmask_d = DI("hmask", [128, 64])

    yout = nc.dram_tensor("yout", [LH, D], F32, kind="ExternalOutput").ap()
    dbg_outs = {}

    def DBG(name, shape, dt=F32):
        dbg_outs[name] = nc.dram_tensor("dbg_" + name, list(shape), dt, kind="ExternalOutput").ap()
        return dbg_outs[name]

    q_sp = nc.dram_tensor("q_sp", [8, 128, LH], BF16, kind="Internal").ap()
    k_sp = nc.dram_tensor("k_sp", [8, 128, LH], BF16, kind="Internal").ap()
    v_sp = nc.dram_tensor("v_sp", [32, 128, D], BF16, kind="Internal").ap()
    xq_sp = nc.dram_tensor("xq_sp", [8, 128, LH + 2], BF16, kind="Internal").ap()
    x2_sp = nc.dram_tensor("x2_sp", [LH, D], F32, kind="Internal").ap()
    w1q_sp = nc.dram_tensor("w1q_sp", [D, DH], BF16, kind="Internal").ap()
    w2q_sp = nc.dram_tensor("w2q_sp", [DH, D], BF16, kind="Internal").ap()
    e1_in = nc.dram_tensor("e1_in", [128, 1024], F32, kind="Internal").ap()
    e1_out = nc.dram_tensor("e1_out", [1024, 1024], F32, kind="Internal").ap()
    E2W = 1042
    e2_in = nc.dram_tensor("e2_in", [128, E2W], F32, kind="Internal").ap()
    e2_out = nc.dram_tensor("e2_out", [1024, E2W], F32, kind="Internal").ap()

    def alloc(name, shape, dt, region):
        t = nc.alloc_sbuf_tensor_at(name, list(shape), dt, offset=region[0])
        bpp = int(np.prod(shape[1:])) * (2 if dt == BF16 else 4)
        region[0] += (bpp + 31) // 32 * 32
        return t

    used = nc.SBUF_PARTITION_SIZE_BYTES - nc.sbuf_bytes_remaining
    PR = [(used + 511) // 512 * 512]
    c_eye_b = alloc("c_eye_b", [128, 128], BF16, PR)
    c_cones = alloc("c_cones", [128, 2], BF16, PR)
    c_onesrow = alloc("c_onesrow", [1, 128], BF16, PR)
    c_m1row = alloc("c_m1row", [1, 128], BF16, PR)
    c_onescol = alloc("c_onescol", [128, 1], BF16, PR)
    c_pmask = alloc("c_pmask", [128, 24], F32, PR)
    c_hmask = alloc("c_hmask", [128, 64], F32, PR)
    c_bq = alloc("c_bq", [128, 8], F32, PR)
    c_bk = alloc("c_bk", [128, 8], F32, PR)
    c_bv_b = alloc("c_bv_b", [1, D], BF16, PR)
    c_convw = alloc("c_convw", [128, 96], F32, PR)
    c_convb = alloc("c_convb", [128, 32], F32, PR)
    c_xh = alloc("c_xh", [128, D], F32, PR)
    fscr = alloc("fscr", [128, 128], F32, PR)
    pscr_b = alloc("pscr_b", [128, 512], BF16, PR)
    rsw1 = alloc("rsw1", [128, 1], F32, PR)
    rsw2 = alloc("rsw2", [128, 1], F32, PR)
    sw1 = alloc("sw1", [128, 1], F32, PR)
    sw2 = alloc("sw2", [128, 1], F32, PR)
    qmax_run = alloc("qmax_run", [128, 1], F32, PR)
    klmb = alloc("klmb", [128, 16 * 128], BF16, PR)
    csmax = alloc("csmax", [128, 1], F32, PR)
    negb1 = alloc("negb1", [128, 1], F32, PR)
    negb2 = alloc("negb2", [128, 1], F32, PR)
    gcol = alloc("gcol", [128, 1], F32, PR)
    rgcol = alloc("rgcol", [128, 1], F32, PR)
    csb = alloc("csb", [128, 1024], BF16, PR)
    qh_sb = alloc("qh_sb", [128, 1024], BF16, PR)
    sm = alloc("sm", [128, 32], F32, PR)
    smb = alloc("smb", [128, 8], BF16, PR)
    lnmv = alloc("lnmv", [128, 8], F32, PR)
    rstd4 = alloc("rstd4", [128, 4], F32, PR)
    veps4 = alloc("veps4", [128, 4], F32, PR)
    negmr4 = alloc("negmr4", [128, 4], F32, PR)
    magicc = alloc("magicc", [128, 1], F32, PR)
    lnscr = alloc("lnscr", [128, 4], F32, PR)
    s2mn = alloc("s2mn", [128, 4], F32, PR)
    rowf = alloc("rowf", [1, 64], F32, PR)
    rowb = alloc("rowb", [1, 64], BF16, PR)
    wabs = alloc("wabs", [128, 40], F32, PR)
    wabs_b = alloc("wabs_b", [128, 2], BF16, PR)
    fcol = alloc("fcol", [128, 2], F32, PR)
    mrun = alloc("mrun", [1, 2], F32, PR)
    rs_sb = alloc("rs_sb", [128, 16], F32, PR)
    s2mx = alloc("s2mx", [128, 4], F32, PR)
    sf2col = alloc("sf2col", [128, 4], F32, PR)
    urow_b = alloc("urow_b", [1, 128], BF16, PR)
    PERSIST_END = PR[0]
    assert PERSIST_END <= PR[0] + 20 * 1024, PERSIST_END
    RBASE = PERSIST_END

    RA = [RBASE]
    wq_sb = alloc("wq_sb", [128, 8 * D], BF16, RA)
    wk_sb = alloc("wk_sb", [128, 8 * D], BF16, RA)
    wv_sb = alloc("wv_sb", [128, 8 * D], BF16, RA)
    xbuf = [alloc("xbuf0", [128, 4 * D], F32, RA), None]
    y1tm = alloc("y1tm", [128, 4 * D], BF16, RA)
    y1fm = alloc("y1fm", [128, 8 * 512], BF16, RA)
    qstage = alloc("qstage", [128, 8 * 512], BF16, RA)
    kstage = alloc("kstage", [128, 8 * 512], BF16, RA)
    vstage = alloc("vstage", [128, 4 * D], BF16, RA)
    wstage = [alloc("wstage0", [128, 4096], F32, RA), alloc("wstage1", [128, 4096], F32, RA)]
    _wqo_off = RA[0]
    wqout = [alloc("wqout0", [128, 4096], BF16, RA), alloc("wqout1", [128, 4096], BF16, RA)]
    xbuf[1] = nc.alloc_sbuf_tensor_at("xbuf1", [128, 4 * D], F32, offset=_wqo_off)
    qmx = alloc("qmx", [128, 512], F32, RA)
    qsq = alloc("qsq", [128, 4096], F32, RA)
    xlm_sb = alloc("xlm_sb", [128, 512], F32, RA)
    xlm_b = alloc("xlm_b", [128, 512], BF16, RA)
    qlm_own = alloc("qlm_own", [128, 512], F32, RA)
    klm_own = alloc("klm_own", [128, 512], F32, RA)
    assert RA[0] <= 229376, RA[0]

    RB = [RBASE]
    kt = [alloc("kt0", [128, 8 * 512], BF16, RB), alloc("kt1", [128, 8 * 512], BF16, RB)]
    vt = [alloc("vt0", [128, 4 * D], BF16, RB), alloc("vt1", [128, 4 * D], BF16, RB)]
    esb = alloc("esb", [128, 2 * 2048], BF16, RB)
    kmx = alloc("kmx", [128, 512], F32, RB)
    ksq = alloc("ksq", [128, 4096], F32, RB)
    grbuf = [alloc("grbuf0", [128, E2W], F32, RB), alloc("grbuf1", [128, E2W], F32, RB)]
    zeven = alloc("zeven", [128, E2W], F32, RB)
    zodd = alloc("zodd", [128, E2W], F32, RB)
    v0t = alloc("v0t", [128, 16 * 128], BF16, RB)
    v1t = alloc("v1t", [128, 16 * 128], BF16, RB)
    k2T = alloc("k2T", [128, 16 * 128], BF16, RB)
    kvt = alloc("kvt", [128, 16 * 128], BF16, RB)
    kvTt = alloc("kvTt", [128, 16 * 128 + 128], BF16, RB)
    t13 = alloc("t13", [128, 16 * 128], BF16, RB)
    sq_scr = alloc("sq_scr", [128, 1024], F32, RB)
    qlm = alloc("qlm", [128, 1024], F32, RB)
    klm = alloc("klm", [128, 1024], F32, RB)
    qlmb = alloc("qlmb", [128, 16 * 128], BF16, RB)
    k2sb = alloc("k2sb", [128, 2048], BF16, RB)
    zrun = alloc("zrun", [128, 1040], F32, RB)
    kvn = alloc("kvn", [128, 1024], BF16, RB)
    c_e7 = alloc("c_e7", [128, 512], BF16, RB)
    c_e15 = alloc("c_e15", [128, 512], BF16, RB)
    c_e13 = alloc("c_e13", [128, 512], BF16, RB)
    wqst = [alloc(f"wqst{q_}", [128, 2048], F32, RB) for q_ in range(4)]
    wqo = [alloc(f"wqo{q_}", [128, 2048], BF16, RB) for q_ in range(4)]
    assert RB[0] <= 229376, RB[0]
    assert RB[0] <= 229376, RB[0]

    RC = [RBASE]
    qt4 = alloc("qt4", [128, 8 * 512], BF16, RC)
    x4 = [alloc("x40", [128, D], F32, RC), alloc("x41", [128, D], F32, RC)]
    e4 = alloc("e4", [128, 8 * 512], BF16, RC)
    aslot = alloc("aslot", [128, D], F32, RC)
    t0slot = alloc("t0slot", [128, D], F32, RC)
    x2slot = [alloc("x2s0", [128, D], F32, RC), alloc("x2s1", [128, D], F32, RC)]
    xqtm2 = [alloc("xqtm0", [128, D], BF16, RC), alloc("xqtm1", [128, D], BF16, RC)]
    xqsub = alloc("xqsub", [128, 1024], BF16, RC)
    assert RC[0] <= 229376, RC[0]

    RD = [RBASE]
    h1buf = alloc("h1buf", [128, 32 * 512], BF16, RD)
    xqw = alloc("xqw", [128, 8 * 512], BF16, RD)
    x2blk = [alloc("x2b0", [128, D], F32, RD), alloc("x2b1", [128, D], F32, RD)]
    csc1 = alloc("csc1", [128, 512], BF16, RD)
    csc2 = alloc("csc2", [128, 512], BF16, RD)
    hmxp = alloc("hmxp", [128, 512], BF16, RD)
    hmnp = alloc("hmnp", [128, 512], BF16, RD)
    uq = alloc("uq", [128, 512], F32, RD)
    s2bc = alloc("s2bc", [128, 512], BF16, RD)
    fbuf = alloc("fbuf", [128, D], F32, RD)
    s2rowb = alloc("s2rowb", [1, 512], BF16, RD)
    w1sb = alloc("w1sb", [128, 8 * DH], BF16, RD)
    w2sb = alloc("w2sb", [128, 32 * D], BF16, RD)
    assert RD[0] <= 229376, RD[0]

    PS = [nc.alloc_psum_tensor(f"B{i}", [128, 512], F32) for i in range(7)]
    BT = nc.alloc_psum_tensor("BT", [128, 1024], BF16)

    p = Prog(nc)
    _warm_n = [0]

    def warm(wait, bank=0, n=64):
        _warm_n[0] += 1
        p.op("tensor", (lambda e, b=bank, n_=n: e.matmul(PS[b][:, 0:n_], c_eye_b[:, :],
                                                         c_eye_b[:, 0:n_], start=True, stop=True)),
             wait=wait, inc="cP")

    def mm(ps_ap, lhsT_ap, rhs_ap, start=True, stop=True):
        return lambda e: e.matmul(ps_ap, lhsT_ap, rhs_ap, start=start, stop=stop)

    def tr(ps_ap, in_ap):
        return lambda e: e.transpose(ps_ap, in_ap, c_eye_b[:])

    def cp(out_ap, in_ap):
        return lambda e: e.tensor_copy(out_ap, in_ap)

    def act2(out_ap, in_ap, func, bias=0.0, scale=1.0):
        return lambda e: e.activation(out_ap, in_ap, func, bias=bias, scale=scale)

    def acta(out_ap, in_ap, func, accum, bias=0.0, scale=1.0):
        return lambda e: e.activation(out_ap, in_ap, func, bias=bias, scale=scale, accum_out=accum)

    def ts1(out_ap, in_ap, s1, op0):
        return lambda e: e.tensor_scalar(out_ap, in_ap, s1, None, op0)

    def ts2(out_ap, in_ap, s1, s2, op0, op1):
        return lambda e: e.tensor_scalar(out_ap, in_ap, s1, s2, op0, op1)

    def tt(out_ap, a_ap, b_ap, opx):
        return lambda e: e.tensor_tensor(out_ap, a_ap, b_ap, opx)

    def stt(out_ap, in0, scalar, in1, op0, op1):
        return lambda e: e.scalar_tensor_tensor(out_ap, in0, scalar, in1, op0, op1)

    def red(out_ap, in_ap, opx, absv=False):
        return lambda e: e.tensor_reduce(out_ap, in_ap, AX.X, opx, apply_absolute_value=absv)

    def recip(out_ap, in_ap):
        return lambda e: e.reciprocal(out_ap, in_ap)

    def mset(ap, v):
        return lambda e: e.memset(ap, v)

    def dmas(*pairs):
        def f(e):
            return [e.dma_start(out=o_, in_=i_) for (o_, i_) in pairs]
        return f

    def dmas_nc(*pairs):
        def f(e):
            return [e.dma_start(out=o_, in_=i_, allow_slow_non_contiguous=True) for (o_, i_) in pairs]
        return f

    p.dma("gpsimd", dmas((fscr[:, 0:128], eye_d)), 1, inc="dG", label="ld_eye")
    p.op("vector", cp(c_eye_b[:], fscr[:, 0:128]), wait=["ld_eye"], inc="cV", label="eyeb")
    p.dma("gpsimd", dmas((c_pmask[:], pmask_d), (c_hmask[:], hmask_d), (c_bq[:], bq_d),
                         (c_bk[:], bk_d), (c_convw[:], convw_d), (c_convb[:], convb_d),
                         (c_xh[:], xh_d)), 7, inc="dG", label="ld_c1")
    p.dma("gpsimd", dmas((fscr[:, 0:2], cones_d), (fscr[:, 2:3], onescol_d)), 2,
          wait=["eyeb"], inc="dG", label="ld_c2")
    p.op("vector", cp(c_cones[:], fscr[:, 0:2]), wait=["ld_c2"], inc="cV")
    p.op("vector", cp(c_onescol[:], fscr[:, 2:3]), inc="cV", label="cones_b")
    p.dma("gpsimd", dmas((fscr[0:1, 0:128], onesrow_d)), 1, wait=["cones_b"], inc="dG", label="ld_or")
    p.op("vector", cp(c_onesrow[:], fscr[0:1, 0:128]), wait=["ld_or"], inc="cV", label="orow_b")
    p.dma("gpsimd", dmas((fscr[0:1, 0:128], m1row_d)), 1, wait=["orow_b"], inc="dG", label="ld_m1")
    p.op("vector", cp(c_m1row[:], fscr[0:1, 0:128]), wait=["ld_m1"], inc="cV", label="m1row_b")
    for c8 in range(8):
        p.dma("gpsimd", dmas((fscr[0:1, 0:128], bv_d[:, c8 * 128:(c8 + 1) * 128])), 1,
              wait=["m1row_b" if c8 == 0 else f"bvb{c8 - 1}"], inc="dG", label=f"ld_bv{c8}")
        p.op("vector", cp(c_bv_b[0:1, c8 * 128:(c8 + 1) * 128], fscr[0:1, 0:128]),
             wait=[f"ld_bv{c8}"], inc="cV", label=f"bvb{c8}")
    p.op("vector", ts1(c_bq[:], c_bq[:], S4, ALU.mult), wait=["ld_c1"], inc="cV")
    p.op("vector", ts1(c_bk[:], c_bk[:], S4, ALU.mult), inc="cV", label="bias_s4")
    p.op("vector", mset(qmax_run[:], 0.0), inc="cV", label="qmax0")
    p.op("vector", mset(fcol[:, 0:2], 1.0), inc="cV", label="fcol1")
    p.op("vector", mset(magicc[:], MAGIC), inc="cV", label="magic_c")
    for i0 in range(2):
        p.dma("sync", dmas((xbuf[i0][:].rearrange("p (s d) -> p s d", d=D),
                            xin[i0 * 512:i0 * 512 + 512, :].rearrange("(s pp) d -> pp s d", pp=128))), 1,
              wait=["bias_s4"], inc="dS", label=f"pa{i0}_xld")

    def wsl(j):
        return wstage[(j % 8) // 4][:, (j % 4) * 1024:(j % 4 + 1) * 1024]
    for wi, (wt_d, wsb) in enumerate([(wqT_d, wq_sb), (wkT_d, wk_sb), (wvT_d, wv_sb)]):
        q_eng, q_inc = [("scalar", "dA"), ("gpsimd", "dG"), ("sync", "dS")][wi]
        for k in range(8):
            j = wi * 8 + k
            wwait = [f"wcast_{j - 8}"] if j >= 8 else ["ld_c2"]
            p.dma(q_eng, dmas((wsl(j), wt_d[k * 128:(k + 1) * 128, :])), 1,
                  wait=wwait, inc=q_inc, label=f"wld_{j}")
            c_eng, c_inc = ("vector", "cV") if j % 2 == 0 else ("gpsimd", "cG")
            p.op(c_eng, cp(wsb[:, k * 1024:(k + 1) * 1024], wsl(j)),
                 wait=[f"wld_{j}"], inc=c_inc, label=f"wcast_{j}")

    wmean_chunks = [("w1", w1T_d, k, 4096, k) for k in range(8)] + \
                   [("w2", w2T_d, k, 1024, 8 + k) for k in range(32)]

    def emit_wmean(j):
        name, src, k, fdim, abscol = wmean_chunks[j]
        slot = j % 2
        wwait = [f"wm_red_{j - 2}"] if j >= 2 else ["wcast_22", "wcast_23"]
        p.dma("sync", dmas((wstage[slot][:, 0:fdim], src[k * 128:(k + 1) * 128, :])), 1,
              wait=wwait, inc="dS", label=f"wm_ld_{j}")
        p.op("vector", red(wabs[:, abscol:abscol + 1], wstage[slot][:, 0:fdim], ALU.add, absv=True),
             wait=[f"wm_ld_{j}"], inc="cV", label=f"wm_red_{j}")

    def emit_ln(x_ap_fn, nsub, pfx, wait0, psz=128, rv_wait=()):
        for s in range(nsub):
            xs = x_ap_fn(s)
            w = list(wait0) if s == 0 else []
            p.op("vector", (lambda e, xs=xs, psz=psz: e.bn_stats(sm[0:psz, 16:22], xs[0:psz, 0:512])),
                 wait=w, inc="cV")
            p.op("vector", (lambda e, xs=xs, psz=psz: e.bn_stats(sm[0:psz, 22:28], xs[0:psz, 512:1024])), inc="cV")
            p.op("vector", (lambda e, psz=psz: e.bn_aggr(sm[0:psz, 28:30], sm[0:psz, 16:22])),
                 inc="cV", sw=True)
            p.op("vector", (lambda e, psz=psz: e.bn_aggr(sm[0:psz, 30:32], sm[0:psz, 22:28])),
                 inc="cV", sw=True)
            p.op("vector", tt(lnmv[0:psz, 2 * s:2 * s + 1], sm[0:psz, 28:29], sm[0:psz, 30:31], ALU.add),
                 inc="cV", sw=True)
            p.op("vector", ts1(lnmv[0:psz, 2 * s:2 * s + 1], lnmv[0:psz, 2 * s:2 * s + 1], 0.5, ALU.mult),
                 inc="cV", sw=True)
            p.op("vector", tt(sm[0:psz, 14:15], sm[0:psz, 28:29], sm[0:psz, 30:31], ALU.subtract),
                 inc="cV", sw=True)
            p.op("vector", tt(sm[0:psz, 14:15], sm[0:psz, 14:15], sm[0:psz, 14:15], ALU.mult),
                 inc="cV", sw=True)
            p.op("vector", tt(sm[0:psz, 13:14], sm[0:psz, 29:30], sm[0:psz, 31:32], ALU.add),
                 inc="cV", sw=True)
            p.op("vector", stt(sm[0:psz, 13:14], sm[0:psz, 14:15], 0.5, sm[0:psz, 13:14], ALU.mult, ALU.add),
                 inc="cV", sw=True)
            p.op("vector", ts1(lnmv[0:psz, 2 * s + 1:2 * s + 2], sm[0:psz, 13:14], 0.5, ALU.mult),
                 inc="cV", sw=True)
            p.op("vector", ts1(veps4[0:psz, s:s + 1], lnmv[0:psz, 2 * s + 1:2 * s + 2], EPS, ALU.add),
                 inc="cV", sw=True, label=f"{pfx}_v{s}")
        p.op("vector", cp(rstd4[0:psz, 0:nsub], veps4[0:psz, 0:nsub]), wait=list(rv_wait), inc="cV",
             sw=True, label=f"{pfx}_rv")
        p.op("scalar", act2(lnscr[0:psz, 0:nsub], rstd4[0:psz, 0:nsub], AF.Ln), wait=[f"{pfx}_rv"], inc="cS")
        p.op("scalar", act2(rstd4[0:psz, 0:nsub], lnscr[0:psz, 0:nsub], AF.Exp, scale=-0.5),
             inc="cS", sw=True, label=f"{pfx}_rstd")

    def emit_pa_ln(i):
        lp = f"pa{i}"
        xb_ = xbuf[i % 2]
        emit_ln(lambda s: xb_[:, s * D:(s + 1) * D], 4, lp, [f"{lp}_xld"],
                rv_wait=([f"pa{i-1}_y3"] if i > 0 else ()))
        for s in range(4):
            p.op("vector", stt(negmr4[:, s:s + 1], lnmv[:, 2 * s:2 * s + 1], -1.0, rstd4[:, s:s + 1],
                               ALU.mult, ALU.mult),
                 wait=[f"{lp}_rstd"] if s == 0 else (), inc="cV", sw=True,
                 label=f"{lp}_nmr" if s == 3 else None)
        for s in range(4):
            yw = [f"{lp}_nmr"] + ([f"pa{i-1}_lm"] if i > 0 else [])
            p.op("scalar", act2(y1tm[:, s * D:(s + 1) * D], xb_[:, s * D:(s + 1) * D], AF.Identity,
                                bias=negmr4[:, s:s + 1], scale=rstd4[:, s:s + 1]),
                 wait=yw, inc="cS", label=f"{lp}_y{s}")

    emit_pa_ln(0)
    for i in range(PA_NT):
        t0 = i * 512
        lp = f"pa{i}"

        for pr in range(4):
            twait = [f"{lp}_tev{pr - 1}"] if pr >= 1 else \
                ([f"pa{i - 1}_tev3", f"pa{i - 1}_qnr3", f"pa{i - 1}_qkvdone"] if i > 0 else ["eyeb"])
            for c in (2 * pr, 2 * pr + 1):
                half = c % 2
                for s in range(4):
                    p.op("tensor", tr(BT[:, half * 512 + s * 128: half * 512 + (s + 1) * 128],
                                      y1tm[:, s * D + c * 128: s * D + (c + 1) * 128]),
                         wait=[f"{lp}_y{s}"] + twait, inc="cP",
                         label=f"{lp}_tr{pr}" if (s == 3 and c == 2 * pr + 1) else None)
            p.op("scalar", act2(y1fm[:, pr * 1024:(pr + 1) * 1024], BT[:, 0:1024], AF.Copy),
                 wait=[f"{lp}_tr{pr}"], inc="cS", label=f"{lp}_tev{pr}")

        for c in range(8):
            for s in range(4):
                p.op("tensor", mm(PS[6][:, c * 64 + i * 8 + s * 2: c * 64 + i * 8 + s * 2 + 2],
                                  y1tm[:, s * D + c * 128: s * D + (c + 1) * 128], c_cones[:]),
                     wait=[f"{lp}_y{s}", "cones_b"], inc="cP",
                     label=f"{lp}_lm" if (c == 7 and s == 3) else None)

        def emit_qk(nm, wsb, stg, bias, dst):
            for m in range(8):
                bank = PS[m % 2]
                for k in range(8):
                    mmw = [f"{lp}_tev{k // 2}"]
                    if m >= 2:
                        mmw.append(f"{lp}_{nm}ev{m - 2}")
                    elif nm == "k":
                        mmw.append(f"{lp}_qev{6 + m}")
                    elif i > 0:
                        mmw.append(f"pa{i-1}_kev{6 + m}")
                    p.op("tensor", mm(bank[:, 0:512], wsb[:, k * 1024 + m * 128: k * 1024 + (m + 1) * 128],
                                      y1fm[:, k * 512:(k + 1) * 512], start=(k == 0), stop=(k == 7)),
                         wait=mmw, inc="cP", label=f"{lp}_{nm}mm{m}" if k == 7 else None)
                evw = [f"{lp}_{nm}mm{m}"]
                if i > 0:
                    evw.append(f"pa{i-1}_{nm}sp")
                p.op("scalar", act2(stg[:, m * 512:(m + 1) * 512], bank[:, 0:512], AF.Identity,
                                    bias=bias[:, m:m + 1], scale=S4),
                     wait=evw, inc="cS", label=f"{lp}_{nm}ev{m}")
            p.dma("scalar", (lambda e, stg=stg, dst=dst, t0=t0:
                             [e.dma_start(out=dst[m_, :, t0:t0 + 512], in_=stg[:, m_ * 512:(m_ + 1) * 512])
                              for m_ in range(8)]), 8,
                  wait=[f"{lp}_{nm}ev7"], inc="dA", label=f"{lp}_{nm}sp")

        emit_qk("q", wq_sb, qstage, c_bq, q_sp)
        p.op("vector", tt(qsq[:], qstage[:], qstage[:], ALU.mult), wait=[f"{lp}_qev7"], inc="cV",
             label=f"{lp}_qsq")
        for c in range(1, 8):
            p.op("vector", tt(qsq[:, 0:512], qsq[:, 0:512], qsq[:, c * 512:(c + 1) * 512], ALU.add),
                 wait=[f"{lp}_qsq"] if c == 1 else (), inc="cV", sw=True, label=f"{lp}_qt{c}")
        for bb_ in range(4):
            p.op("vector", cp(pscr_b[:, bb_ * 128:(bb_ + 1) * 128], qsq[:, bb_ * 128:(bb_ + 1) * 128]),
                 inc="cV", sw=True, label=f"{lp}_qnc{bb_}")
        emit_qk("k", wk_sb, kstage, c_bk, k_sp)

        if i + 2 < PA_NT:
            xb2 = xbuf[i % 2]
            p.dma("sync", dmas((xb2[:].rearrange("p (s d) -> p s d", d=D),
                                xin[(i + 2) * 512:(i + 2) * 512 + 512, :].rearrange(
                                    "(s pp) d -> pp s d", pp=128))), 1,
                  wait=[f"pa{i}_y3"], inc="dS", label=f"pa{i + 2}_xld")
        if i + 1 < PA_NT:
            emit_pa_ln(i + 1)

        for s in range(4):
            vw = [f"{lp}_vev{s - 2}"] if s >= 2 else ([f"pa{i-1}_vev{2 + s}"] if i > 0 else [])
            for k in range(8):
                p.op("tensor", mm(PS[2 + (s % 2)][:, 0:512],
                                  y1fm[:, k * 512 + s * 128: k * 512 + s * 128 + 128],
                                  wv_sb[:, k * 1024: k * 1024 + 512], start=(k == 0), stop=(k == 7)),
                     wait=[f"{lp}_tev{k // 2}"] + vw, inc="cP")
            for k in range(8):
                p.op("tensor", mm(PS[4 + (s % 2)][:, 0:512],
                                  y1fm[:, k * 512 + s * 128: k * 512 + s * 128 + 128],
                                  wv_sb[:, k * 1024 + 512: k * 1024 + 1024], start=(k == 0), stop=(k == 7)),
                     inc="cP", label=f"{lp}_vmm{s}" if k == 7 else None)
            evw = [f"{lp}_vmm{s}"]
            if s >= 1:
                evw.append(f"{lp}_vsp{s-1}")
            elif i > 0:
                evw.append(f"pa{i-1}_vsp3")
            p.op("scalar", act2(vstage[:, s * D: s * D + 512], PS[2 + (s % 2)][:, 0:512], AF.Copy),
                 wait=evw, inc="cS")
            p.op("scalar", act2(vstage[:, s * D + 512: (s + 1) * D], PS[4 + (s % 2)][:, 0:512], AF.Copy),
                 inc="cS", label=f"{lp}_vev{s}")
            p.dma("scalar", dmas((v_sp[i * 4 + s, :, :], vstage[:, s * D:(s + 1) * D])), 1,
                  wait=[f"{lp}_vev{s}"], inc="dA", label=f"{lp}_vsp{s}")

        for bb_ in range(4):
            p.op("tensor", tr(BT[:, 512 + bb_ * 128: 512 + (bb_ + 1) * 128],
                              pscr_b[:, bb_ * 128:(bb_ + 1) * 128]),
                 wait=[f"{lp}_qnc{bb_}", f"{lp}_tev3"] + ([f"{lp}_qnr{bb_ - 1}"] if bb_ >= 1 else []),
                 inc="cP", label=f"{lp}_qntr{bb_}")
            p.op("vector", red(qmx[:, bb_:bb_ + 1], BT[:, 512 + bb_ * 128: 512 + (bb_ + 1) * 128], ALU.add),
                 wait=[f"{lp}_qntr{bb_}"], inc="cV", label=f"{lp}_qnr{bb_}")
        p.op("vector", red(sm[:, 15:16], qmx[:, 0:4], ALU.max), wait=["qmax0"], inc="cV", sw=True)
        p.op("vector", tt(qmax_run[:], qmax_run[:], sm[:, 15:16], ALU.max), inc="cV", sw=True,
             label=f"{lp}_qmax")
        p.op("vector", cp(sm[:, 14:15], sm[:, 14:15]), inc="cV", label=f"{lp}_qkvdone")
        for j in range(5 * i, min(5 * i + 5, 40)):
            emit_wmean(j)

    if "y1" in dbg:
        p.dma("sync", dmas((DBG("y1", [128, 4 * D]), xbuf[(PA_NT - 1) % 2][:]), (DBG("y1b", [128, 4 * D], BF16), y1tm[:]),
                           (DBG("lnst", [128, 32]), sm[:]), (DBG("lnmv", [128, 8]), lnmv[:]),
                           (DBG("rstd", [128, 4]), rstd4[:]), (DBG("veps", [128, 4]), veps4[:])), 6,
              wait=[f"pa{PA_NT-1}_y3"], inc="dS", label="dbg_y1")

    emit_ln(lambda s: c_xh[:], 1, "pah", ["ld_c1", f"pa{PA_NT-1}_y3"])
    p.op("vector", ts2(y1tm[:, 0:D], c_xh[:], lnmv[:, 0:1], rstd4[:, 0:1], ALU.subtract, ALU.mult),
         wait=["pah_rstd", f"pa{PA_NT-1}_lm"], inc="cV", label="pah_y0")
    for c in range(8):
        trw = ["pah_y0", f"pa{PA_NT-1}_qkvdone", f"pa{PA_NT-1}_tev3", f"pa{PA_NT-1}_qnr3"]
        if c >= 1:
            trw.append(f"pah_tev{c - 1}")
        p.op("tensor", tr(BT[:, c * 128:(c + 1) * 128], y1tm[:, c * 128:(c + 1) * 128]),
             wait=trw, inc="cP", label=f"pah_tr{c}")
        p.op("vector", cp(y1fm[:, c * 128:(c + 1) * 128], BT[:, c * 128:(c + 1) * 128]),
             wait=[f"pah_tr{c}"], inc="cV", label=f"pah_tev{c}")
    for m in range(8):
        for k in range(8):
            p.op("tensor", mm(PS[m % 2][:, 0:128], wq_sb[:, k * 1024 + m * 128: k * 1024 + (m + 1) * 128],
                              y1fm[:, k * 128:(k + 1) * 128], start=(k == 0), stop=(k == 7)),
                 wait=[f"pah_tev{k}"] + ([f"pah_qev{m - 2}"] if m >= 2 else [f"pa{PA_NT-1}_kev{6+m}"]),
                 inc="cP", label=f"pah_qmm{m}" if k == 7 else None)
        p.op("scalar", act2(qh_sb[:, m * 128:(m + 1) * 128], PS[m % 2][:, 0:128], AF.Identity,
                            bias=c_bq[:, m:m + 1], scale=S4),
             wait=[f"pah_qmm{m}"], inc="cS", label=f"pah_qev{m}")

    p.op("scalar", act2(xlm_sb[:], PS[6][:, 0:512], AF.Copy), wait=[f"pa{PA_NT-1}_lm"], inc="cS",
         label="xlm_ev")
    p.op("vector", cp(xlm_b[:], xlm_sb[:]), wait=["xlm_ev"], inc="cV", label="xlm_bb")
    for nm, wsb, dstp, bias in (("q", wq_sb, qlm_own, c_bq), ("k", wk_sb, klm_own, c_bk)):
        for m in range(8):
            bank = PS[2 + (m % 2)]
            for k in range(8):
                mmw = ["xlm_bb"]
                if m >= 2:
                    mmw.append(f"lmp_{nm}ev{m - 2}")
                elif nm == "k":
                    mmw.append(f"lmp_qev{6 + m}")
                else:
                    mmw.append(f"pa{PA_NT-1}_vev{2 + (m % 2)}")
                p.op("tensor", mm(bank[:, 0:64], wsb[:, k * 1024 + m * 128: k * 1024 + (m + 1) * 128],
                                  xlm_b[:, k * 64:(k + 1) * 64], start=(k == 0), stop=(k == 7)),
                     wait=mmw, inc="cP", label=f"lmp_{nm}mm{m}" if k == 7 else None)
            p.op("scalar", act2(dstp[:, m * 64:(m + 1) * 64], bank[:, 0:64], AF.Identity,
                                bias=bias[:, m:m + 1], scale=S4 / SEG),
                 wait=[f"lmp_{nm}mm{m}"], inc="cS", label=f"lmp_{nm}ev{m}")
    p.dma("scalar", dmas((e1_in[:, 0:512], qlm_own[:]), (e1_in[:, 512:1024], klm_own[:])), 2,
          wait=["lmp_qev7", "lmp_kev7"], inc="dA", label="e1_stage")
    p.cc(lambda e: e.collective_compute("AllGather", ALU.bypass, replica_groups=[list(range(8))],
                                        ins=[e1_in], outs=[e1_out]),
         wait=["e1_stage"], inc="cc1", label="e1_done")

    p.op("vector", red(wabs[:, 0:1], wabs[:, 0:8], ALU.add), wait=["wm_red_7"], inc="cV", sw=True)
    p.op("vector", red(wabs[:, 8:9], wabs[:, 8:40], ALU.add), wait=["wm_red_39"], inc="cV", sw=True, label="wabs_red")
    p.op("vector", cp(wabs_b[:, 0:1], wabs[:, 0:1]), wait=["wabs_red"], inc="cV", sw=True)
    p.op("vector", cp(wabs_b[:, 1:2], wabs[:, 8:9]), inc="cV", sw=True, label="wabs_bb")
    p.op("tensor", mm(PS[6][0:1, 0:2], c_onescol[:], wabs_b[:, 0:2]),
         wait=["wabs_bb", "cones_b", "xlm_ev"], inc="cP", label="wsum_mm")
    p.op("vector", ts2(rowf[0:1, 0:1], PS[6][0:1, 0:1], 1.0 / (DH * D), 1e-5, ALU.mult, ALU.max),
         wait=["wsum_mm"], inc="cV")
    p.op("vector", ts2(rowf[0:1, 1:2], PS[6][0:1, 1:2], 1.0 / (DH * D), 1e-5, ALU.mult, ALU.max),
         inc="cV", sw=True, label="rsw_row")
    p.op("vector", cp(rowb[0:1, 0:2], rowf[0:1, 0:2]), wait=["rsw_row"], inc="cV", sw=True, label="rsw_rb")
    p.op("tensor", mm(PS[6][:, 2:4], c_onesrow[:], rowb[0:1, 0:2]), wait=["rsw_rb", "orow_b"],
         inc="cP", label="rsw_bc")
    p.op("vector", cp(rsw1[:], PS[6][:, 2:3]), wait=["rsw_bc"], inc="cV")
    p.op("vector", cp(rsw2[:], PS[6][:, 3:4]), inc="cV", label="rsw_done")
    p.op("vector", recip(sw1[:], rsw1[:]), inc="cV", sw=True)
    p.op("vector", recip(sw2[:], rsw2[:]), inc="cV", sw=True, label="sw_done")

    wq_chunks = []
    for i_ in range(8):
        for h_ in range(2):
            wq_chunks.append(("w1", w1T_d, w1q_sp, i_, 2 * i_ + h_, 2048, 0))
        for k_ in range(4 * i_, 4 * i_ + 4):
            wq_chunks.append(("w2", w2T_d, w2q_sp, k_, None, 1024, 1))

    def emit_wq(j):
        name, src, dst, k, hh, fdim, wsel = wq_chunks[j]
        slot = j % 4
        if name == "w1":
            src_ap = src[k * 128:(k + 1) * 128, (hh % 2) * 2048:(hh % 2) * 2048 + 2048]
            dst_ap = dst[k * 128:(k + 1) * 128, (hh % 2) * 2048:(hh % 2) * 2048 + 2048]
        else:
            src_ap = src[k * 128:(k + 1) * 128, :]
            dst_ap = dst[k * 128:(k + 1) * 128, :]
        wwait = [f"wq_st_{j - 4}"] if j >= 4 else ["wm_red_38", "wm_red_39"]
        p.dma("scalar", dmas((wqst[slot][:, 0:fdim], src_ap)), 1,
              wait=wwait, inc="dA", label=f"wq_ld_{j}")
        swc = sw1[:, 0:1] if wsel == 0 else sw2[:, 0:1]
        p.op("scalar", act2(wqst[slot][:, 0:fdim], wqst[slot][:, 0:fdim], AF.Identity,
                            bias=magicc[:, 0:1], scale=swc),
             wait=[f"wq_ld_{j}", "sw_done", "magic_c"], inc="cS", label=f"wq_a_{j}")
        p.op("vector", ts2(wqst[slot][:, 0:fdim], wqst[slot][:, 0:fdim], MAGIC, 1.0,
                           ALU.subtract, ALU.min),
             wait=[f"wq_a_{j}"], inc="cV")
        p.op("vector", ts1(wqo[slot][:, 0:fdim], wqst[slot][:, 0:fdim], -1.0, ALU.max),
             inc="cV", label=f"wq_q_{j}")
        p.dma("gpsimd", dmas((dst_ap, wqo[slot][:, 0:fdim])), 1,
              wait=[f"wq_q_{j}"], inc="dG", label=f"wq_st_{j}")

    p.op("vector", mset(qlm[:], 0.0), wait=["e1_done"], inc="cV")
    p.op("vector", mset(klm[:], 0.0), inc="cV", label="lmacc0")
    for r in range(8):
        slot = r % 2
        gw = ["e1_done", "lmacc0"] + ([f"s2a_acc{r - 2}"] if r >= 2 else [])
        p.dma("sync", dmas((grbuf[slot][:, 0:1024], e1_out[r * 128:(r + 1) * 128, :])), 1,
              wait=gw, inc="dS", label=f"s2a_gld{r}")
        half = r % 2
        for c in range(8):
            p.op("vector", stt(qlm[:, c * 128 + half * 64: c * 128 + half * 64 + 64],
                               grbuf[slot][:, c * 64:(c + 1) * 64], c_pmask[:, r:r + 1],
                               qlm[:, c * 128 + half * 64: c * 128 + half * 64 + 64], ALU.mult, ALU.add),
                 wait=[f"s2a_gld{r}"], inc="cV", sw=(c == 0))
            p.op("vector", stt(klm[:, c * 128 + half * 64: c * 128 + half * 64 + 64],
                               grbuf[slot][:, 512 + c * 64: 512 + (c + 1) * 64], c_pmask[:, r:r + 1],
                               klm[:, c * 128 + half * 64: c * 128 + half * 64 + 64], ALU.mult, ALU.add),
                 inc="cV", label=f"s2a_acc{r}")
    p.op("vector", mset(qlmb[:], 0.0), wait=["s2a_acc7"], inc="cV", sw=True)
    p.op("vector", mset(klmb[:], 0.0), inc="cV")
    for h in range(H):
        pb_ = 64 * (h % 2)
        cc0_ = (h // 2) * 128
        p.op("vector", cp(qlmb[pb_:pb_ + 64, h * 128:(h + 1) * 128], qlm[pb_:pb_ + 64, cc0_:cc0_ + 128]),
             inc="cV")
        p.op("vector", cp(klmb[pb_:pb_ + 64, h * 128:(h + 1) * 128], klm[pb_:pb_ + 64, cc0_:cc0_ + 128]),
             inc="cV", label=f"lmp_pad{h}")
    p.op("vector", cp(sm[:, 12:13], sm[:, 12:13]), inc="cV", label="qlmb_done")
    p.op("vector", cp(sm[:, 12:13], sm[:, 12:13]), inc="cV", label="klmb_done")
    if "qlm" in dbg:
        p.dma("sync", dmas((DBG("qlm", [128, 1024]), qlm[:]), (DBG("klm", [128, 1024]), klm[:])), 2,
              wait=["klmb_done"], inc="dS", label="dbg_qlm")

    def lmnorm(src, outc, lbl):
        p.op("scalar", act2(sq_scr[:, 0:1024], src, AF.Square), wait=["klmb_done"], inc="cS",
             label=f"{lbl}_sq")
        p.op("vector", tt(sq_scr[:, 0:128], sq_scr[:, 0:128], sq_scr[:, 128:256], ALU.add),
             wait=[f"{lbl}_sq"], inc="cV", sw=True)
        for c in range(2, 8):
            p.op("vector", tt(sq_scr[:, 0:128], sq_scr[:, 0:128], sq_scr[:, c * 128:(c + 1) * 128], ALU.add),
                 inc="cV", sw=True)
        p.op("vector", cp(pscr_b[:, 0:128], sq_scr[:, 0:128]), inc="cV", sw=True, label=f"{lbl}_cb")
        p.op("tensor", tr(BT[:, 0:128], pscr_b[:, 0:128]), wait=[f"{lbl}_cb"], inc="cP", label=f"{lbl}_tr")
        p.op("vector", red(sm[:, 13:14], BT[:, 0:128], ALU.add), wait=[f"{lbl}_tr"], inc="cV", sw=True)
        p.op("vector", cp(smb[:, 1:2], sm[:, 13:14]), inc="cV", sw=True, label=f"{lbl}_c2")
        p.op("tensor", tr(BT[0:1, 128:256], smb[:, 1:2]), wait=[f"{lbl}_c2"], inc="cP", label=f"{lbl}_tr2")
        p.op("vector", red(rowf[0:1, outc:outc + 1], BT[0:1, 128:256], ALU.max),
             wait=[f"{lbl}_tr2"], inc="cV", label=f"{lbl}_mx")
        p.op("scalar", act2(rowf[0:1, outc + 1:outc + 2], rowf[0:1, outc:outc + 1], AF.Ln),
             wait=[f"{lbl}_mx"], inc="cS")
        p.op("scalar", act2(rowf[0:1, outc:outc + 1], rowf[0:1, outc + 1:outc + 2], AF.Exp, scale=0.5),
             inc="cS", label=lbl)

    lmnorm(klm[:], 8, "knorm")
    lmnorm(qlm[:], 10, "qnorm")

    p.op("vector", mset(csmax[:], 0.0), inc="cV", label="cs0")
    for h in range(H):
        bank = PS[h % 2]
        k2w = ["qlmb_done", "klmb_done"] + ([f"k2_{h - 2}"] if h >= 2 else [])
        p.op("tensor", mm(bank[:, 0:128], qlmb[:, h * 128:(h + 1) * 128], klmb[:, h * 128:(h + 1) * 128]),
             wait=k2w, inc="cP", label=f"k2mm_{h}")
        p.op("vector", red(sm[:, 12:13], bank[:, 0:128], ALU.max), wait=[f"k2mm_{h}"], inc="cV")
        p.op("vector", ts1(sm[:, 11:12], sm[:, 12:13], -1.0, ALU.mult), inc="cV", sw=True, label=f"k2neg_{h}")
        p.op("scalar", acta(k2sb[:, h * 128:(h + 1) * 128], bank[:, 0:128], AF.Exp,
                            sm[:, 10:11], bias=sm[:, 11:12]),
             wait=[f"k2neg_{h}"], inc="cS", label=f"k2exp_{h}")
        p.op("vector", recip(sm[:, 9:10], sm[:, 10:11]), wait=[f"k2exp_{h}"], inc="cV")
        p.op("vector", ts1(k2sb[:, h * 128:(h + 1) * 128], k2sb[:, h * 128:(h + 1) * 128],
                           sm[:, 9:10], ALU.mult), inc="cV", sw=True, label=f"k2n_{h}")
        p.op("tensor", mm(PS[2][:, (h % 4):(h % 4) + 1], k2sb[:, h * 128:(h + 1) * 128], c_onescol[:]),
             wait=[f"k2n_{h}"] + ([f"k2_{h - 1}"] if h >= 1 else []), inc="cP", label=f"k2cs_{h}")
        p.op("vector", tt(csmax[:], csmax[:], PS[2][:, (h % 4):(h % 4) + 1], ALU.max),
             wait=[f"k2cs_{h}", "cs0"], inc="cV", sw=True, label=f"k2_{h}")
    if "k2" in dbg:
        p.dma("sync", dmas((DBG("k2", [128, 2048], BF16), k2sb[:])), 1, wait=["k2_15"], inc="dS",
              label="dbg_k2")

    for i in range(PA_NT):
        t0 = i * 512
        lp = f"pb{i}"
        slot = i % 2
        kw = [f"pb{i - 2}_zm"] if i >= 2 else ["qlmb_done"]
        p.dma("sync", dmas((kt[slot][:].rearrange("p (c t) -> p c t", t=512),
                            k_sp[:, :, t0:t0 + 512].rearrange("c p t -> p c t"))), 1,
              wait=kw + [f"pa{PA_NT-1}_ksp"], inc="dS", label=f"{lp}_kld")
        p.dma("sync", dmas((vt[slot][:].rearrange("p (s f) -> p s f", f=D),
                            v_sp[i * 4:(i + 1) * 4, :, :].rearrange("s p f -> p s f"))), 1,
              wait=kw + [f"pa{PA_NT-1}_vsp3"], inc="dS", label=f"{lp}_vld")

        p.op("scalar", act2(ksq[:], kt[slot][:], AF.Square), wait=[f"{lp}_kld"], inc="cS",
             label=f"{lp}_ksq")
        for c in range(1, 8):
            p.op("vector", tt(ksq[:, 0:512], ksq[:, 0:512], ksq[:, c * 512:(c + 1) * 512], ALU.add),
                 wait=[f"{lp}_ksq"] if c == 1 else (), inc="cV", sw=True)
        for bb_ in range(4):
            p.op("vector", cp(pscr_b[:, 0:128], ksq[:, bb_ * 128:(bb_ + 1) * 128]), inc="cV", sw=True,
                 label=f"{lp}_knc{bb_}")
            p.op("tensor", tr(BT[:, bb_ * 128:(bb_ + 1) * 128], pscr_b[:, 0:128]),
                 wait=[f"{lp}_knc{bb_}"] + ([f"{lp}_knr{bb_ - 1}"] if bb_ >= 1 else
                                            ([f"pb{i-1}_km"] if i > 0 else [])),
                 inc="cP", label=f"{lp}_kntr{bb_}")
            p.op("vector", red(kmx[:, bb_:bb_ + 1], BT[:, bb_ * 128:(bb_ + 1) * 128], ALU.add),
                 wait=[f"{lp}_kntr{bb_}"], inc="cV", label=f"{lp}_knr{bb_}")
        for bb_ in range(4):
            warm([f"{lp}_knr{bb_}"], bank=0)
        p.op("vector", red(sm[:, 8:9], kmx[:, 0:4], ALU.max), inc="cV", sw=True)
        p.op("vector", cp(smb[:, 2:3], sm[:, 8:9]), inc="cV", sw=True, label=f"{lp}_kmc")
        p.op("tensor", tr(BT[0:1, 512:640], smb[:, 2:3]),
             wait=[f"{lp}_kmc", f"{lp}_kntr3"], inc="cP", label=f"{lp}_kmtr")
        p.op("vector", red(rowf[0:1, 12:13], BT[0:1, 512:640], ALU.max), wait=[f"{lp}_kmtr"], inc="cV",
             label=f"{lp}_km")
        p.op("scalar", act2(rowf[0:1, 11:12], rowf[0:1, 12:13], AF.Ln), wait=[f"{lp}_km"], inc="cS")
        p.op("scalar", act2(rowf[0:1, 12:13], rowf[0:1, 11:12], AF.Exp, scale=0.5), inc="cS", sw=True,
             label=f"{lp}_ksqrt")
        p.op("vector", tt(rowf[0:1, 13:14], rowf[0:1, 12:13], rowf[0:1, 10:11], ALU.mult),
             wait=[f"{lp}_ksqrt", "qnorm"], inc="cV", sw=True)
        p.op("vector", ts1(rowf[0:1, 13:14], rowf[0:1, 13:14], 1.02, ALU.mult),
             inc="cV", sw=True, label=f"{lp}_b2")
        p.op("vector", cp(rowb[0:1, 4:5], rowf[0:1, 13:14]), inc="cV", sw=True, label=f"{lp}_b2b")
        p.op("tensor", mm(PS[6][:, 16:17], c_m1row[:], rowb[0:1, 4:5]),
             wait=[f"{lp}_b2b", "m1row_b", "wsum_mm"] + ([f"pb{i-1}_negb2", f"pb{i-1}s3_zacc"] if i > 0 else []),
             inc="cP", label=f"{lp}_b2bc")
        nbw = [f"{lp}_b2bc"] + ([f"pb{i-1}s3_eexp3"] if i > 0 else [])
        p.op("vector", cp(negb2[:], PS[6][:, 16:17]), wait=nbw, inc="cV", label=f"{lp}_negb2")

        for s in range(4):
            sub = f"{lp}s{s}"
            prev = f"{lp}s{s-1}" if s > 0 else (f"pb{i-1}s3" if i > 0 else None)
            for h in range(H):
                bank = PS[h // 4]
                col = (h % 4) * 128
                mw = [f"{lp}_kld", "qlmb_done"]
                if prev is not None:
                    mw.append(f"{prev}_eexp{h // 4}")
                p.op("tensor", mm(bank[:, col:col + 128],
                                  kt[slot][:, (h // 2) * 512 + s * 128:(h // 2) * 512 + s * 128 + 128],
                                  qlmb[:, h * 128:(h + 1) * 128]),
                     wait=mw, inc="cP", label=f"{sub}_st{h}" if h % 4 == 3 else None)
            for b in range(4):
                ew = [f"{sub}_st{b * 4 + 3}", f"{lp}_negb2"]
                if prev is not None:
                    ew.append(f"{prev}_zdone")
                p.op("scalar", act2(esb[:, (s % 2) * 2048 + b * 512:(s % 2) * 2048 + (b + 1) * 512],
                                    PS[b][:, 0:512], AF.Exp, bias=negb2[:]),
                     wait=ew, inc="cS", label=f"{sub}_eexp{b}")
            for h in range(H):
                ebase = (s % 2) * 2048 + h * 128
                zw = [f"{sub}_eexp{h // 4}", f"{lp}_vld"]
                if prev is not None:
                    zw.append(f"{prev}_zacc")
                p.op("tensor", mm(PS[4 + h // 8][:, (h % 8) * 64:(h % 8) * 64 + 64],
                                  esb[:, ebase:ebase + 128],
                                  vt[slot][:, s * D + h * 64: s * D + h * 64 + 64]),
                     wait=zw, inc="cP")
                p.op("tensor", mm(PS[6][:, 32 + h: 33 + h], esb[:, ebase:ebase + 128], c_onescol[:]),
                     inc="cP", label=f"{sub}_zdone" if h == H - 1 else None)
            if i == 0 and s == 0:
                p.op("vector", cp(mrun[0:1, 0:1], rowf[0:1, 13:14]), wait=[f"{lp}_b2", f"{sub}_zdone"],
                     inc="cV", label=f"{lp}_mupd")
                p.op("vector", cp(zrun[:, 0:512], PS[4][:, 0:512]), inc="cV")
                p.op("vector", cp(zrun[:, 512:1024], PS[5][:, 0:512]), inc="cV")
                p.op("vector", cp(zrun[:, 1024:1040], PS[6][:, 32:48]), inc="cV", label=f"{sub}_zacc")
            else:
                zw2 = [f"{sub}_zdone"]
                if s == 0:
                    p.op("vector", tt(rowf[0:1, 14:15], mrun[0:1, 0:1], rowf[0:1, 13:14], ALU.max),
                         wait=[f"{lp}_b2", f"pb{i-1}_mupd", f"{sub}_zdone"], inc="cV", sw=True)
                    p.op("vector", tt(rowf[0:1, 15:16], mrun[0:1, 0:1], rowf[0:1, 14:15], ALU.subtract),
                         inc="cV", sw=True)
                    p.op("vector", tt(rowf[0:1, 16:17], rowf[0:1, 13:14], rowf[0:1, 14:15], ALU.subtract),
                         inc="cV", sw=True, label=f"{lp}_fsub")
                    p.op("scalar", act2(rowf[0:1, 15:17], rowf[0:1, 15:17], AF.Exp), wait=[f"{lp}_fsub"],
                         inc="cS", label=f"{lp}_fexp")
                    p.op("vector", cp(mrun[0:1, 0:1], rowf[0:1, 14:15]), wait=[f"{lp}_fexp"], inc="cV", sw=True,
                         label=f"{lp}_mupd")
                    p.op("vector", cp(rowb[0:1, 6:8], rowf[0:1, 15:17]), inc="cV", label=f"{lp}_fb")
                    p.op("tensor", mm(PS[6][:, 17:19], c_m1row[:], rowb[0:1, 6:8]),
                         wait=[f"{lp}_fb"], inc="cP", label=f"{lp}_fbc")
                    p.op("vector", ts1(fcol[:, 0:2], PS[6][:, 17:19], -1.0, ALU.mult),
                         wait=[f"{lp}_fbc"], inc="cV", label=f"{lp}_fcol")
                    p.op("vector", ts1(zrun[:, 0:1040], zrun[:, 0:1040], fcol[:, 0:1], ALU.mult),
                         wait=[f"{lp}_fcol"], inc="cV", sw=True, label=f"{lp}_zrs")
                    zw2.append(f"{lp}_zrs")
                fz = fcol[:, 1:2]
                p.op("vector", stt(zrun[:, 0:512], PS[4][:, 0:512], fz, zrun[:, 0:512], ALU.mult, ALU.add),
                     wait=zw2 + (["fcol1"] if (i == 0) else []), inc="cV", sw=True)
                p.op("vector", stt(zrun[:, 512:1024], PS[5][:, 0:512], fz, zrun[:, 512:1024],
                                   ALU.mult, ALU.add), inc="cV")
                p.op("vector", stt(zrun[:, 1024:1040], PS[6][:, 32:48], fz, zrun[:, 1024:1040],
                                   ALU.mult, ALU.add), inc="cV", label=f"{sub}_zacc")
        p.op("vector", cp(sm[:, 7:8], sm[:, 7:8]), wait=[f"{lp}s3_zacc"], inc="cV", label=f"{lp}_zm")
        for j_ in range(6 * i, min(6 * i + 6, 48)):
            emit_wq(j_)

    lastz = f"pb{PA_NT-1}_zm"
    p.op("vector", cp(rowb[0:1, 5:6], mrun[0:1, 0:1]), wait=[lastz, f"pb{PA_NT-1}_mupd"], inc="cV", sw=True,
         label="e2_mb")
    p.op("tensor", mm(PS[6][:, 20:21], c_m1row[:], rowb[0:1, 5:6]), wait=["e2_mb"], inc="cP", label="e2_mbc")
    p.op("vector", ts1(sm[:, 6:7], PS[6][:, 20:21], -1.0, ALU.mult), wait=["e2_mbc"], inc="cV",
         label="e2_mcol")
    p.dma("scalar", dmas_nc((e2_in[:, 0:1040], zrun[:, 0:1040]),
                            (e2_in[:, 1040:1041], sm[:, 6:7]),
                            (e2_in[:, 1041:1042], csmax[:])), 3,
          wait=[lastz, "e2_mcol", "k2_15"], inc="dA", label="e2_stage")
    p.cc(lambda e: e.collective_compute("AllGather", ALU.bypass, replica_groups=[list(range(8))],
                                        ins=[e2_in], outs=[e2_out]),
         wait=["e2_stage"], inc="cc2", label="e2_done")
    if "zrun" in dbg:
        p.dma("sync", dmas((DBG("zrun", [128, 1040]), zrun[:]), (DBG("rowf", [1, 64]), rowf[:]),
                           (DBG("mrun", [1, 2]), mrun[:])), 3,
              wait=["e2_stage"], inc="dS", label="dbg_zrun")

    p.op("vector", mset(zeven[:], 0.0), wait=["e2_stage"], inc="cV")
    p.op("vector", mset(zodd[:], 0.0), inc="cV")
    p.op("vector", mset(gcol[:], 0.0), inc="cV", label="s3_init")
    for r in range(8):
        slot = r % 2
        gw = ["e2_done", "s3_init"] + ([f"s3_acc{r - 2}"] if r >= 2 else [])
        p.dma("sync", dmas((grbuf[slot][:], e2_out[r * 128:(r + 1) * 128, :])), 1,
              wait=gw, inc="dS", label=f"s3_gld{r}")
        p.op("vector", stt(zeven[:], grbuf[slot][:], c_pmask[:, 8 + r:9 + r], zeven[:], ALU.mult, ALU.add),
             wait=[f"s3_gld{r}"], inc="cV")
        p.op("vector", stt(zodd[:], grbuf[slot][:], c_pmask[:, 16 + r:17 + r], zodd[:], ALU.mult, ALU.add),
             inc="cV")
        p.op("vector", tt(gcol[:], gcol[:], grbuf[slot][:, 1041:1042], ALU.max), inc="cV", sw=True,
             label=f"s3_acc{r}")
        warm([f"s3_acc{r}"], bank=3)
    p.op("vector", tt(sm[:, 0:1], zeven[:, 1040:1041], zodd[:, 1040:1041], ALU.max), wait=["s3_acc7"],
         inc="cV", sw=True)
    p.op("vector", tt(sm[:, 1:2], zeven[:, 1040:1041], sm[:, 0:1], ALU.subtract), inc="cV", sw=True)
    p.op("vector", tt(sm[:, 2:3], zodd[:, 1040:1041], sm[:, 0:1], ALU.subtract), inc="cV", sw=True, label="s3_fsub")
    p.op("scalar", act2(sm[:, 1:3], sm[:, 1:3], AF.Exp), wait=["s3_fsub"], inc="cS", label="s3_fexp")
    p.op("vector", ts1(zrun[:, 0:1040], zeven[:, 0:1040], sm[:, 1:2], ALU.mult), wait=["s3_fexp"],
         inc="cV")
    p.op("vector", stt(zrun[:, 0:1040], zodd[:, 0:1040], sm[:, 2:3], zrun[:, 0:1040], ALU.mult, ALU.add),
         inc="cV", label="s3_zmg")
    p.op("vector", recip(rgcol[:], gcol[:]), wait=["s3_acc7"], inc="cV", sw=True, label="s3_rg")
    warm(["s3_zmg"], bank=3)
    p.op("vector", recip(rs_sb[:, 0:16], zrun[:, 1024:1040]), wait=["s3_zmg"], inc="cV", sw=True, label="s3_rs")
    for h in range(H):
        p.op("vector", ts1(kvn[:, h * 64:(h + 1) * 64], zrun[:, h * 64:(h + 1) * 64],
                           rs_sb[:, h:h + 1], ALU.mult),
             wait=["s3_rs"], inc="cV", sw=(h == 0), label=f"s3_kvn{h}")
        if h % 4 == 3:
            warm([f"s3_kvn{h}"], bank=3)
    if "kvn" in dbg:
        p.dma("sync", dmas((DBG("kvn", [128, 1024], BF16), kvn[:])), 1, wait=["s3_kvn15"], inc="dS",
              label="dbg_kvn")

    for g in range(4):
        for j in range(4):
            h = g * 4 + j
            trw = [f"k2n_{h}", "s3_rg"] + ([f"s3_k2T{g - 1}g"] if g >= 1 else ["e2_mbc"])
            p.op("tensor", tr(BT[:, j * 128:(j + 1) * 128], k2sb[:, h * 128:(h + 1) * 128]),
                 wait=trw, inc="cP", label=f"s3_k2tr{g}" if j == 3 else None)
        p.op("vector", ts1(v0t[:, g * 512:(g + 1) * 512], BT[:, 0:512], rgcol[:], ALU.mult),
             wait=[f"s3_k2tr{g}"], inc="cV", label=f"s3_v0g{g}")
        p.op("vector", cp(k2T[:, g * 512:(g + 1) * 512], BT[:, 0:512]), inc="cV", label=f"s3_k2T{g}g")
    for h in range(H):
        p.labels[f"s3_v0_{h}"] = p.labels[f"s3_v0g{h // 4}"]
        p.labels[f"s3_k2T{h}"] = p.labels[f"s3_k2T{h // 4}g"]

    p.dma("sync", dmas((sq_scr[:, 0:512], eye4_d)), 1, wait=["qnorm", "knorm"], inc="dS",
          label="s3_e4ld")
    p.op("vector", ts1(c_e7[:], sq_scr[:, 0:512], 7.0, ALU.mult), wait=["s3_e4ld"], inc="cV", label="e7")
    p.op("vector", ts1(c_e15[:], sq_scr[:, 0:512], 15.0, ALU.mult), inc="cV", label="e15")
    p.op("vector", ts1(c_e13[:], sq_scr[:, 0:512], 13.0, ALU.mult), inc="cV", label="e13")
    vcur, vnext = v0t, v1t
    for it in range(NITER):
        for g in range(4):
            hs = list(range(g * 4, g * 4 + 4))
            gl = f"i{it}g{g}"
            prevg = f"i{it}g{g-1}" if g > 0 else (f"i{it-1}g3" if it > 0 else None)
            for j, h in enumerate(hs):
                w = [f"s3_k2T{h}"]
                if it == 0:
                    w.append(f"s3_v0_{h}")
                else:
                    w.append(f"i{it-1}g{g}_vev")
                if prevg is not None:
                    w.append(f"{prevg}_kvev")
                p.op("tensor", mm(PS[0][:, j * 128:(j + 1) * 128], k2T[:, h * 128:(h + 1) * 128],
                                  vcur[:, h * 128:(h + 1) * 128]),
                     wait=w, inc="cP", label=f"{gl}_kv" if j == 3 else None)
            p.op("scalar", act2(kvt[:, g * 512:(g + 1) * 512], PS[0][:, 0:512], AF.Copy),
                 wait=[f"{gl}_kv"], inc="cS", label=f"{gl}_kvev")
            for j, h in enumerate(hs):
                trw = [f"{gl}_kvev"]
                if prevg is not None:
                    trw.append(f"{prevg}_kvTev")
                    trw.append(f"{prevg}_vTev")
                p.op("tensor", tr(BT[:, j * 128:(j + 1) * 128], kvt[:, h * 128:(h + 1) * 128]),
                     wait=trw, inc="cP", label=f"{gl}_kvtr" if j == 3 else None)
            p.op("scalar", act2(kvTt[:, g * 512:(g + 1) * 512], BT[:, 0:512], AF.Copy),
                 wait=[f"{gl}_kvtr"], inc="cS", label=f"{gl}_kvTev")
            p.op("vector", stt(t13[:, g * 512:(g + 1) * 512], kvt[:, g * 512:(g + 1) * 512], -1.0,
                               c_e7[:], ALU.mult, ALU.add),
                 wait=[f"{gl}_kvev", "e7"] + ([f"{prevg}_t2"] if prevg else []), inc="cV",
                 label=f"{gl}_t1")
            for j, h in enumerate(hs):
                w = [f"{gl}_t1", f"{gl}_kvTev"]
                if prevg is not None:
                    w.append(f"{prevg}_t3")
                p.op("tensor", mm(PS[1][:, j * 128:(j + 1) * 128], kvTt[:, h * 128:(h + 1) * 128],
                                  t13[:, h * 128:(h + 1) * 128]),
                     wait=w, inc="cP", label=f"{gl}_t2" if j == 3 else None)
            p.op("vector", stt(t13[:, g * 512:(g + 1) * 512], PS[1][:, 0:512], -1.0, c_e15[:],
                               ALU.mult, ALU.add),
                 wait=[f"{gl}_t2", "e15"], inc="cV", label=f"{gl}_t3")
            for j, h in enumerate(hs):
                w = [f"{gl}_t3"]
                if prevg is not None:
                    w.append(f"{prevg}_t5")
                p.op("tensor", mm(PS[2][:, j * 128:(j + 1) * 128], kvTt[:, h * 128:(h + 1) * 128],
                                  t13[:, h * 128:(h + 1) * 128]),
                     wait=w, inc="cP", label=f"{gl}_t4" if j == 3 else None)
            p.op("vector", stt(t13[:, g * 512:(g + 1) * 512], PS[2][:, 0:512], -1.0, c_e13[:],
                               ALU.mult, ALU.add),
                 wait=[f"{gl}_t4", "e13"], inc="cV", label=f"{gl}_t5")
            for j, h in enumerate(hs):
                trw = [f"{gl}_t5"]
                if prevg is not None:
                    trw.append(f"{prevg}_vTev")
                p.op("tensor", tr(BT[:, 512 + j * 128: 512 + (j + 1) * 128], vcur[:, h * 128:(h + 1) * 128]),
                     wait=trw, inc="cP", label=f"{gl}_vtr" if j == 3 else None)
            p.op("scalar", act2(kvt[:, g * 512:(g + 1) * 512], BT[:, 512:1024], AF.Copy),
                 wait=[f"{gl}_vtr"], inc="cS", label=f"{gl}_vTev")
            for j, h in enumerate(hs):
                w = [f"{gl}_vTev", f"{gl}_t5"]
                if prevg is not None:
                    w.append(f"{prevg}_vev")
                p.op("tensor", mm(PS[3][:, j * 128:(j + 1) * 128], kvt[:, g * 512 + j * 128: g * 512 + (j + 1) * 128],
                                  t13[:, h * 128:(h + 1) * 128]),
                     wait=w, inc="cP", label=f"{gl}_vmm" if j == 3 else None)
            p.op("vector", ts1(vnext[:, g * 512:(g + 1) * 512], PS[3][:, 0:512], 0.25, ALU.mult),
                 wait=[f"{gl}_vmm"], inc="cV", label=f"{gl}_vev")
        vcur, vnext = vnext, vcur
    W2 = vcur

    for h in range(H):
        gl = f"ch{h}"
        w0 = [f"i{NITER-1}g{h // 4}_vev"] + ([f"ch{h-1}_ev"] if h >= 1 else [])
        p.op("tensor", tr(BT[:, 0:128], W2[:, h * 128:(h + 1) * 128]), wait=w0, inc="cP", label=f"{gl}_tr")
        p.op("scalar", act2(kvTt[:, 2048:2176], BT[:, 0:128], AF.Copy), wait=[f"{gl}_tr"], inc="cS",
             label=f"{gl}_w2T")
        p.op("tensor", mm(PS[0][:, 0:64], kvTt[:, 2048:2176], kvn[:, h * 64:(h + 1) * 64]),
             wait=[f"{gl}_w2T", f"s3_kvn{h}"], inc="cP", label=f"{gl}_c1")
        p.op("tensor", mm(PS[1][:, 0:1], kvTt[:, 2048:2176], c_onescol[:]),
             wait=[f"{gl}_w2T"], inc="cP", label=f"{gl}_u")
        p.op("vector", cp(smb[:, 4:5], PS[1][:, 0:1]), wait=[f"{gl}_u"], inc="cV", label=f"{gl}_ub")
        p.op("tensor", tr(BT[0:1, 512:640], smb[:, 4:5]), wait=[f"{gl}_ub"], inc="cP", label=f"{gl}_utr")
        p.op("scalar", act2(urow_b[0:1, 0:128], BT[0:1, 512:640], AF.Copy), wait=[f"{gl}_utr"], inc="cS",
             label=f"{gl}_urow")
        p.op("tensor", mm(PS[0][:, 0:64], urow_b[0:1, 0:128], c_bv_b[0:1, h * 64:(h + 1) * 64],
                          start=False, stop=True),
             wait=[f"{gl}_urow", "bvb7", f"{gl}_c1"], inc="cP", label=f"{gl}_c2")
        p.op("vector", cp(csb[:, h * 64:(h + 1) * 64], PS[0][:, 0:64]), wait=[f"{gl}_c2"], inc="cV",
             label=f"{gl}_ev")
    if "csb" in dbg:
        p.dma("sync", dmas((DBG("csb", [128, 1024], BF16), csb[:])), 1, wait=["ch15_ev"], inc="dS",
              label="dbg_csb")

    p.op("vector", cp(smb[:, 3:4], qmax_run[:]), wait=[f"pa{PA_NT-1}_qmax"], inc="cV", label="b1_c")
    p.op("tensor", tr(BT[0:1, 384:512], smb[:, 3:4]), wait=["b1_c", "ch15_ev"], inc="cP", label="b1_tr")
    p.op("vector", red(rowf[0:1, 20:21], BT[0:1, 384:512], ALU.max), wait=["b1_tr"], inc="cV", label="b1_mx")
    p.op("scalar", act2(rowf[0:1, 19:20], rowf[0:1, 20:21], AF.Ln), wait=["b1_mx"], inc="cS")
    p.op("scalar", act2(rowf[0:1, 20:21], rowf[0:1, 19:20], AF.Exp, scale=0.5), inc="cS", sw=True,
         label="b1_sqrt")
    p.op("vector", tt(rowf[0:1, 21:22], rowf[0:1, 20:21], rowf[0:1, 8:9], ALU.mult),
         wait=["b1_sqrt", "knorm"], inc="cV", sw=True)
    p.op("vector", ts1(rowf[0:1, 21:22], rowf[0:1, 21:22], 1.02, ALU.mult), inc="cV", sw=True, label="b1_s")
    p.op("vector", cp(rowb[0:1, 9:10], rowf[0:1, 21:22]), inc="cV", sw=True, label="b1_sb")
    p.op("tensor", mm(PS[6][:, 21:22], c_m1row[:], rowb[0:1, 9:10]), wait=["b1_sb"], inc="cP", label="b1_bc")
    p.op("vector", cp(negb1[:], PS[6][:, 21:22]), wait=["b1_bc"], inc="cV", label="b1_done")

    p.dma("sync", dmas((w1sb[:].rearrange("p (c h) -> p c h", h=DH),
                        w1q_sp.rearrange("(c pp) h -> pp c h", pp=128))), 1,
          wait=["wq_st_46", "wq_st_47", "ch15_ev", "b1_done"], inc="dS", label="w1_ld")
    p.dma("sync", dmas((w2sb[:].rearrange("p (c dd) -> p c dd", dd=D),
                        w2q_sp.rearrange("(c pp) dd -> pp c dd", pp=128))), 1,
          wait=["wq_st_46", "wq_st_47", "ch15_ev", "b1_done"], inc="dS", label="w2_ld")

    def emit_attn_sub(sub, qsrc_fn, ntok, extra_l1_wait, prev_sub):
        for h in range(H):
            bank = PS[h % 2]
            w = list(extra_l1_wait) + ["b1_done", "klmb_done"]
            if h >= 2:
                w.append(f"{sub}_ee{h - 2}")
            elif prev_sub is not None:
                w.append(f"{prev_sub}_ee{14 + h}")
            p.op("tensor", mm(bank[:, 0:ntok], klmb[:, h * 128:(h + 1) * 128], qsrc_fn(h)),
                 wait=w, inc="cP", label=f"{sub}_l1_{h}")
            ew = [f"{sub}_l1_{h}"]
            if h >= 4:
                ew.append(f"{sub}_om{h - 4}")
            elif prev_sub is not None:
                ew.append(f"{prev_sub}_om{12 + h}")
            p.op("scalar", act2(e4[:, (h % 8) * 512 + (h // 8) * 128: (h % 8) * 512 + (h // 8) * 128 + ntok],
                                bank[:, 0:ntok], AF.Exp, bias=negb1[:]),
                 wait=ew, inc="cS", label=f"{sub}_ee{h}")
            zw = [f"{sub}_ee{h}", "ch15_ev"]
            if prev_sub is not None:
                zw.append(f"{prev_sub}_aev")
            p.op("tensor", mm(PS[4 + h // 8][0:ntok, (h % 8) * 64:(h % 8) * 64 + 64],
                              e4[:, (h % 8) * 512 + (h // 8) * 128: (h % 8) * 512 + (h // 8) * 128 + ntok],
                              csb[:, h * 64:(h + 1) * 64]),
                 wait=zw, inc="cP")
            p.op("tensor", mm(PS[6][0:ntok, 48 + h:49 + h],
                              e4[:, (h % 8) * 512 + (h // 8) * 128: (h % 8) * 512 + (h // 8) * 128 + ntok],
                              c_onescol[:]),
                 inc="cP", label=f"{sub}_om{h}")
        p.op("vector", recip(rs_sb[0:ntok, 0:16], PS[6][0:ntok, 48:64]), wait=[f"{sub}_om15"], inc="cV",
             label=f"{sub}_rs")
        for h in range(H):
            p.op("scalar", (lambda e, h=h, ntok=ntok, sub=sub: e.activation(
                aslot[0:ntok, h * 64:(h + 1) * 64],
                PS[4 + h // 8][0:ntok, (h % 8) * 64:(h % 8) * 64 + 64],
                AF.Identity, scale=rs_sb[0:ntok, h:h + 1])),
                 wait=[f"{sub}_rs"], inc="cS", label=f"{sub}_aev" if h == H - 1 else None)

    prev_sub = None
    pending_tr = []
    for i in range(8):
        t0 = i * 512
        lp = f"fa{i}"
        qw = [f"fa{i-1}s3_xq"] if i > 0 else ["b1_done"]
        p.dma("sync", dmas((qt4[:].rearrange("p (c t) -> p c t", t=512),
                            q_sp[:, :, t0:t0 + 512].rearrange("c p t -> p c t"))), 1,
              wait=qw + [f"pa{PA_NT-1}_qsp"], inc="dS", label=f"{lp}_qld")
        for s in range(4):
            sub = f"{lp}s{s}"
            ss = s % 2
            xldw = [f"{lp}s{s-2}_x2"] if s >= 2 else ([f"fa{i-1}s{2+s}_x2"] if i > 0 else ["b1_done"])
            p.dma("sync", dmas((x4[ss][:], xin[t0 + s * 128:t0 + s * 128 + 128, :])), 1,
                  wait=xldw, inc="dS", label=f"{sub}_xld")
            emit_attn_sub(sub, (lambda h, s=s: qt4[:, (h // 2) * 512 + s * 128:(h // 2) * 512 + s * 128 + 128]),
                          128, [f"{lp}_qld"], prev_sub)
            prev_sub = sub
            while pending_tr:
                pending_tr.pop(0)()
            emit_ln(lambda s_: aslot[:], 1, sub + "A", [f"{sub}_aev"])
            p.op("vector", ts2(t0slot[:], aslot[:], lnmv[:, 0:1], rstd4[:, 0:1], ALU.subtract, ALU.mult),
                 wait=[sub + "A_rstd"], inc="cV", label=f"{sub}_t0")
            x2w = [f"{sub}_t0", f"{sub}_xld"]
            if s >= 2:
                x2w.append(f"{lp}s{s-2}_x2sp")
            elif i > 0:
                x2w.append(f"fa{i-1}s{2+s}_x2sp")
            p.op("gpsimd", tt(x2slot[ss][:], t0slot[:], x4[ss][:], ALU.add), wait=x2w, inc="cG",
                 label=f"{sub}_x2")
            p.dma("scalar", dmas((x2_sp[t0 + s * 128: t0 + s * 128 + 128, :], x2slot[ss][:])), 1,
                  wait=[f"{sub}_x2"], inc="dA", label=f"{sub}_x2sp")
            emit_ln(lambda s_: x2slot[ss][:], 1, sub + "B", [f"{sub}_x2"])
            p.op("vector", ts2(t0slot[:], x2slot[ss][:], lnmv[:, 0:1], rstd4[:, 0:1], ALU.subtract, ALU.mult),
                 wait=[sub + "B_rstd", f"{sub}_t0"], inc="cV", label=f"{sub}_y2")
            p.op("vector", red(sm[:, 5:6], t0slot[:], ALU.max, absv=True), wait=[f"{sub}_y2"], inc="cV")
            p.op("vector", ts1(sm[:, 4:5], sm[:, 5:6], 1e-5, ALU.max), inc="cV", sw=True)
            p.op("vector", recip(sm[:, 3:4], sm[:, 4:5]), inc="cV", sw=True)
            p.op("vector", ts1(sm[:, 2:3], sm[:, 3:4], 127.0, ALU.mult), inc="cV", sw=True, label=f"{sub}_s1")
            p.op("vector", ts1(sm[:, 1:2], sm[:, 4:5], 1.0 / 127.0, ALU.mult), inc="cV", sw=True)
            p.op("vector", tt(sm[:, 0:1], sm[:, 1:2], rsw1[:], ALU.mult), wait=["rsw_done"], inc="cV", sw=True,
                 label=f"{sub}_sf1")
            p.op("vector", ts2(t0slot[:], t0slot[:], sm[:, 2:3], MAGIC, ALU.mult, ALU.add),
                 wait=[f"{sub}_s1"], inc="cV", sw=True)
            p.op("vector", ts2(xqtm2[s % 2][:], t0slot[:], MAGIC, sm[:, 0:1], ALU.subtract, ALU.mult),
                 wait=[f"{sub}_sf1"] + ([f"{lp}s{s-2}_xqt"] if s >= 2 else
                                        ([f"fa{i-1}s{2+s}_xqt"] if i > 0 else [])),
                 inc="cV", label=f"{sub}_xq")
            def _emit_xqtr(sub=sub, s=s, i=i, lp=lp, t0=t0):
                xtw = [f"{sub}_xq"]
                if s > 0:
                    xtw.append(f"{lp}s{s-1}_xqe")
                elif i > 0:
                    xtw.append(f"fa{i-1}s3_xqe")
                for c in range(8):
                    p.op("tensor", tr(BT[:, c * 128:(c + 1) * 128], xqtm2[s % 2][:, c * 128:(c + 1) * 128]),
                         wait=xtw, inc="cP", label=f"{sub}_xqt" if c == 7 else None)
                p.op("scalar", act2(xqsub[:], BT[:, 0:1024], AF.Copy),
                     wait=[f"{sub}_xqt"] + ([f"{lp}s{s-1}_xqsp"] if s >= 1 else
                                            ([f"fa{i-1}s3_xqsp"] if i > 0 else [])),
                     inc="cS", label=f"{sub}_xqe")
                p.dma("scalar", (lambda e, t0=t0, s=s: [
                    e.dma_start(out=xq_sp[c_, :, 1 + t0 + s * 128: 1 + t0 + s * 128 + 128],
                                in_=xqsub[:, c_ * 128:(c_ + 1) * 128]) for c_ in range(8)]), 8,
                      wait=[f"{sub}_xqe"], inc="dA", label=f"{sub}_xqsp")
            pending_tr.append(_emit_xqtr)
        p.op("vector", cp(sm[:, 13:14], sm[:, 13:14]), wait=[f"{lp}s3_xq"], inc="cV",
             label=f"{lp}_xq_pre")

    while pending_tr:
        pending_tr.pop(0)()
    p.op("vector", cp(sm[:, 13:14], sm[:, 13:14]), wait=["fa7s3_xqsp"], inc="cV", label="fa7_xq_sp")
    if "x2" in dbg:
        d = DBG("x2", [LH, D])
        p.dma("sync", dmas((d, x2_sp)), 1, wait=["fa7s3_x2sp"], inc="dS", label="dbg_x2")

    hp = "hal"
    emit_attn_sub(hp, (lambda h: qh_sb[:, (h // 2) * 128:(h // 2) * 128 + 128]),
                  128, ["pah_qev7", "fa7_xq_sp"], prev_sub)
    emit_ln(lambda s_: aslot[:], 1, hp + "A", [f"{hp}_aev"])
    p.op("vector", ts2(t0slot[:], aslot[:], lnmv[:, 0:1], rstd4[:, 0:1], ALU.subtract, ALU.mult),
         wait=[hp + "A_rstd"], inc="cV", label=f"{hp}_t0")
    p.op("gpsimd", tt(x2slot[0][:], t0slot[:], c_xh[:], ALU.add),
         wait=[f"{hp}_t0", "fa7s2_x2sp"], inc="cG", label=f"{hp}_x2")
    emit_ln(lambda s_: x2slot[0][:], 1, hp + "B", [f"{hp}_x2"])
    p.op("vector", ts2(t0slot[:], x2slot[0][:], lnmv[:, 0:1], rstd4[:, 0:1], ALU.subtract, ALU.mult),
         wait=[hp + "B_rstd", f"{hp}_t0"], inc="cV", label=f"{hp}_y2")
    p.op("vector", red(sm[:, 5:6], t0slot[:], ALU.max, absv=True), wait=[f"{hp}_y2"], inc="cV")
    p.op("vector", ts1(sm[:, 4:5], sm[:, 5:6], 1e-5, ALU.max), inc="cV", sw=True)
    p.op("vector", recip(sm[:, 3:4], sm[:, 4:5]), inc="cV", sw=True)
    p.op("vector", ts1(sm[:, 2:3], sm[:, 3:4], 127.0, ALU.mult), inc="cV", sw=True, label=f"{hp}_s1")
    p.op("vector", ts1(sm[:, 1:2], sm[:, 4:5], 1.0 / 127.0, ALU.mult), inc="cV", sw=True)
    p.op("vector", tt(sm[:, 0:1], sm[:, 1:2], rsw1[:], ALU.mult), inc="cV", sw=True, label=f"{hp}_sf1")
    p.op("vector", ts2(t0slot[:], t0slot[:], sm[:, 2:3], MAGIC, ALU.mult, ALU.add),
         wait=[f"{hp}_s1"], inc="cV", sw=True)
    p.op("vector", ts2(xqtm2[0][:], t0slot[:], MAGIC, sm[:, 0:1], ALU.subtract, ALU.mult),
         wait=[f"{hp}_sf1", "fa7s2_xqt"], inc="cV")
    p.op("vector", ts1(xqtm2[0][:], xqtm2[0][:], c_hmask[:, 0:1], ALU.mult),
         wait=["ld_c1"], inc="cV", label=f"{hp}_xq")
    for c in range(8):
        p.op("tensor", tr(BT[:, c * 128:(c + 1) * 128], xqtm2[0][:, c * 128:(c + 1) * 128]),
             wait=[f"{hp}_xq", "fa7s3_xqe"], inc="cP", label=f"{hp}_xqt" if c == 7 else None)
    p.op("scalar", act2(xqsub[:], BT[:, 0:1024], AF.Copy),
         wait=[f"{hp}_xqt", "fa7s3_xqsp"], inc="cS", label=f"{hp}_xqe")
    p.dma("scalar", (lambda e: [e.dma_start(out=xq_sp[c_, :, 0:1], in_=xqsub[:, c_ * 128:c_ * 128 + 1],
                                            allow_slow_non_contiguous=True)
                                for c_ in range(8)] +
                               [e.dma_start(out=xq_sp[c_, :, LH + 1:LH + 2],
                                            in_=xqsub[:, c_ * 128 + 1:c_ * 128 + 2],
                                            allow_slow_non_contiguous=True)
                                for c_ in range(8)]), 16,
          wait=[f"{hp}_xqe"], inc="dA", label="hal_xq_sp")

    store_labels = []
    p0s = np.cumsum([0] + S4B_SIZES).tolist()
    for j, nj in enumerate(S4B_SIZES):
        p0 = int(p0s[j])
        lp = f"fb{j}"
        nw = nj + 2
        bsz = [128, 128, 128, nj - 384] if nj > 384 else [nj]
        nblk = len(bsz)
        xw = [f"fb{j-1}_mm1_31"] if j > 0 else ["hal_xq_sp"]
        p.dma("sync", dmas((xqw[:].rearrange("p (c t) -> p c t", t=512)[:, :, 0:nw],
                            xq_sp[:, :, p0:p0 + nw].rearrange("c p t -> p c t"))), 1,
              wait=xw, inc="dS", label=f"{lp}_xqld")
        for m in range(32):
            bank = PS[m % 2]
            for k in range(8):
                w = [f"{lp}_xqld", "w1_ld"]
                if m >= 2:
                    w.append(f"{lp}_h1ev{m - 2}")
                elif j > 0:
                    w.append(f"fb{j-1}_mm2done{m}")
                p.op("tensor", mm(bank[:, 0:nw], w1sb[:, k * DH + m * 128: k * DH + (m + 1) * 128],
                                  xqw[:, k * 512: k * 512 + nw], start=(k == 0), stop=(k == 7)),
                     wait=w, inc="cP", label=f"{lp}_mm1_{m}" if k == 7 else None)
            p.op("scalar", act2(h1buf[:, m * 512: m * 512 + nw], bank[:, 0:nw], AF.Copy),
                 wait=[f"{lp}_mm1_{m}"] + ([f"fb{j-1}_q2_{m}"] if j > 0 else []),
                 inc="cS", label=f"{lp}_h1ev{m}")
            cw = [f"{lp}_h1ev{m}"]
            if m >= 1:
                cw.append(f"{lp}_cb{m - 1}")
            p.op("vector", ts1(csc1[:, 0:nj], h1buf[:, m * 512: m * 512 + nj],
                               c_convw[:, 3 * m: 3 * m + 1], ALU.mult),
                 wait=cw + ["ld_c1"], inc="cV", label=f"{lp}_ca{m}")
            p.op("vector", stt(csc2[:, 0:nj], h1buf[:, m * 512 + 1: m * 512 + 1 + nj],
                               c_convw[:, 3 * m + 1: 3 * m + 2], csc1[:, 0:nj], ALU.mult, ALU.add),
                 wait=[f"{lp}_ca{m}"], inc="cV", sw=(nj < 128))
            p.op("vector", stt(h1buf[:, m * 512 + 1: m * 512 + 1 + nj],
                               h1buf[:, m * 512 + 2: m * 512 + 2 + nj],
                               c_convw[:, 3 * m + 2: 3 * m + 3], csc2[:, 0:nj], ALU.mult, ALU.add),
                 inc="cV", sw=(nj < 128), label=f"{lp}_cb{m}")
            p.op("scalar", act2(h1buf[:, m * 512 + 1: m * 512 + 1 + nj],
                                h1buf[:, m * 512 + 1: m * 512 + 1 + nj],
                                AF.Silu, bias=c_convb[:, m:m + 1]),
                 wait=[f"{lp}_cb{m}"], inc="cS", label=f"{lp}_cv{m}")
            if m == 1:
                p.op("vector", tt(hmxp[:, 0:nj], h1buf[:, 1: 1 + nj], h1buf[:, 512 + 1: 512 + 1 + nj],
                                  ALU.max), wait=[f"{lp}_cv0", f"{lp}_cv1"], inc="cV", sw=(nj < 128))
                p.op("vector", tt(hmnp[:, 0:nj], h1buf[:, 1: 1 + nj], h1buf[:, 512 + 1: 512 + 1 + nj],
                                  ALU.min), inc="cV", sw=(nj < 128))
            elif m >= 2:
                lblm = f"{lp}_hm" if m == 31 else (f"{lp}_tm{m}" if m >= 28 else None)
                p.op("vector", tt(hmxp[:, 0:nj], hmxp[:, 0:nj], h1buf[:, m * 512 + 1: m * 512 + 1 + nj],
                                  ALU.max), wait=[f"{lp}_cv{m}"], inc="cV", sw=(nj < 128))
                p.op("vector", tt(hmnp[:, 0:nj], hmnp[:, 0:nj], h1buf[:, m * 512 + 1: m * 512 + 1 + nj],
                                  ALU.min), inc="cV", sw=(nj < 128), label=lblm)
                if m >= 28:
                    warm([lblm], bank=0)
        for b in range(nblk):
            bs = bsz[b]
            trw = [f"{lp}_hm"] + ([f"{lp}_s2r{b - 2}"] if b >= 2 else [])
            p.op("tensor", tr(BT[0:bs, (b % 2) * 256:(b % 2) * 256 + 128], hmxp[:, b * 128: b * 128 + bs]),
                 wait=trw, inc="cP", label=f"{lp}_s2t{b}")
            p.op("tensor", tr(BT[0:bs, (b % 2) * 256 + 128:(b % 2) * 256 + 256],
                              hmnp[:, b * 128: b * 128 + bs]),
                 wait=trw, inc="cP", label=f"{lp}_s2tn{b}")
            p.op("vector", red(s2mx[0:bs, b:b + 1], BT[0:bs, (b % 2) * 256:(b % 2) * 256 + 128], ALU.max),
                 wait=[f"{lp}_s2t{b}"], inc="cV")
            p.op("vector", red(s2mn[0:bs, b:b + 1], BT[0:bs, (b % 2) * 256 + 128:(b % 2) * 256 + 256],
                               ALU.min),
                 wait=[f"{lp}_s2tn{b}"], inc="cV", label=f"{lp}_s2r{b}")
            warm([f"{lp}_s2r{b}"], bank=0)
        p.op("vector", ts1(s2mn[:, 0:nblk], s2mn[:, 0:nblk], -1.0, ALU.mult),
             wait=[f"{lp}_s2r{b_}" for b_ in range(nblk)], inc="cV", sw=True)
        p.op("vector", tt(s2mx[:, 0:nblk], s2mx[:, 0:nblk], s2mn[:, 0:nblk], ALU.max), inc="cV", sw=True)
        p.op("vector", ts1(s2mx[:, 0:nblk], s2mx[:, 0:nblk], 1e-5, ALU.max), inc="cV", sw=True,
             label=f"{lp}_s2m")
        warm([f"{lp}_s2m"], bank=0)
        p.op("vector", recip(s2mn[:, 0:nblk], s2mx[:, 0:nblk]), inc="cV", sw=True)
        p.op("vector", ts1(s2mn[:, 0:nblk], s2mn[:, 0:nblk], 127.0, ALU.mult), inc="cV", sw=True,
             label=f"{lp}_s2s")
        p.op("vector", ts1(sf2col[:, 0:nblk], s2mx[:, 0:nblk], 1.0 / 127.0, ALU.mult), inc="cV", sw=True)
        p.op("vector", ts1(sf2col[:, 0:nblk], sf2col[:, 0:nblk], rsw2[:, 0:1], ALU.mult),
             wait=["rsw_done"], inc="cV", sw=True, label=f"{lp}_sf2")
        warm([f"{lp}_sf2"], bank=0)
        for b in range(nblk):
            bs = bsz[b]
            p.op("vector", cp(smb[0:bs, 5:6], s2mn[0:bs, b:b + 1]),
                 wait=[f"{lp}_s2s"] + ([f"{lp}_s2rt{b - 1}"] if b >= 1 else []), inc="cV", sw=True,
                 label=f"{lp}_s2cb{b}")
            p.op("tensor", tr(BT[0:1, 512 + (b % 2) * 128: 512 + (b % 2) * 128 + 128], smb[:, 5:6]),
                 wait=[f"{lp}_s2cb{b}"] + ([f"{lp}_s2c{b - 2}"] if b >= 2 else []),
                 inc="cP", label=f"{lp}_s2rt{b}")
            p.op("scalar", act2(s2rowb[0:1, b * 128: b * 128 + bs],
                                BT[0:1, 512 + (b % 2) * 128: 512 + (b % 2) * 128 + bs], AF.Copy),
                 wait=[f"{lp}_s2rt{b}"], inc="cS", label=f"{lp}_s2c{b}")
        p.op("tensor", mm(PS[6][:, 0:nj], c_onesrow[:], s2rowb[0:1, 0:nj]),
             wait=[f"{lp}_s2c{b}" for b in range(nblk)] + ["orow_b"] + ([f"fb{j-1}_s2bce"] if j > 0 else []), inc="cP", label=f"{lp}_s2bc")
        p.op("vector", cp(s2bc[:, 0:nj], PS[6][:, 0:nj]), wait=[f"{lp}_s2bc"], inc="cV",
             label=f"{lp}_s2bce")

        for m in range(32):
            p.op("vector", tt(uq[:, 0:nj], h1buf[:, m * 512 + 1: m * 512 + 1 + nj], s2bc[:, 0:nj],
                              ALU.mult), wait=[f"{lp}_s2bce", f"{lp}_hm"], inc="cV", sw=(m == 0))
            p.op("vector", ts2(h1buf[:, m * 512 + 1: m * 512 + 1 + nj], uq[:, 0:nj], MAGIC, MAGIC,
                               ALU.add, ALU.subtract), inc="cV", sw=(nj < 128), label=f"{lp}_q2_{m}")
            warm([f"{lp}_q2_{m}"], bank=0)
        if j == 8 and "fb8" in dbg:
            p.dma("sync", dmas((DBG("fb8h1", [128, 1024], BF16), h1buf[:, 0:1024]),
                               (DBG("fb8xq", [128, 1024], BF16), xqw[:, 0:1024]),
                               (DBG("fb8sc", [128, 4]), sf2col[:, 0:4]),
                               (DBG("fb8s2", [128, 16], BF16), s2bc[:, 0:16]),
                               (DBG("fb8mx", [128, 16], BF16), hmxp[:, 0:16])), 5,
                  wait=[f"{lp}_q2_31"], inc="dS", label="dbg_fb8")
        for b in range(nblk):
            bs = bsz[b]
            bb = 2 + (b % 2) * 2
            for k in range(32):
                w = [f"{lp}_q2_{k}", "w2_ld"]
                if b >= 2:
                    w.append(f"{lp}_f{b - 2}")
                elif j > 0:
                    w.append(f"fb{j-1}_f{nblk_prev - 2 + b}" if nblk_prev >= 3 else f"fb{j-1}_f0")
                p.op("tensor", mm(PS[bb][0:bs, 0:512],
                                  h1buf[:, k * 512 + 1 + b * 128: k * 512 + 1 + b * 128 + bs],
                                  w2sb[:, k * 1024: k * 1024 + 512], start=(k == 0), stop=(k == 31)),
                     wait=w, inc="cP")
                p.op("tensor", mm(PS[bb + 1][0:bs, 0:512],
                                  h1buf[:, k * 512 + 1 + b * 128: k * 512 + 1 + b * 128 + bs],
                                  w2sb[:, k * 1024 + 512: k * 1024 + 1024], start=(k == 0), stop=(k == 31)),
                     inc="cP", label=f"{lp}_mm2_{b}" if k == 31 else None)
            fw = [f"{lp}_mm2_{b}", f"{lp}_sf2"]
            if b >= 1:
                fw.append(f"{lp}_st{b - 1}")
            elif j > 0:
                fw.append(f"fb{j-1}_st{nblk_prev - 1}")
            p.op("scalar", act2(fbuf[0:bs, 0:512], PS[bb][0:bs, 0:512], AF.Identity,
                                scale=sf2col[0:bs, b:b + 1]),
                 wait=fw, inc="cS")
            p.op("scalar", act2(fbuf[0:bs, 512:1024], PS[bb + 1][0:bs, 0:512], AF.Identity,
                                scale=sf2col[0:bs, b:b + 1]), inc="cS", label=f"{lp}_f{b}")
            x2w_ = [f"{lp}_f{b}"]
            if j == 0 and b == 0:
                x2w_.append("fa7s3_x2sp")
            if b >= 2:
                x2w_.append(f"{lp}_o{b - 2}")
            elif j > 0:
                x2w_.append(f"fb{j-1}_o{max(0, nblk_prev - 2 + b)}")
            p.dma("sync", dmas((x2blk[b % 2][0:bs, :], x2_sp[p0 + b * 128: p0 + b * 128 + bs, :])), 1,
                  wait=x2w_, inc="dS", label=f"{lp}_x2ld{b}")
            emit_ln(lambda s_: fbuf[:], 1, f"{lp}LN{b}", [f"{lp}_f{b}"], psz=bs)
            p.op("vector", ts2(fbuf[0:bs, :], fbuf[0:bs, :], lnmv[0:bs, 0:1], rstd4[0:bs, 0:1],
                               ALU.subtract, ALU.mult),
                 wait=[f"{lp}LN{b}_rstd"], inc="cV", label=f"{lp}_fn{b}")
            p.op("gpsimd", tt(fbuf[0:bs, :], fbuf[0:bs, :], x2blk[b % 2][0:bs, :], ALU.add),
                 wait=[f"{lp}_fn{b}", f"{lp}_x2ld{b}"], inc="cG", label=f"{lp}_o{b}")
            p.dma("scalar", dmas((yout[p0 + b * 128: p0 + b * 128 + bs, :], fbuf[0:bs, :])), 1,
                  wait=[f"{lp}_o{b}"], inc="dA", label=f"{lp}_st{b}")
            store_labels.append(f"{lp}_st{b}")
        for m in range(32):
            p.labels[f"{lp}_mm2done{m}"] = p.labels[f"{lp}_mm2_{nblk - 1}"]
        nblk_prev = nblk

    if stop is not None:
        p.ops = p.ops[: p.label_idx[stop] + 1]
        p.barrier_wait("sync", [stop])
    else:
        p.barrier_wait("sync", store_labels)

    from contextlib import ExitStack
    sems = {}
    ctx = ExitStack()
    for s in ["dS", "dA", "dG", "cV", "cS", "cP", "cG", "cc1", "cc2"]:
        sems[s] = ctx.enter_context(nc.semaphore(s))
    p.lower(sems)
    nc._sem_ctx = ctx
    return nc, dbg_outs


def _stage_inputs(L0_attn_x, Wq, bq, Wk, bk, Wv, bv, w1, conv_w, conv_b, w2):
    x = np.ascontiguousarray(np.asarray(L0_attn_x, dtype=np.float32))
    B, L, _ = x.shape
    wqT = np.ascontiguousarray(np.asarray(Wq, np.float32).T)
    wkT = np.ascontiguousarray(np.asarray(Wk, np.float32).T)
    wvT = np.ascontiguousarray(np.asarray(Wv, np.float32).T)
    w1T = np.ascontiguousarray(np.asarray(w1, np.float32).T)
    w2T = np.ascontiguousarray(np.asarray(w2, np.float32).T)
    bq_r = np.ascontiguousarray(np.asarray(bq, np.float32).reshape(8, 128).T)
    bk_r = np.ascontiguousarray(np.asarray(bk, np.float32).reshape(8, 128).T)
    bv_row = np.ascontiguousarray(np.asarray(bv, np.float32).reshape(1, D))
    convw_r = np.ascontiguousarray(np.asarray(conv_w, np.float32).reshape(DH, 3)
                                   .reshape(32, 128, 3).transpose(1, 0, 2).reshape(128, 96))
    convb_r = np.ascontiguousarray(np.asarray(conv_b, np.float32).reshape(32, 128).T)
    cones = np.zeros((128, 2), np.float32)
    cones[0:64, 0] = 1.0
    cones[64:128, 1] = 1.0
    eye_f = np.eye(128, dtype=np.float32)
    eye4_f = np.ascontiguousarray(np.tile(eye_f, (1, 4)))
    onesrow = np.ones((1, 128), np.float32)
    m1row = -np.ones((1, 128), np.float32)
    onescol = np.ones((128, 1), np.float32)

    in_maps = []
    for c in range(8):
        b = c // 2
        half = c % 2
        t0 = half * LH
        xs = np.ascontiguousarray(x[b, t0:t0 + LH, :])
        xhv = np.zeros((128, D), np.float32)
        if t0 > 0:
            xhv[0] = x[b, t0 - 1, :]
        if t0 + LH < L:
            xhv[1] = x[b, t0 + LH, :]
        hmask = np.zeros((128, 64), np.float32)
        hmask[0, 0] = 1.0 if t0 > 0 else 0.0
        hmask[1, 0] = 1.0 if t0 + LH < L else 0.0
        pmask = np.zeros((128, 24), np.float32)
        pmask[:, 2 * b] = 1.0
        pmask[:, 2 * b + 1] = 1.0
        pmask[:, 8 + 2 * b] = 1.0
        pmask[:, 16 + 2 * b + 1] = 1.0
        in_maps.append(dict(
            xin=xs, xh=xhv, wqT=wqT, wkT=wkT, wvT=wvT, w1T=w1T, w2T=w2T,
            bq_r=bq_r, bk_r=bk_r, bv_row=bv_row, convw_r=convw_r, convb_r=convb_r,
            cones=cones, eye_f=eye_f, eye4_f=eye4_f, onesrow=onesrow, m1row=m1row,
            onescol=onescol, pmask=pmask, hmask=hmask,
        ))
    return in_maps


_CACHED = {}


def kernel(L0_attn_x, L0_attn_Wq, L0_attn_bq, L0_attn_Wk, L0_attn_bk, L0_attn_Wv,
           L0_attn_bv, L0_ffn_w1, L0_ffn_conv_w, L0_ffn_conv_b, L0_ffn_w2):
    in_maps = _stage_inputs(L0_attn_x, L0_attn_Wq, L0_attn_bq, L0_attn_Wk, L0_attn_bk,
                            L0_attn_Wv, L0_attn_bv, L0_ffn_w1, L0_ffn_conv_w,
                            L0_ffn_conv_b, L0_ffn_w2)
    if "nc" not in _CACHED:
        _CACHED["nc"] = build_program()[0]
    nc = _CACHED["nc"]
    res = run_bass_kernel_spmd(nc, in_maps, core_ids=list(range(8)))
    out = np.empty((4, 2 * LH, D), np.float32)
    for c in range(8):
        out[c // 2, (c % 2) * LH:(c % 2 + 1) * LH, :] = res.results[c]["yout"]
    return out
